# version 24
# baseline (speedup 1.0000x reference)
import contextlib
import numpy as np
import ml_dtypes
import concourse.bass as bass
import concourse.mybir as mybir
from concourse.bass_utils import run_bass_kernel_spmd

F32 = mybir.dt.float32
BF16 = mybir.dt.bfloat16
I32 = mybir.dt.int32
I8 = mybir.dt.int8
AF = mybir.ActivationFunctionType
ALU = mybir.AluOpType
AX = mybir.AxisListType

P = 128
D = 2048
NE = D // P
B = 2
S = 8192
NCORE = 8
STRIPE = 512
NSTRIPE = S // STRIPE
TPC = 2048
NTC = TPC // STRIPE
LN_EPS = 1e-5
BF = ml_dtypes.bfloat16


def stripes_of(r):
    return [r, NSTRIPE - 1 - r]


def chunks_of(r):
    return [(b, s) for b in range(B) for s in stripes_of(r)]



class _OpHandle:
    def __init__(self, rec, entry):
        self.rec = rec
        self.entry = entry

    def inc(self, sem, n=1):
        self.entry[3].append((sem, n))
        self.rec.semval[sem] = self.rec.semval.get(sem, 0) + n
        return self


class _EngProxy:
    def __init__(self, rec, eng):
        object.__setattr__(self, "_rec", rec)
        object.__setattr__(self, "_eng", eng)

    def wait(self, sem, val=None):
        rec = self._rec
        if val is None:
            val = rec.semval.get(sem, 0)
        if val <= 0:
            return
        if val <= rec.lastwait[self._eng].get(sem, -1):
            return
        rec.lastwait[self._eng][sem] = val
        rec.streams[self._eng].append(("wait", sem, val, []))

    def __getattr__(self, m):
        if m.startswith("_"):
            raise AttributeError(m)
        rec, eng = self._rec, self._eng

        def call(*a, **k):
            entry = ["op", m, (a, k), [], rec.phase]
            rec.streams[eng].append(entry)
            return _OpHandle(rec, entry)
        return call


class Rec:
    def __init__(self, nc, stack):
        self.nc = nc
        self.stack = stack
        self.cur = stack
        self.streams = {e: [] for e in ("tensor", "vector", "scalar", "gpsimd", "sync")}
        self.semval = {}
        self.lastwait = {e: {} for e in self.streams}
        self._semnames = 0
        self.phase = "init"
        self.t = _EngProxy(self, "tensor")
        self.v = _EngProxy(self, "vector")
        self.s = _EngProxy(self, "scalar")
        self.g = _EngProxy(self, "gpsimd")
        self.y = _EngProxy(self, "sync")

    def sem(self, name=None, pool=None, barrier=True):
        if not barrier:
            s = self.stack.enter_context(self.nc.semaphore(name or "nb"))
            self.semval[s] = 0
            if not hasattr(self, "nobar"):
                self.nobar = set()
            self.nobar.add(s)
            return s
        if pool is not None:
            if not hasattr(self, "pools"):
                self.pools = {}
            if pool not in self.pools:
                s = self.stack.enter_context(self.nc.semaphore(f"pool_{pool}"))
                self.semval[s] = 0
                self.pools[pool] = s
            return self.pools[pool]
        if name is None:
            name = f"sem{self._semnames}"
            self._semnames += 1
        s = self.stack.enter_context(self.nc.semaphore(name))
        self.semval[s] = 0
        return s

    def phase_barrier(self):
        nob = getattr(self, "nobar", set())
        sems = [s for s in self.semval if s not in nob]
        for e in (self.t, self.v, self.s, self.g, self.y):
            for s in sems:
                e.wait(s)

    def val(self, sem):
        return self.semval.get(sem, 0)

    def emit(self):
        with self.nc.Block() as block:
            for eng_name in ("tensor", "vector", "scalar", "gpsimd", "sync"):
                stream = self.streams[eng_name]
                if not stream:
                    continue

                def make(stream):
                    def body(eng):
                        for entry in stream:
                            if entry[0] == "wait":
                                _, sem, valv, _ = entry
                                eng.wait_ge(sem, valv)
                            else:
                                _, m, (a, k), incs, ph = entry
                                inst = getattr(eng, m)(*a, **k)
                                try:
                                    inst.annotate(ph)
                                except Exception:
                                    pass
                                for sem, n in incs:
                                    inst = inst.then_inc(sem, n)
                    return body
                getattr(block, eng_name)(make(stream))

    def sbuf(self, name, shape, dtype, stack=None):
        return (stack or self.cur).enter_context(
            self.nc.sbuf_tensor(name, shape, dtype))

    def psum(self, name, shape, dtype=F32, stack=None):
        return (stack or self.cur).enter_context(
            self.nc.psum_tensor(name, shape, dtype))



def emit_ln(R, x_dram, xn_sb, cst, tag, lstack):
    ld = [R.sem(pool=f"y{i}") for i in range(2)]
    mm = R.sem(pool="pe0")
    evs = R.sem(pool="act0")
    evv = R.sem(pool="dve0")
    x3 = x_dram.rearrange("(k p) t -> p k t", p=P)

    xbuf = [R.sbuf(f"{tag}_x{i}", [P, NE * STRIPE], F32, lstack) for i in range(2)]
    xbf = R.sbuf(f"{tag}_xbf", [P, 8 * STRIPE], BF16, lstack)
    sqbuf = R.sbuf(f"{tag}_sq", [P, 2 * STRIPE], BF16, lstack)
    stat = R.sbuf(f"{tag}_stat", [1, 6 * STRIPE], F32, lstack)
    tmpbuf = R.sbuf(f"{tag}_tmp", [P, 2 * STRIPE], F32, lstack)
    bcast = R.sbuf(f"{tag}_bc", [P, 2 * STRIPE], F32, lstack)
    p_sum = R.psum(f"{tag}_ps", [1, STRIPE], stack=lstack)
    p_sq = R.psum(f"{tag}_pq", [1, STRIPE], stack=lstack)
    p_bc = R.psum(f"{tag}_pb", [P, 2 * STRIPE], stack=lstack)

    norm_done = {}
    sq_hist = []
    for tc in range(NTC):
        buf = xbuf[tc % 2]
        if tc >= 2:
            R.y.wait(evv, norm_done[tc - 2])
        for kq in range(4):
            R.y.dma_start(
                out=buf[:].rearrange("p (k t) -> p k t", k=NE)[:, 4 * kq:4 * kq + 4, :],
                in_=x3[:, 4 * kq:4 * kq + 4, tc * STRIPE:(tc + 1) * STRIPE],
            ).inc(ld[tc % 2], 16)
        R.t.wait(ld[tc % 2])
        R.s.wait(ld[tc % 2])
        R.v.wait(ld[tc % 2])
        for k in range(NE):
            if k % 8 == 0:
                if len(sq_hist) >= 1:
                    R.v.wait(mm, sq_hist[-1])
                R.v.wait(evs)
                R.v.tensor_copy(xbf[:],
                                buf[:, (k // 8) * 8 * STRIPE:
                                    (k // 8 + 1) * 8 * STRIPE]).inc(evv, 1)
            xt = xbf[:, (k % 8) * STRIPE:(k % 8 + 1) * STRIPE]
            sq = sqbuf[:, (k % 2) * STRIPE:(k % 2 + 1) * STRIPE]
            slot = tc * NE + k
            if slot >= 2:
                R.s.wait(mm, sq_hist[slot - 2])
            R.s.wait(evv)
            R.s.activation(sq, xt, AF.Square).inc(evs, 1)
            R.t.wait(evs)
            R.t.matmul(p_sum[:], cst["ones_col_bf"][:], xt,
                       start=(k == 0), stop=(k == NE - 1))
            R.t.matmul(p_sq[:], cst["ones_col_bf"][:], sq,
                       start=(k == 0), stop=(k == NE - 1)).inc(mm, 1)
            sq_hist.append(R.val(mm))
        mean = stat[0:1, 0:STRIPE]
        m2 = stat[0:1, STRIPE:2 * STRIPE]
        var = stat[0:1, 2 * STRIPE:3 * STRIPE]
        sd = stat[0:1, 3 * STRIPE:4 * STRIPE]
        rstd = stat[0:1, 4 * STRIPE:5 * STRIPE]
        mb = stat[0:1, 5 * STRIPE:6 * STRIPE]
        R.v.wait(mm)
        R.v.tensor_single_scalar(mean, p_sum[:], 1.0 / D, ALU.mult).inc(evv, 1)
        R.s.wait(evv)
        R.s.activation(m2, mean, AF.Square).inc(evs, 1)
        R.v.wait(evs)
        R.v.scalar_tensor_tensor(var, p_sq[:], 1.0 / D, m2,
                                 ALU.mult, ALU.subtract).inc(evv, 1)
        R.v.wait(evv)
        R.v.tensor_single_scalar(var, var, LN_EPS, ALU.add).inc(evv, 1)
        R.s.wait(evv)
        R.s.activation(sd, var, AF.Sqrt).inc(evs, 1)
        R.v.wait(evs)
        R.v.reciprocal(rstd, sd).inc(evv, 1)
        R.v.wait(evv)
        R.v.scalar_tensor_tensor(mb, mean, -1.0, rstd, ALU.mult, ALU.mult).inc(evv, 1)
        R.t.wait(evv)
        R.t.matmul(p_bc[:, 0:STRIPE], cst["ones_row_f32"][:], rstd,
                   start=True, stop=True)
        R.t.matmul(p_bc[:, STRIPE:2 * STRIPE], cst["ones_row_f32"][:], mb,
                   start=True, stop=True).inc(mm, 1)
        R.s.wait(mm)
        R.s.activation(bcast[:], p_bc[:], AF.Copy).inc(evs, 1)
        R.v.wait(evs)
        for k in range(NE):
            xt = buf[:, k * STRIPE:(k + 1) * STRIPE]
            tmp = tmpbuf[:, (k % 2) * STRIPE:(k % 2 + 1) * STRIPE]
            R.v.wait(evv)
            R.v.tensor_tensor(tmp, xt, bcast[:, 0:STRIPE], ALU.mult).inc(evv, 1)
            R.v.wait(evv)
            R.v.tensor_tensor(
                xn_sb[:, k * TPC + tc * STRIPE: k * TPC + (tc + 1) * STRIPE],
                tmp, bcast[:, STRIPE:2 * STRIPE], ALU.add).inc(evv, 1)
        norm_done[tc] = R.val(evv)
    return evv, R.val(evv)


def emit_ffn_pass(R, x_dram, xout_dram, W, cst, tag, *, inter, kind, stop_after=None, max_chunks=8, co_job=None, mid_hook=None):
    nc = R.nc
    NSUP = inter // 512
    NSTEP = inter // 128
    DEPTH = 3
    xn = R.sbuf(f"{tag}_xn", [P, NE * TPC], BF16)

    with contextlib.ExitStack() as lnstack:
        lnev, ln_done = emit_ln(R, x_dram, xn, cst, f"{tag}_ln", lnstack)

    ld = [R.sem(pool=f"g{i}") for i in range(2)]
    fse = [R.sem(pool=f"y{[2, 3, 8][i]}") for i in range(DEPTH)]
    mm = R.sem(pool="pe0")
    evs = R.sem(pool="act0")
    evv = R.sem(pool="dve0")

    fuse_dram = nc.dram_tensor(f"{tag}_fuse", [inter, TPC], BF16, kind="Internal").ap()

    with contextlib.ExitStack() as p1stack:
        wg_buf = [R.sbuf(f"{tag}_wg{i}", [P, NE * 512], BF16, p1stack) for i in range(2)]
        wu_buf = [R.sbuf(f"{tag}_wu{i}", [P, NE * 512], BF16, p1stack) for i in range(2)]
        g_sb = R.sbuf(f"{tag}_g", [P, DEPTH * STRIPE], F32, p1stack)
        u_sb = R.sbuf(f"{tag}_u", [P, DEPTH * STRIPE], F32, p1stack)
        f_sb = R.sbuf(f"{tag}_f", [P, DEPTH * STRIPE], BF16, p1stack)
        gel = R.sbuf(f"{tag}_gel", [P, DEPTH * 2 * STRIPE], F32, p1stack) \
            if kind == "l1" else None
        p_g = [R.psum(f"{tag}_pg{i}", [P, STRIPE], stack=p1stack) for i in range(DEPTH)]
        p_u = [R.psum(f"{tag}_pu{i}", [P, STRIPE], stack=p1stack) for i in range(DEPTH)]
        if co_job is not None:
            R.dq_pt = [R.psum(f"{tag}_dqpt{i}", [P, 1024], BF16, stack=p1stack)
                       for i in range(2)]

        if "wg_slice" in W:
            wg_slice = W["wg_slice"]
            wu_slice = W["wu_slice"]
        else:
            wg_full = (W["w_gate"] if kind == "l0" else W["wg_t"]).rearrange(
                "(k p) n -> p k n", p=P)
            wu_full = (W["w_up"] if kind == "l0" else W["wu_t"]).rearrange(
                "(k p) n -> p k n", p=P)
            wg_slice = lambda n0: wg_full[:, :, n0:n0 + 512]
            wu_slice = lambda n0: wu_full[:, :, n0:n0 + 512]

        R.g.wait(lnev, ln_done)
        sup_mm = []
        pipe = 0
        tile_evs = []
        tile_evv = []
        tile_fse = []
        for sup in range(NSUP):
            par = sup % 2
            if sup >= 2:
                R.g.wait(mm, sup_mm[sup - 2])
            n0 = sup * 512
            for kq in range(4):
                R.g.dma_start(
                    out=wg_buf[par][:].rearrange("p (k n) -> p k n", k=NE)
                    [:, 4 * kq:4 * kq + 4, :],
                    in_=wg_slice(n0)[:, 4 * kq:4 * kq + 4, :]).inc(ld[par], 16)
                R.g.dma_start(
                    out=wu_buf[par][:].rearrange("p (k n) -> p k n", k=NE)
                    [:, 4 * kq:4 * kq + 4, :],
                    in_=wu_slice(n0)[:, 4 * kq:4 * kq + 4, :]).inc(ld[par], 16)
            R.t.wait(ld[par])
            if sup == 0:
                R.t.wait(lnev, ln_done)
            if co_job is not None:
                co_job()
            for ns in range(4):
                for tc in range(NTC):
                    pp = pipe % DEPTH
                    if pipe >= DEPTH:
                        R.t.wait(evs, tile_evs[pipe - DEPTH])
                        if kind == "l1":
                            R.t.wait(evv, tile_evv[pipe - DEPTH])
                    for k in range(NE):
                        xnt = xn[:, k * TPC + tc * STRIPE: k * TPC + (tc + 1) * STRIPE]
                        R.t.matmul(p_g[pp][:],
                                   wg_buf[par][:, k * 512 + ns * 128:
                                               k * 512 + (ns + 1) * 128],
                                   xnt, start=(k == 0), stop=(k == NE - 1))
                    for k in range(NE):
                        xnt = xn[:, k * TPC + tc * STRIPE: k * TPC + (tc + 1) * STRIPE]
                        R.t.matmul(p_u[pp][:],
                                   wu_buf[par][:, k * 512 + ns * 128:
                                               k * 512 + (ns + 1) * 128],
                                   xnt, start=(k == 0),
                                   stop=(k == NE - 1)).inc(mm, 1 if k == NE - 1 else 0)
                    nt = sup * 4 + ns
                    gs = g_sb[:, pp * STRIPE:(pp + 1) * STRIPE]
                    us = u_sb[:, pp * STRIPE:(pp + 1) * STRIPE]
                    fs = f_sb[:, pp * STRIPE:(pp + 1) * STRIPE]
                    R.s.wait(mm)
                    if pipe >= DEPTH:
                        R.s.wait(evv, tile_evv[pipe - DEPTH])
                        R.s.wait(fse[pp], tile_fse[pipe - DEPTH])
                        R.v.wait(fse[pp], tile_fse[pipe - DEPTH])
                    if kind == "l0":
                        R.s.activation(gs, p_g[pp][:], AF.Silu,
                                       scale=W["sg_sb"][:, nt:nt + 1]).inc(evs, 1)
                        R.s.activation(us, p_u[pp][:], AF.Copy,
                                       scale=W["su_sb"][:, nt:nt + 1]).inc(evs, 1)
                        R.v.wait(evs)
                        R.v.tensor_tensor(fs, gs, us, ALU.mult).inc(evv, 1)
                    else:
                        emit_gelu_fuse(R, p_g[pp][:], p_u[pp][:], gs, us, fs,
                                       gel, pp, evs, evv,
                                       tile_evv[pipe - DEPTH] if pipe >= DEPTH else 0)
                    R.y.wait(evv)
                    R.y.dma_start(out=fuse_dram[nt * P:(nt + 1) * P,
                                                tc * STRIPE:(tc + 1) * STRIPE],
                                  in_=fs).inc(fse[pp], 16)
                    tile_evs.append(R.val(evs))
                    tile_evv.append(R.val(evv))
                    tile_fse.append(R.val(fse[pp]))
                    pipe += 1
            sup_mm.append(R.val(mm))
        phase1_evs = R.val(evs)
        phase1_evv = R.val(evv)
        phase1_fse = [R.val(s) for s in fse]

    if mid_hook is not None:
        mid_hook()

    HH = 1024
    with contextlib.ExitStack() as p2stack:
        wd_buf = [R.sbuf(f"{tag}_wd{i}", [P, 4 * HH], BF16, p2stack) for i in range(2)]
        fu_buf = [R.sbuf(f"{tag}_fu{i}", [P, 4 * STRIPE], BF16, p2stack) for i in range(2)]
        xr_buf = R.sbuf(f"{tag}_xr", [P, 8 * STRIPE], F32, p2stack)
        o_sb = R.sbuf(f"{tag}_o", [P, 8 * STRIPE], F32, p2stack)
        p_d = R.psum(f"{tag}_pd", [P, 8 * STRIPE], stack=p2stack)
        ld2g = [R.sem(pool=f"g{2 + i}") for i in range(2)]
        ld2y = [R.sem(pool=f"y{4 + i}") for i in range(2)]
        xld = R.sem(pool="s0")
        st = R.sem(pool="y6")
        x3 = x_dram.rearrange("(k p) t -> p k t", p=P)
        xo3 = xout_dram.rearrange("(k p) t -> p k t", p=P)
        if "wd_slice" in W:
            wd_slice = W["wd_slice"]
        else:
            wd_full = W["w_down"] if kind == "l0" else W["wd_t"]
            wd_slice = lambda n0, h0, hl: wd_full[n0:n0 + P, h0:h0 + hl]

        for e in (R.y, R.g, R.s, R.t):
            e.wait(evs, phase1_evs)
            e.wait(evv, phase1_evv)
            for i, s in enumerate(fse):
                e.wait(s, phase1_fse[i])

        step = 0
        step_mm = []
        chunk_evv = []
        chunk_st = []
        GRP = 4
        for hh in range(2):
            for tc in range(NTC):
                ci = hh * NTC + tc
                if ci >= 1:
                    R.s.wait(st, chunk_st[ci - 1])
                    R.s.wait(evv, chunk_evv[ci - 1])
                R.s.dma_start(
                    out=xr_buf[:].rearrange("p (k t) -> p k t", k=8),
                    in_=x3[:, hh * 8:hh * 8 + 8,
                           tc * STRIPE:(tc + 1) * STRIPE]).inc(xld, 16)
                for ng in range(NSTEP // GRP):
                    par = step % 2
                    if step >= 2:
                        R.g.wait(mm, step_mm[step - 2])
                        R.y.wait(mm, step_mm[step - 2])
                    for j in range(GRP):
                        R.g.dma_start(
                            out=wd_buf[par][:, j * HH:(j + 1) * HH],
                            in_=wd_slice((ng * GRP + j) * P, hh * HH, HH)
                            ).inc(ld2g[par], 16)
                    R.y.dma_start(
                        out=fu_buf[par][:].rearrange("p (j t) -> p j t", j=GRP),
                        in_=fuse_dram[ng * GRP * P:(ng + 1) * GRP * P,
                                      tc * STRIPE:(tc + 1) * STRIPE]
                        .rearrange("(j p) t -> p j t", p=P)).inc(ld2y[par], 16)
                    R.t.wait(ld2g[par])
                    R.t.wait(ld2y[par])
                    if ng == 0 and ci >= 1:
                        R.t.wait(evv, chunk_evv[ci - 1])
                    for j in range(GRP):
                        n = ng * GRP + j
                        for ht in range(8):
                            R.t.matmul(p_d[:, ht * STRIPE:(ht + 1) * STRIPE],
                                       wd_buf[par][:, j * HH + ht * P:
                                                    j * HH + (ht + 1) * P],
                                       fu_buf[par][:, j * STRIPE:(j + 1) * STRIPE],
                                       start=(n == 0),
                                       stop=(n == NSTEP - 1)).inc(
                                           mm, 1 if ht == 7 and j == GRP - 1 else 0)
                    step_mm.append(R.val(mm))
                    step += 1
                R.v.wait(mm)
                R.v.wait(xld)
                if ci >= 1:
                    R.v.wait(st, chunk_st[ci - 1])
                for ht in range(8):
                    k = hh * 8 + ht
                    ot = o_sb[:, ht * STRIPE:(ht + 1) * STRIPE]
                    xt = xr_buf[:, ht * STRIPE:(ht + 1) * STRIPE]
                    pd = p_d[:, ht * STRIPE:(ht + 1) * STRIPE]
                    if kind == "l0":
                        R.v.tensor_scalar(ot, pd, W["sd_sb"][:, k:k + 1], None,
                                          ALU.mult).inc(evv, 1)
                        R.v.wait(evv)
                        R.v.tensor_tensor(ot, ot, xt, ALU.add).inc(evv, 1)
                    else:
                        R.v.scalar_tensor_tensor(ot, pd, 0.5, xt,
                                                 ALU.mult, ALU.add).inc(evv, 1)
                chunk_evv.append(R.val(evv))
                R.y.wait(evv)
                R.y.dma_start(out=xo3[:, hh * 8:hh * 8 + 8,
                                      tc * STRIPE:(tc + 1) * STRIPE],
                              in_=o_sb[:].rearrange("p (k t) -> p k t", k=8)
                              ).inc(st, 16)
                chunk_st.append(R.val(st))
        R.y.wait(st)
    return st


def emit_gelu_fuse(R, p_g, p_u, gs, us, fs, gel, pp, evs, evv, prev_evv):
    s1 = gel[:, (2 * pp + 0) * STRIPE:(2 * pp + 1) * STRIPE]
    w = gel[:, (2 * pp + 1) * STRIPE:(2 * pp + 2) * STRIPE]
    CC = 0.7978845608028654
    R.s.activation(s1, p_g, AF.Square).inc(evs, 1)
    R.s.activation(us, p_u, AF.Copy).inc(evs, 1)
    R.v.wait(evs)
    R.v.wait(evv, prev_evv)
    R.v.tensor_scalar(s1, s1, CC * 0.044715, CC, ALU.mult, ALU.add).inc(evv, 1)
    R.v.wait(evv)
    R.v.tensor_tensor(w, s1, p_g, ALU.mult).inc(evv, 1)
    R.v.tensor_tensor(gs, us, p_g, ALU.mult).inc(evv, 1)
    R.s.wait(evv)
    R.s.activation(w, w, AF.Tanh).inc(evs, 1)
    R.v.wait(evs)
    R.v.tensor_scalar(w, w, 0.5, 0.5, ALU.mult, ALU.add).inc(evv, 1)
    R.v.wait(evv)
    R.v.tensor_tensor(fs, gs, w, ALU.mult).inc(evv, 1)


def _mk_consts(R):
    cst = {}
    cst["ones_col_f32"] = R.sbuf("c_ones_col", [P, 1], F32)
    cst["ones_row_f32"] = R.sbuf("c_ones_row", [1, P], F32)
    cst["ones_col_bf"] = R.sbuf("c_ones_col_bf", [P, 1], BF16)
    cst["ones_row_bf"] = R.sbuf("c_ones_row_bf", [1, P], BF16)
    csem = R.sem("c_init")
    R.v.memset(cst["ones_col_f32"][:], 1.0).inc(csem, 1)
    R.v.memset(cst["ones_row_f32"][:], 1.0).inc(csem, 1)
    R.v.memset(cst["ones_col_bf"][:], 1.0).inc(csem, 1)
    R.v.memset(cst["ones_row_bf"][:], 1.0).inc(csem, 1)
    for e in (R.t, R.s, R.y, R.g):
        e.wait(csem)
    return cst



def build_ffn0(stop_after=None, max_chunks=8):
    nc = bass.Bass("TRN2", target_bir_lowering=False)
    with contextlib.ExitStack() as stack:
        R = Rec(nc, stack)
        x_in = nc.dram_tensor("x_in", [D, TPC], F32, kind="ExternalInput").ap()
        w_gate = nc.dram_tensor("w_gate", [D, 8192], I8, kind="ExternalInput").ap()
        w_up = nc.dram_tensor("w_up", [D, 8192], I8, kind="ExternalInput").ap()
        w_down = nc.dram_tensor("w_down", [8192, D], I8, kind="ExternalInput").ap()
        s_gate = nc.dram_tensor("s_gate", [P, 64], F32, kind="ExternalInput").ap()
        s_up = nc.dram_tensor("s_up", [P, 64], F32, kind="ExternalInput").ap()
        s_down = nc.dram_tensor("s_down", [P, 16], F32, kind="ExternalInput").ap()
        x_out = nc.dram_tensor("x_out", [D, TPC], F32, kind="ExternalOutput").ap()

        cst = _mk_consts(R)
        sg_sb = R.sbuf("sg_sb", [P, 64], F32)
        su_sb = R.sbuf("su_sb", [P, 64], F32)
        sd_sb = R.sbuf("sd_sb", [P, 16], F32)
        sld = R.sem("sld")
        sini = R.sem("sini")
        R.g.dma_start(out=sg_sb[:], in_=s_gate).inc(sld, 16)
        R.g.dma_start(out=su_sb[:], in_=s_up).inc(sld, 16)
        R.g.dma_start(out=sd_sb[:], in_=s_down).inc(sld, 16)
        R.v.wait(sld)
        R.v.tensor_single_scalar(sd_sb[:], sd_sb[:], 0.5, ALU.mult).inc(sini, 1)
        for e in (R.t, R.s, R.y, R.g, R.v):
            e.wait(sld)
            e.wait(sini)

        W = dict(w_gate=w_gate, w_up=w_up, w_down=w_down,
                 sg_sb=sg_sb, su_sb=su_sb, sd_sb=sd_sb)
        emit_ffn_pass(R, x_in, x_out, W, cst, "f0", inter=8192, kind="l0",
                      stop_after=stop_after, max_chunks=max_chunks)
        R.emit()
    return nc


SC_ATTN = 0.08838834764831845


def emit_proj_ch(R, xn, w_dram, bias_sb, out_cb, cst, tag, joins):
    with contextlib.ExitStack() as pstack:
        ld = [R.sem(pool=f"g{i}") for i in range(2)]
        mm = R.sem(pool="pe0")
        evs = R.sem(pool="act0")
        wbuf = [R.sbuf(f"{tag}_w{i}", [P, NE * 512], BF16, pstack) for i in range(2)]
        e_sb = R.sbuf(f"{tag}_e", [P, 2 * STRIPE], F32, pstack)
        p_o = [R.psum(f"{tag}_po{i}", [P, STRIPE], stack=pstack) for i in range(2)]
        w_src = w_dram.rearrange("(k p) n -> p k n", p=P)

        for e in (R.g, R.t, R.s, R.v, R.y):
            for sem, valv in joins:
                e.wait(sem, valv)

        pipe = 0
        sup_mm = []
        tile_done = []
        tile_evs = []
        for sup in range(4):
            par = sup % 2
            if sup >= 2:
                R.g.wait(mm, sup_mm[sup - 2])
            n0 = sup * 512
            for kq in range(4):
                R.g.dma_start(
                    out=wbuf[par][:].rearrange("p (k n) -> p k n", k=NE)
                    [:, 4 * kq:4 * kq + 4, :],
                    in_=w_src[:, 4 * kq:4 * kq + 4, n0:n0 + 512]).inc(ld[par], 16)
            R.t.wait(ld[par])
            for ns in range(4):
                ft = sup * 4 + ns
                for tc in range(NTC):
                    pp = pipe % 2
                    if pipe >= 2:
                        for sem, valv in tile_done[pipe - 2]:
                            R.t.wait(sem, valv)
                        R.t.wait(evs, tile_evs[pipe - 2])
                    for k in range(NE):
                        xnt = xn[:, k * TPC + tc * STRIPE: k * TPC + (tc + 1) * STRIPE]
                        R.t.matmul(p_o[pp][:],
                                   wbuf[par][:, k * 512 + ns * 128:
                                             k * 512 + (ns + 1) * 128],
                                   xnt, start=(k == 0),
                                   stop=(k == NE - 1)).inc(
                                       mm, 1 if k == NE - 1 else 0)
                    R.s.wait(mm)
                    if pipe >= 2:
                        for sem, valv in tile_done[pipe - 2]:
                            R.s.wait(sem, valv)
                    es = e_sb[:, pp * STRIPE:(pp + 1) * STRIPE]
                    if bias_sb is not None:
                        R.s.activation(es, p_o[pp][:], AF.Identity,
                                       bias=bias_sb[:, ft:ft + 1]).inc(evs, 1)
                    else:
                        R.s.activation(es, p_o[pp][:], AF.Copy).inc(evs, 1)
                    tile_evs.append(R.val(evs))
                    tile_done.append(out_cb(ft, tc, pp, es, evs))
                    pipe += 1
            sup_mm.append(R.val(mm))
        out_joins = [(mm, R.val(mm)), (evs, R.val(evs))]
        for cond in tile_done[-2:]:
            out_joins += cond
    return out_joins


def emit_qkv0(R, x_dram, W, cst, tag, after_k=None, after_v=None):
    nc = R.nc
    xn = R.sbuf(f"{tag}_xn", [P, NE * TPC], BF16)
    nrm = R.sem(pool="pe1")
    nev_v = R.sem(pool="dve0")
    sq_sb = R.sbuf(f"{tag}_sqn", [P, 3 * STRIPE], F32)
    nmax = R.sbuf(f"{tag}_nmax", [1, 64], F32)
    R.qk_nmax = nmax
    p_n = R.psum(f"{tag}_pn", [1, TPC])
    out_sb = R.sbuf(f"{tag}_ob", [P, 3 * STRIPE], BF16)
    r_scr = R.sbuf(f"{tag}_rs", [64, 2 * STRIPE], F32)
    bq_sb = R.sbuf(f"{tag}_bq", [P, NE], F32)
    bk_sb = R.sbuf(f"{tag}_bk", [P, NE], F32)
    bvrow = R.sbuf(f"{tag}_bv", [1, D], BF16)
    cl = R.sem(pool="y6")
    bl = R.sem(pool="g4")
    R.y.dma_start(out=bq_sb[:], in_=W["bq"]).inc(cl, 16)
    R.y.dma_start(out=bk_sb[:], in_=W["bk"]).inc(cl, 16)
    R.g.dma_start(out=bvrow[0:1, :], in_=W["bv"]).inc(bl, 16)
    for e in (R.s, R.t, R.v):
        e.wait(cl)
        e.wait(bl)

    with contextlib.ExitStack() as lnstack:
        lnev, ln_done = emit_ln(R, x_dram, xn, cst, f"{tag}_ln", lnstack)
    R.g.wait(lnev, ln_done)
    cs2 = R.sbuf(f"{tag}_cs2", [P, 2 * TPC], F32)
    R.y.wait(lnev, ln_done)
    R.y.dma_start(out=cs2[:], in_=W["cs2"]).inc(cl, 16)
    for e in (R.s, R.t, R.v):
        e.wait(cl)
    QD = 3
    wbuf = [R.sbuf(f"{tag}_w{i}", [P, NE * 512], BF16) for i in range(2)]
    e_sb = R.sbuf(f"{tag}_e", [P, QD * STRIPE], F32)
    p_o = [R.psum(f"{tag}_po{i}", [P, STRIPE]) for i in range(QD)]

    ld = [R.sem(pool=f"g{i}") for i in range(2)]
    mm = R.sem(pool="pe0")
    evs = R.sem(pool="act0")
    sqs = R.sem(pool="act1")
    evv = R.sem(pool="dve0")
    sst = [R.sem(pool=f"y{[2, 3, 10][i]}") for i in range(QD)]

    qT3 = W["qT_out"].rearrange("(h p) t -> p h t", p=P)
    kT3 = W["kT_out"].rearrange("(h p) t -> p h t", p=P)
    vo = W["v_out"]
    srcs = dict(q=W["wq_t"].rearrange("(k p) n -> p k n", p=P),
                k=W["wk_t"].rearrange("(k p) n -> p k n", p=P),
                v=W["wv_t"].rearrange("(k p) n -> p k n", p=P))
    jobs = [("k", s) for s in range(4)] + [("v", s) for s in range(4)] + \
           [("q", s) for s in range(4)]

    pipe = 0
    job_mm = []
    tile_done = []
    tile_sst = []
    nrm_hist = []
    for ji, (proj, sup) in enumerate(jobs):
        par = ji % 2
        if ji >= 2:
            R.g.wait(mm, job_mm[ji - 2])
        n0 = sup * 512
        for kq in range(4):
            R.g.dma_start(
                out=wbuf[par][:].rearrange("p (k n) -> p k n", k=NE)
                [:, 4 * kq:4 * kq + 4, :],
                in_=srcs[proj][:, 4 * kq:4 * kq + 4, n0:n0 + 512]).inc(ld[par], 16)
        R.t.wait(ld[par])
        if ji == 0:
            R.t.wait(lnev, ln_done)
        for t16 in range(NE):
            pp = pipe % QD
            if pipe >= QD:
                for sem, valv in tile_done[pipe - QD]:
                    R.t.wait(sem, valv)
            if proj == "v":
                tt = t16
                for k in range(NE):
                    R.t.matmul(p_o[pp][:],
                               xn[:, k * TPC + tt * P: k * TPC + (tt + 1) * P],
                               wbuf[par][:, k * 512:(k + 1) * 512],
                               start=(k == 0), stop=False)
                R.t.matmul(p_o[pp][:], cst["ones_row_bf"][:],
                           bvrow[0:1, n0:n0 + 512],
                           start=False, stop=True).inc(mm, 1)
                R.s.wait(mm)
                if pipe >= QD:
                    for sem, valv in tile_done[pipe - QD]:
                        R.s.wait(sem, valv)
                    R.s.wait(sst[pp], tile_sst[pipe - QD])
                ob = out_sb[:, pp * STRIPE:(pp + 1) * STRIPE]
                R.s.activation(ob, p_o[pp][:], AF.Copy).inc(evs, 1)
                R.y.wait(evs)
                for hd in range(4):
                    h = sup * 4 + hd
                    R.y.dma_start(
                        out=vo[h * TPC + tt * P: h * TPC + (tt + 1) * P, :],
                        in_=ob[:, hd * P:(hd + 1) * P]).inc(sst[pp], 16)
                tile_done.append([(evs, R.val(evs))])
                tile_sst.append(R.val(sst[pp]))
            else:
                ns, tc = t16 // 4, t16 % 4
                ft = sup * 4 + ns
                for k in range(NE):
                    xnt = xn[:, k * TPC + tc * STRIPE: k * TPC + (tc + 1) * STRIPE]
                    R.t.matmul(p_o[pp][:],
                               wbuf[par][:, k * 512 + ns * 128:
                                         k * 512 + (ns + 1) * 128],
                               xnt, start=(k == 0),
                               stop=(k == NE - 1)).inc(mm, 1 if k == NE - 1 else 0)
                R.s.wait(mm)
                if pipe >= QD:
                    for sem, valv in tile_done[pipe - QD]:
                        R.s.wait(sem, valv)
                es = e_sb[:, pp * STRIPE:(pp + 1) * STRIPE]
                bias = bk_sb if proj == "k" else bq_sb
                R.s.activation(es, p_o[pp][:], AF.Identity,
                               bias=bias[:, ft:ft + 1]).inc(evs, 1)
                o3 = kT3 if proj == "k" else qT3
                qn_off = 32 if proj == "k" else 0
                qe = es[0:64, :]
                qo = es[64:128, :]
                ct0 = cs2[0:64, tc * STRIPE:(tc + 1) * STRIPE]
                sn64 = cs2[64:128, tc * STRIPE:(tc + 1) * STRIPE]
                sn0 = cs2[0:64, TPC + tc * STRIPE: TPC + (tc + 1) * STRIPE]
                ct64 = cs2[64:128, TPC + tc * STRIPE: TPC + (tc + 1) * STRIPE]
                r1 = r_scr[0:64, 0:STRIPE]
                r2 = r_scr[0:64, STRIPE:2 * STRIPE]
                ob = out_sb[:, pp * STRIPE:(pp + 1) * STRIPE]
                R.v.wait(evs)
                if pipe >= QD:
                    R.v.wait(sst[pp], tile_sst[pipe - QD])
                R.v.wait(evv)
                R.v.tensor_tensor(r1, qe, ct0, ALU.mult).inc(evv, 1)
                R.v.tensor_tensor(r2, qo, sn64, ALU.mult).inc(evv, 1)
                R.v.wait(evv)
                R.v.tensor_tensor(ob[0:64, :], r1, r2, ALU.subtract).inc(evv, 1)
                R.v.wait(evv)
                R.v.tensor_tensor(r1, qe, sn0, ALU.mult).inc(evv, 1)
                R.v.tensor_tensor(r2, qo, ct64, ALU.mult).inc(evv, 1)
                R.v.wait(evv)
                R.v.tensor_tensor(ob[64:128, :], r1, r2, ALU.add).inc(evv, 1)
                R.s.wait(evv)
                sqt = sq_sb[:, pp * STRIPE:(pp + 1) * STRIPE]
                if len(nrm_hist) >= 2:
                    R.s.wait(nrm, nrm_hist[-2])
                R.s.activation(sqt, ob, AF.Square).inc(sqs, 1)
                R.t.wait(sqs)
                if ft > 0 and tc == 0:
                    R.t.wait(nev_v)
                R.t.matmul(p_n[0:1, tc * STRIPE:(tc + 1) * STRIPE],
                           cst["ones_col_f32"][:], sqt,
                           start=True, stop=True).inc(nrm, 1)
                nrm_hist.append(R.val(nrm))
                R.y.wait(evv)
                R.y.dma_start(out=o3[:, ft, tc * STRIPE:(tc + 1) * STRIPE],
                              in_=ob).inc(sst[pp], 16)
                tile_sst.append(R.val(sst[pp]))
                if tc == NTC - 1:
                    R.v.wait(nrm)
                    for b in range(B):
                        R.v.tensor_reduce(
                            nmax[0:1, qn_off + b * 16 + ft:
                                 qn_off + b * 16 + ft + 1],
                            p_n[0:1, b * 1024:(b + 1) * 1024],
                            AX.X, ALU.max).inc(nev_v, 1)
                tile_done.append([(evv, R.val(evv)), (sqs, R.val(sqs))])
            pipe += 1
        job_mm.append(R.val(mm))
        if ji == 3 and after_k is not None:
            after_k(R, [(sst[0], R.val(sst[0])), (sst[1], R.val(sst[1])),
                        (nev_v, R.val(nev_v))])
        if ji == 7 and after_v is not None:
            after_v(R, [(sst[0], R.val(sst[0])), (sst[1], R.val(sst[1]))])

    fst = R.sem(pool="y7")
    R.y.wait(nev_v)
    R.y.dma_start(out=W["nrm_out"], in_=nmax[0:1, :]).inc(fst, 16)
    R.y.wait(fst)
    R.y.wait(sst[0])
    R.y.wait(sst[1])


def emit_v_tok(R, xn, W, cst, tag, bvrow, joins):
    with contextlib.ExitStack() as pstack:
        ld = [R.sem(pool=f"g{i}") for i in range(2)]
        mm = R.sem(pool="pe0")
        evs = R.sem(pool="act0")
        sst = [R.sem(pool=f"y{2 + i}") for i in range(2)]
        wbuf = [R.sbuf(f"{tag}_w{i}", [P, NE * 512], BF16, pstack) for i in range(2)]
        vb_sb = R.sbuf(f"{tag}_vb", [P, 2 * STRIPE], BF16, pstack)
        p_o = [R.psum(f"{tag}_po{i}", [P, STRIPE], stack=pstack) for i in range(2)]
        w_src = W["wv_t"].rearrange("(k p) n -> p k n", p=P)
        vo = W["v_out"]

        for e in (R.g, R.t, R.s, R.y):
            for sem, valv in joins:
                e.wait(sem, valv)

        pipe = 0
        sup_mm = []
        tile_done = []
        for sup in range(4):
            par = sup % 2
            if sup >= 2:
                R.g.wait(mm, sup_mm[sup - 2])
            n0 = sup * 512
            for kq in range(4):
                R.g.dma_start(
                    out=wbuf[par][:].rearrange("p (k n) -> p k n", k=NE)
                    [:, 4 * kq:4 * kq + 4, :],
                    in_=w_src[:, 4 * kq:4 * kq + 4, n0:n0 + 512]).inc(ld[par], 16)
            R.t.wait(ld[par])
            for tt in range(NE):
                pp = pipe % 2
                if pipe >= 2:
                    R.t.wait(evs, tile_done[pipe - 2])
                for k in range(NE):
                    R.t.matmul(p_o[pp][:],
                               xn[:, k * TPC + tt * P: k * TPC + (tt + 1) * P],
                               wbuf[par][:, k * 512:(k + 1) * 512],
                               start=(k == 0), stop=False)
                R.t.matmul(p_o[pp][:], cst["ones_row_bf"][:],
                           bvrow[0:1, n0:n0 + 512],
                           start=False, stop=True).inc(mm, 1)
                R.s.wait(mm)
                R.s.wait(sst[pp])
                vb = vb_sb[:, pp * STRIPE:(pp + 1) * STRIPE]
                R.s.activation(vb, p_o[pp][:], AF.Copy).inc(evs, 1)
                R.y.wait(evs)
                for hd in range(4):
                    h = sup * 4 + hd
                    R.y.dma_start(
                        out=vo[h * TPC + tt * P: h * TPC + (tt + 1) * P, :],
                        in_=vb[:, hd * P:(hd + 1) * P]).inc(sst[pp], 16)
                tile_done.append(R.val(evs))
                pipe += 1
            sup_mm.append(R.val(mm))
        out_joins = [(mm, R.val(mm)), (evs, R.val(evs)),
                     (sst[0], R.val(sst[0])), (sst[1], R.val(sst[1]))]
    return out_joins


def emit_attn0(R, W, cst, tag):
    nc = R.nc
    NPAIR = 32

    c0s = R.sem(pool="act0")
    c0v = R.sem(pool="dve0")
    c0m = R.sem(pool="pe0")
    cln = R.sem(pool="y6")
    qn_sb = R.sbuf(f"{tag}_qn", [1, 32], F32)
    kn_sb = R.sbuf(f"{tag}_kn", [1, 256], F32)
    c0_sb = R.sbuf(f"{tag}_c0", [1, 128], F32)
    fut_sb = R.sbuf(f"{tag}_fut", [1, 128], F32)
    mask_sb = R.sbuf(f"{tag}_mask", [P, 896], BF16)
    c0b = R.sbuf(f"{tag}_c0b", [P, 32], F32)
    futb = R.sbuf(f"{tag}_futb", [P, 128], F32)
    bias_sb = R.sbuf(f"{tag}_bias", [P, 2 * NPAIR], F32)
    c0stack = contextlib.ExitStack()
    p_c0 = R.psum(f"{tag}_pc0", [P, 160], stack=c0stack)
    R.y.dma_start(out=qn_sb[:], in_=W["qn"]).inc(cln, 16)
    R.y.dma_start(out=kn_sb[:], in_=W["kn_all"]).inc(cln, 16)
    R.y.dma_start(out=fut_sb[:], in_=W["fut"]).inc(cln, 16)
    R.y.dma_start(out=mask_sb[:], in_=W["masks"]).inc(cln, 16)
    R.v.wait(cln)
    km = c0_sb[0:1, 32:64]
    prod = c0_sb[0:1, 64:96]
    c0r = c0_sb[0:1, 0:32]
    for bh in range(32):
        R.v.tensor_reduce(km[0:1, bh:bh + 1],
                          bass.AP(kn_sb, bh, [[256, 1], [32, 8]]),
                          AX.X, ALU.max).inc(c0v, 1)
    R.v.wait(c0v)
    R.v.tensor_tensor(prod, qn_sb[0:1, :], km, ALU.mult).inc(c0v, 1)
    R.s.wait(c0v)
    R.s.activation(c0_sb[0:1, 96:128], prod, AF.Sqrt).inc(c0s, 1)
    R.v.wait(c0s)
    R.v.tensor_single_scalar(c0r, c0_sb[0:1, 96:128], -SC_ATTN,
                             ALU.mult).inc(c0v, 1)
    R.t.wait(c0v)
    R.t.matmul(p_c0[:, 0:32], cst["ones_row_f32"][:], c0r,
               start=True, stop=True)
    R.t.matmul(p_c0[:, 32:160], cst["ones_row_f32"][:], fut_sb[0:1, :],
               start=True, stop=True).inc(c0m, 1)
    R.s.wait(c0m)
    R.s.activation(c0b[:], p_c0[:, 0:32], AF.Copy).inc(c0s, 1)
    R.s.activation(futb[:], p_c0[:, 32:160], AF.Copy).inc(c0s, 1)
    c0_ready = R.val(c0s)
    c0stack.close()

    kbuf = [R.sbuf(f"{tag}_kb{i}", [P, 64 * P], BF16) for i in range(2)]
    vbuf = [R.sbuf(f"{tag}_vb{i}", [P, 64 * P], BF16) for i in range(2)]
    qbuf = [R.sbuf(f"{tag}_qb{i}", [P, NE * STRIPE], BF16) for i in range(2)]
    kdb = [R.sbuf(f"{tag}_kd{i}", [P, 4 * P], BF16) for i in range(2)]
    vdb = [R.sbuf(f"{tag}_vd{i}", [P, 4 * P], BF16) for i in range(2)]
    ptb = [R.sbuf(f"{tag}_pt{i}", [P, 1024], BF16) for i in range(2)]
    att = [R.sbuf(f"{tag}_at{i}", [P, NE * STRIPE], BF16) for i in range(2)]
    lsb = R.sbuf(f"{tag}_l", [1, 2 * STRIPE], F32)
    lbc = R.sbuf(f"{tag}_lbc", [P, STRIPE], F32)
    p_s = [R.psum(f"{tag}_ps{i}", [P, 1024]) for i in range(2)]
    p_o = [R.psum(f"{tag}_po{i}", [P, STRIPE]) for i in range(2)]
    p_l = R.psum(f"{tag}_pl", [1, STRIPE])
    p_b2 = R.psum(f"{tag}_pb2", [P, STRIPE])

    kld = [R.sem(pool=f"y{i}") for i in range(2)]
    vld = [R.sem(pool=f"g{i}") for i in range(2)]
    qld = [R.sem(pool=f"s{1 + i}") for i in range(2)]
    dld = [R.sem(pool=f"y{2 + i}") for i in range(2)]
    mm = R.sem(pool="pe0")
    exs = R.sem(pool="act0")
    evv = R.sem(pool="dve0")

    kT_all = W["kT_all"]
    v_all3 = W["v_all"].rearrange("(x p) d -> p x d", p=P)
    kT_own3 = W["kT"].rearrange("(h p) t -> p h t", p=P)
    qT3 = W["qT"].rearrange("(h p) t -> p h t", p=P)
    v_own3 = W["v_hb"].rearrange("(x p) d -> p x d", p=P)

    kvdone = []
    att_evv = []
    recip_done = []
    hist_mask = []
    diagdone = []
    expdone_slot = []
    hist_exp = []
    hist_ptb = []
    batch_mm = {}
    oproj_state = dict(wp=0, wo_mm=[], nft=0, evac=[], nsup=0, sup_evac=[])
    wo_ld = [R.sem(pool=f"g{2 + i}") for i in range(2)]
    xld = [R.sem(pool=["s0", "s3"][i]) for i in range(2)]
    sst = [R.sem(pool=f"y{4 + i}") for i in range(2)]
    wo_blk = [R.sbuf(f"{tag}_wo{i}", [P, NE * 256], BF16) for i in range(2)]
    xrb = R.sbuf(f"{tag}_xr", [P, 2 * 2 * STRIPE], F32)
    o_sb = R.sbuf(f"{tag}_osb", [P, 2 * 2 * STRIPE], F32)
    p_op = [p_b2]
    wo_src = W["wo_t"].rearrange("(k p) n -> p k n", p=P)
    bo_sb = R.sbuf(f"{tag}_bo", [P, NE], F32)
    bol = R.sem(pool="y7")
    R.y.dma_start(out=bo_sb[:], in_=W["bo"]).inc(bol, 16)
    R.v.wait(bol)

    x3 = W["x_in"].rearrange("(k p) t -> p k t", p=P)
    xo3 = W["x_out"].rearrange("(k p) t -> p k t", p=P)

    pipe_pair = 0
    hslot = 0
    for b in range(B):
        for h in range(NE):
            hp = hslot % 2
            if hslot >= 2:
                R.y.wait(mm, kvdone[hslot - 2])
                R.g.wait(mm, kvdone[hslot - 2])
            for rk in range(8):
                for j, s in enumerate([rk, 15 - rk]):
                    R.y.dma_start(
                        out=kbuf[hp][:, (4 * s) * P:(4 * s + 4) * P],
                        in_=kT_all[rk * D + h * P: rk * D + (h + 1) * P,
                                   (2 * b + j) * STRIPE:(2 * b + j + 1) * STRIPE]
                    ).inc(kld[hp], 16)
                    R.g.dma_start(
                        out=vbuf[hp][:].rearrange("p (kt d) -> p kt d", d=P)
                        [:, 4 * s:4 * s + 4, :],
                        in_=v_all3[:, ((rk * NE + h) * NE + (2 * b + j) * 4):
                                   ((rk * NE + h) * NE + (2 * b + j + 1) * 4), :]
                    ).inc(vld[hp], 16)
            R.t.wait(kld[hp])
            R.t.wait(vld[hp])
            for cil in range(2):
                ci = 2 * b + cil
                cp = ci % 2
                dslot = hslot * 2 + cil
                dp = dslot % 2
                bh = b * NE + h
                if h == 0:
                    if b > 0:
                        R.s.wait(mm, batch_mm[b - 1])
                    R.s.dma_start(
                        out=qbuf[cp][:].rearrange("p (k t) -> p k t", k=NE),
                        in_=qT3[:, :, ci * STRIPE:(ci + 1) * STRIPE]
                    ).inc(qld[cp], 16)
                if dslot >= 2:
                    R.y.wait(mm, diagdone[dslot - 2])
                R.y.dma_start(out=kdb[dp][:],
                              in_=kT_own3[:, h, ci * STRIPE:(ci + 1) * STRIPE]
                              ).inc(dld[dp], 16)
                R.y.dma_start(out=vdb[dp][:].rearrange("p (j d) -> p j d", d=P),
                              in_=v_own3[:, h * NE + ci * 4: h * NE + ci * 4 + 4, :]
                              ).inc(dld[dp], 16)
                R.v.wait(c0s, c0_ready)
                if dslot >= 2:
                    R.v.wait(exs, expdone_slot[dslot - 2])
                bt = bias_sb[:, dp * NPAIR:(dp + 1) * NPAIR]
                R.v.wait(evv)
                R.v.tensor_scalar(bt, futb[:, ci * NPAIR:(ci + 1) * NPAIR],
                                  -1e30, None, ALU.mult).inc(evv, 1)
                R.v.wait(evv)
                R.v.tensor_scalar(bt, bt, c0b[:, bh:bh + 1], None,
                                  ALU.add).inc(evv, 1)
                bias_ready = R.val(evv)
                R.t.wait(evv, bias_ready)
                R.t.wait(qld[cp])
                R.t.wait(dld[dp])
                qh = qbuf[cp][:, h * STRIPE:(h + 1) * STRIPE]
                npair_ci = 14 if cil == 0 else 30
                NPC = npair_ci + 2
                pob = (hslot * 2 + cil) % 2

                def pair_operands(p):
                    diag = p >= npair_ci
                    if diag:
                        j0 = (p - npair_ci) * 2
                        return (diag, j0,
                                kdb[dp][:, j0 * P:(j0 + 1) * P],
                                kdb[dp][:, (j0 + 1) * P:(j0 + 2) * P],
                                vdb[dp][:, j0 * P:(j0 + 1) * P],
                                vdb[dp][:, (j0 + 1) * P:(j0 + 2) * P])
                    return (diag, 0,
                            kbuf[hp][:, (2 * p) * P:(2 * p + 1) * P],
                            kbuf[hp][:, (2 * p + 1) * P:(2 * p + 2) * P],
                            vbuf[hp][:, (2 * p) * P:(2 * p + 1) * P],
                            vbuf[hp][:, (2 * p + 1) * P:(2 * p + 2) * P])

                for p in range(NPC + 1):
                    if p < NPC:
                        par = pipe_pair % 2
                        diag, j0, lk0, lk1, _, _ = pair_operands(p)
                        if pipe_pair >= 2:
                            R.t.wait(exs, hist_exp[pipe_pair - 2])
                        R.t.matmul(p_s[par][:, 0:512], lk0, qh,
                                   start=True, stop=True)
                        R.t.matmul(p_s[par][:, 512:1024], lk1, qh,
                                   start=True, stop=True).inc(mm, 1)
                        R.s.wait(mm)
                        if pipe_pair >= 2:
                            R.s.wait(mm, hist_ptb[pipe_pair - 2])
                        bias_ap = (c0b[:, bh:bh + 1] if diag
                                   else bt[:, p:p + 1])
                        R.s.activation(ptb[par][:], p_s[par][:], AF.Exp,
                                       bias=bias_ap, scale=SC_ATTN).inc(exs, 1)
                        hist_exp.append(R.val(exs))
                        if diag:
                            R.v.wait(exs)
                            R.v.tensor_tensor(
                                ptb[par][:, 0:512], ptb[par][:, 0:512],
                                mask_sb[:, (3 - j0) * 128:(3 - j0) * 128 + 512],
                                ALU.mult)
                            R.v.tensor_tensor(
                                ptb[par][:, 512:1024], ptb[par][:, 512:1024],
                                mask_sb[:, (2 - j0) * 128:(2 - j0) * 128 + 512],
                                ALU.mult).inc(evv, 1)
                        hist_mask.append(R.val(evv))
                        pipe_pair += 1
                    if p >= 1:
                        pv = p - 1
                        gp = pipe_pair - 2 if p < NPC else pipe_pair - 1
                        par2 = gp % 2
                        diag2, j02, _, _, lv0, lv1 = pair_operands(pv)
                        if diag2:
                            R.t.wait(evv, hist_mask[gp])
                        else:
                            R.t.wait(exs, hist_exp[gp])
                        first = (pv == 0)
                        last = (pv == NPC - 1)
                        if first:
                            ds = hslot * 2 + cil
                            if ds >= 2:
                                R.t.wait(evv, att_evv[ds - 2])
                            if ds >= 1:
                                R.t.wait(evv, recip_done[ds - 1])
                        R.t.matmul(p_o[pob][:], lv0, ptb[par2][:, 0:512],
                                   start=first, stop=False)
                        R.t.matmul(p_o[pob][:], lv1, ptb[par2][:, 512:1024],
                                   start=False, stop=last)
                        R.t.matmul(p_l[:], cst["ones_col_bf"][:],
                                   ptb[par2][:, 0:512],
                                   start=first, stop=False)
                        R.t.matmul(p_l[:], cst["ones_col_bf"][:],
                                   ptb[par2][:, 512:1024],
                                   start=False, stop=last).inc(mm, 1)
                        hist_ptb.append(R.val(mm))
                expdone_slot.append(R.val(exs))
                diagdone.append(R.val(mm))
                R.v.wait(mm)
                lrow = lsb[0:1, dp * STRIPE:(dp + 1) * STRIPE]
                R.v.reciprocal(lrow, p_l[:]).inc(evv, 1)
                recip_done.append(R.val(evv))
                R.t.wait(evv)
                R.t.wait(exs)
                R.t.matmul(p_b2[:], cst["ones_row_f32"][:], lrow,
                           start=True, stop=True).inc(mm, 1)
                R.s.wait(mm)
                R.s.wait(evv)
                R.s.activation(lbc[:], p_b2[:], AF.Copy).inc(exs, 1)
                R.v.wait(exs)
                if b > 0:
                    R.v.wait(mm, batch_mm[b - 1])
                R.v.tensor_tensor(att[cil][:, h * STRIPE:(h + 1) * STRIPE],
                                  p_o[pob][:], lbc[:], ALU.mult).inc(evv, 1)
                att_evv.append(R.val(evv))
            kvdone.append(R.val(mm))
            hslot += 1
        for cil in range(2):
            ci = 2 * b + cil
            for fsup in range(8):
                xp = oproj_state["nsup"] % 2
                if oproj_state["nsup"] >= 2:
                    R.s.wait(evv, oproj_state["sup_evac"][oproj_state["nsup"] - 2])
                R.s.dma_start(
                    out=xrb[:, xp * 2 * STRIPE:(xp + 1) * 2 * STRIPE]
                    .rearrange("p (k t) -> p k t", k=2),
                    in_=x3[:, fsup * 2:(fsup + 1) * 2,
                           ci * STRIPE:(ci + 1) * STRIPE]).inc(xld[xp], 16)
                wp = oproj_state["wp"] % 2
                if oproj_state["wp"] >= 2:
                    R.g.wait(mm, oproj_state["wo_mm"][oproj_state["wp"] - 2])
                n0 = fsup * 256
                for kq in range(4):
                    R.g.dma_start(
                        out=wo_blk[wp][:].rearrange("p (k n) -> p k n", k=NE)
                        [:, 4 * kq:4 * kq + 4, :],
                        in_=wo_src[:, 4 * kq:4 * kq + 4, n0:n0 + 256]
                    ).inc(wo_ld[wp], 16)
                R.t.wait(wo_ld[wp])
                R.t.wait(evv)
                for ft4 in range(2):
                    ft = fsup * 2 + ft4
                    nft = oproj_state["nft"]
                    pq = 0
                    if nft >= 1:
                        R.t.wait(evv, oproj_state["evac"][nft - 1])
                    for k in range(NE):
                        R.t.matmul(p_op[pq][:, 0:STRIPE],
                                   wo_blk[wp][:, k * 256 + ft4 * 128:
                                              k * 256 + (ft4 + 1) * 128],
                                   att[cil][:, k * STRIPE:(k + 1) * STRIPE],
                                   start=(k == 0),
                                   stop=(k == NE - 1)).inc(
                                       mm, 1 if k == NE - 1 else 0)
                    R.v.wait(mm)
                    R.v.wait(xld[xp])
                    opair = oproj_state["nft"] // 2
                    op = opair % 2
                    ot = o_sb[:, (op * 2 + ft4 % 2) * STRIPE:
                              (op * 2 + ft4 % 2 + 1) * STRIPE]
                    if opair >= 2:
                        R.v.wait(sst[op])
                    R.v.scalar_tensor_tensor(
                        ot, p_op[pq][:, 0:STRIPE], bo_sb[:, ft:ft + 1],
                        xrb[:, (xp * 2 + ft4) * STRIPE:(xp * 2 + ft4 + 1) * STRIPE],
                        ALU.add, ALU.add).inc(evv, 1)
                    oproj_state["evac"].append(R.val(evv))
                    oproj_state["nft"] += 1
                    if ft4 % 2 == 1:
                        R.y.wait(evv)
                        R.y.dma_start(
                            out=xo3[:, fsup * 2: fsup * 2 + 2,
                                    ci * STRIPE:(ci + 1) * STRIPE],
                            in_=o_sb[:, op * 2 * STRIPE:(op + 1) * 2 * STRIPE]
                            .rearrange("p (k t) -> p k t", k=2)).inc(sst[op], 16)
                oproj_state["wo_mm"].append(R.val(mm))
                oproj_state["wp"] += 1
                oproj_state["sup_evac"].append(R.val(evv))
                oproj_state["nsup"] += 1
        batch_mm[b] = R.val(mm)
    R.y.wait(sst[0])
    R.y.wait(sst[1])
    return


def build_qkv0():
    nc = bass.Bass("TRN2", target_bir_lowering=False)
    with contextlib.ExitStack() as stack:
        R = Rec(nc, stack)
        W = {}
        x_in = nc.dram_tensor("x_in", [D, TPC], F32, kind="ExternalInput").ap()
        for nm, shp, dt in [("wq_t", [D, D], F32), ("wk_t", [D, D], F32),
                            ("wv_t", [D, D], F32), ("bq", [P, NE], F32),
                            ("bk", [P, NE], F32), ("bv", [1, D], F32),
                            ("cs2", [P, 2 * TPC], F32)]:
            W[nm] = nc.dram_tensor(nm, shp, dt, kind="ExternalInput").ap()
        W["qT_out"] = nc.dram_tensor("qT_out", [D, TPC], BF16, kind="ExternalOutput").ap()
        W["kT_out"] = nc.dram_tensor("kT_out", [D, TPC], BF16, kind="ExternalOutput").ap()
        W["v_out"] = nc.dram_tensor("v_out", [NE * TPC, P], BF16, kind="ExternalOutput").ap()
        W["nrm_out"] = nc.dram_tensor("nrm_out", [1, 64], F32, kind="ExternalOutput").ap()
        cst = _mk_consts(R)
        emit_qkv0(R, x_in, W, cst, "qk")
        R.emit()
    return nc


def build_attn0():
    nc = bass.Bass("TRN2", target_bir_lowering=False)
    with contextlib.ExitStack() as stack:
        R = Rec(nc, stack)
        W = {}
        W["x_in"] = nc.dram_tensor("x_in", [D, TPC], F32, kind="ExternalInput").ap()
        for nm, shp, dt in [("qT", [D, TPC], BF16), ("kT", [D, TPC], BF16),
                            ("v_hb", [NE * TPC, P], BF16),
                            ("kT_all", [8 * D, TPC], BF16),
                            ("v_all", [8 * NE * TPC, P], BF16),
                            ("qn", [1, 32], F32), ("kn_all", [1, 256], F32),
                            ("fut", [1, 128], F32), ("masks", [P, 896], BF16),
                            ("wo_t", [D, D], F32), ("bo", [P, NE], F32)]:
            W[nm] = nc.dram_tensor(nm, shp, dt, kind="ExternalInput").ap()
        W["x_out"] = nc.dram_tensor("x_out", [D, TPC], F32, kind="ExternalOutput").ap()
        cst = _mk_consts(R)
        emit_attn0(R, W, cst, "a0")
        R.emit()
    return nc



ROPE_PERM = np.concatenate(
    [np.concatenate([h * 128 + 2 * np.arange(64),
                     h * 128 + 2 * np.arange(64) + 1]) for h in range(16)])


def host_rope_tables(r):
    inv = (10000.0 ** (-np.arange(0, 128, 2) / 128.0)).astype(np.float64)
    cols = []
    for (b, s) in chunks_of(r):
        pos = np.arange(s * STRIPE, (s + 1) * STRIPE, dtype=np.float64)
        cols.append(pos)
    pos = np.concatenate(cols)
    ang = pos[None, :] * inv[:, None]
    c = np.cos(ang).astype(np.float32)
    s_ = np.sin(ang).astype(np.float32)
    return np.ascontiguousarray(np.block([[c, s_], [s_, c]]))


def host_masks():
    k = np.arange(128)[:, None]
    c = np.arange(896)[None, :]
    return (c >= k + 384).astype(np.float32).astype(BF)


def host_fut(r):
    out = np.zeros((1, 128), np.float32)
    for ci, (b, s) in enumerate(chunks_of(r)):
        for pr in range(32):
            if 2 * pr >= 4 * s:
                out[0, ci * 32 + pr] = 1.0
    return out


def bias_tile(bvec, perm=None):
    v = np.asarray(bvec, np.float32)
    if perm is not None:
        v = v[perm]
    return np.ascontiguousarray(v.reshape(NE, P).T)


def emit_nvfp4_tile(R, sp, sc_view, out_bf, scr, evv):
    W2 = sp.shape[1] * 2
    c = scr["c"]
    sg = scr["sg"]
    g2 = scr["g2"]
    g1 = scr["g1"]
    def op(fn):
        R.v.wait(evv)
        fn().inc(evv, 1)
    op(lambda: R.v.tensor_single_scalar(c[:, 0:W2:2], sp, 15, ALU.bitwise_and))
    op(lambda: R.v.tensor_scalar(c[:, 1:W2:2], sp, 4, 15,
                                 ALU.logical_shift_right, ALU.bitwise_and))
    op(lambda: R.v.tensor_scalar(sg[:], c[:], 8, 28,
                                 ALU.bitwise_and, ALU.logical_shift_left))
    op(lambda: R.v.tensor_single_scalar(c[:], c[:], 7, ALU.bitwise_and))
    op(lambda: R.v.tensor_single_scalar(g2[:], c[:], 2, ALU.is_ge))
    op(lambda: R.v.tensor_single_scalar(g1[:], c[:], 1, ALU.is_ge))
    op(lambda: R.v.scalar_tensor_tensor(g2[:], c[:], 251, g2[:], ALU.add, ALU.add))
    op(lambda: R.v.tensor_tensor(g2[:], g2[:], g1[:], ALU.mult))
    op(lambda: R.v.tensor_single_scalar(g2[:], g2[:], 22, ALU.logical_shift_left))
    op(lambda: R.v.tensor_tensor(g2[:], g2[:], sg[:], ALU.bitwise_or))
    op(lambda: R.v.tensor_tensor(
        out_bf.rearrange("p (b i) -> p b i", i=32),
        g2[:].bitcast(F32).rearrange("p (b i) -> p b i", i=32),
        sc_view, ALU.mult))


def make_dequant_jobs(R, W, cst, tag):
    nc = R.nc
    ld = [R.sem(pool=f"g{4 + i}") for i in range(2)]
    mm = R.sem(pool="pe0")
    evs = R.sem(pool="act0")
    evv = R.sem(pool="dve0")
    sst = [R.sem(pool=["y7", "y9"][i]) for i in range(2)]

    sp_b = [R.sbuf(f"{tag}_sp{i}", [P, 512], I32, stack=R.stack) for i in range(2)]
    sc_b = [R.sbuf(f"{tag}_sc{i}", [P, 32], F32, stack=R.stack) for i in range(2)]
    scr = {k: R.sbuf(f"{tag}_{k}", [P, 1024], I32, stack=R.stack)
           for k in ("c", "sg", "g2", "g1")}
    vals = R.sbuf(f"{tag}_vals", [P, 1024], BF16, stack=R.stack)
    tv_b = [R.sbuf(f"{tag}_tv{i}", [P, 1024], BF16, stack=R.stack) for i in range(2)]
    ident = R.sbuf(f"{tag}_id", [P, P], BF16, stack=R.stack)
    idl = R.sem(pool="y6")
    R.y.dma_start(out=ident[:], in_=W["ident"]).inc(idl, 16)
    R.t.wait(idl)

    jobs = []
    for mat, wp, sc, od in (("g", W["gWp"], W["gSc"], W["wg_t_part"]),
                            ("u", W["uWp"], W["uSc"], W["wu_t_part"]),
                            ("d", W["dWp"], W["dSc"], W["wd_t_part"])):
        for rt in range(16):
            for half in range(2):
                jobs.append((wp[rt * P:(rt + 1) * P, half * 512:(half + 1) * 512],
                             sc[rt * P:(rt + 1) * P, half * 32:(half + 1) * 32],
                             od, rt, half))

    state = dict(ji=0, tile_evv=[], tile_mm=[], tile_evs=[])

    def run_jobs(kk):
        for _ in range(kk):
            ji = state["ji"]
            if ji >= len(jobs):
                return
            src_, scsrc, outspec, rblk, half = jobs[ji]
            par = ji % 2
            if ji >= 2:
                R.g.wait(evv, state["tile_evv"][ji - 2])
            R.g.dma_start(out=sp_b[par][:], in_=src_).inc(ld[par], 16)
            R.g.dma_start(out=sc_b[par][:], in_=scsrc).inc(ld[par], 16)
            R.v.wait(ld[par])
            if ji >= 1:
                R.v.wait(mm, state["tile_mm"][ji - 1])
            sc_view = bass.AP(sc_b[par], 0, [[32, P], [1, 32], [0, 32]])
            emit_nvfp4_tile(R, sp_b[par][:], sc_view, vals[:], scr, evv)
            state["tile_evv"].append(R.val(evv))
            p_t = R.dq_pt
            R.t.wait(evv)
            if ji >= 2:
                R.t.wait(evs, state["tile_evs"][ji - 2])
            for k in range(8):
                h = R.t.transpose(p_t[par][:, k * P:(k + 1) * P],
                                  vals[:, k * P:(k + 1) * P], ident[:])
            h.inc(mm, 1)
            state["tile_mm"].append(R.val(mm))
            R.s.wait(mm)
            if ji >= 2:
                R.s.wait(sst[par])
            R.s.activation(tv_b[par][:], p_t[par][:], AF.Copy).inc(evs, 1)
            state["tile_evs"].append(R.val(evs))
            R.y.wait(evs)
            o3 = outspec.rearrange("(k p) n -> p k n", p=P)
            for kq in range(2):
                R.y.dma_start(
                    out=o3[:, half * 8 + 4 * kq: half * 8 + 4 * kq + 4,
                           rblk * P:(rblk + 1) * P],
                    in_=tv_b[par][:].rearrange("p (k c) -> p k c", k=8)
                    [:, 4 * kq:4 * kq + 4, :]).inc(sst[par], 16)
            state["ji"] = ji + 1

    def finish():
        run_jobs(len(jobs) - state["ji"])
        return [(sst[0], R.val(sst[0])), (sst[1], R.val(sst[1])),
                (evs, R.val(evs))]

    return run_jobs, len(jobs), finish


def emit_dequant(R, W, cst, tag):
    run_jobs, n, finish = make_dequant_jobs(R, W, cst, tag)
    finish()


def build_dequant():
    nc = bass.Bass("TRN2", target_bir_lowering=False)
    with contextlib.ExitStack() as stack:
        R = Rec(nc, stack)
        W = {}
        W["gWp"] = nc.dram_tensor("gWp", [D, 1024], I32, kind="ExternalInput").ap()
        W["gSc"] = nc.dram_tensor("gSc", [D, 64], F32, kind="ExternalInput").ap()
        W["uWp"] = nc.dram_tensor("uWp", [D, 1024], I32, kind="ExternalInput").ap()
        W["uSc"] = nc.dram_tensor("uSc", [D, 64], F32, kind="ExternalInput").ap()
        W["dWp"] = nc.dram_tensor("dWp", [D, 1024], I32, kind="ExternalInput").ap()
        W["dSc"] = nc.dram_tensor("dSc", [D, 64], F32, kind="ExternalInput").ap()
        W["ident"] = nc.dram_tensor("ident", [P, P], BF16, kind="ExternalInput").ap()
        W["wg_t_part"] = nc.dram_tensor("wg_t_part", [D, D], BF16, kind="ExternalOutput").ap()
        W["wu_t_part"] = nc.dram_tensor("wu_t_part", [D, D], BF16, kind="ExternalOutput").ap()
        W["wd_t_part"] = nc.dram_tensor("wd_t_part", [D, D], BF16, kind="ExternalOutput").ap()
        cst = _mk_consts(R)
        emit_dequant(R, W, cst, "dq")
        R.emit()
    return nc


def build_ffn1():
    nc = bass.Bass("TRN2", target_bir_lowering=False)
    with contextlib.ExitStack() as stack:
        R = Rec(nc, stack)
        x_in = nc.dram_tensor("x_in", [D, TPC], F32, kind="ExternalInput").ap()
        wg_t = nc.dram_tensor("wg_t", [D, 16384], BF16, kind="ExternalInput").ap()
        wu_t = nc.dram_tensor("wu_t", [D, 16384], BF16, kind="ExternalInput").ap()
        wd_t = nc.dram_tensor("wd_t", [16384, D], BF16, kind="ExternalInput").ap()
        x_out = nc.dram_tensor("x_out", [D, TPC], F32, kind="ExternalOutput").ap()
        cst = _mk_consts(R)
        W = dict(wg_t=wg_t, wu_t=wu_t, wd_t=wd_t)
        emit_ffn_pass(R, x_in, x_out, W, cst, "f1", inter=16384, kind="l1")
        R.emit()
    return nc


def emit_attn1a(R, x_dram, W, cst, tag):
    nc = R.nc
    xn = R.sbuf(f"{tag}_xn", [P, NE * TPC], BF16)
    bq_sb = R.sbuf(f"{tag}_bq", [P, NE], F32)
    bkrow = R.sbuf(f"{tag}_bk", [1, D], BF16)
    bvrow = R.sbuf(f"{tag}_bv", [1, D], BF16)
    elu_sb = R.sbuf(f"{tag}_elu", [P, 4 * STRIPE], F32)
    qp_sb = R.sbuf(f"{tag}_qp", [P, 2 * STRIPE], BF16)
    bl = R.sem(pool="y6")
    blg = R.sem(pool="g4")
    R.y.dma_start(out=bq_sb[:], in_=W["qb"]).inc(bl, 16)
    R.g.dma_start(out=bkrow[0:1, :], in_=W["kb"]).inc(blg, 16)
    R.g.dma_start(out=bvrow[0:1, :], in_=W["vb"]).inc(blg, 16)
    for e in (R.s, R.t, R.v):
        e.wait(bl)
        e.wait(blg)

    with contextlib.ExitStack() as lnstack:
        lnev, ln_done = emit_ln(R, x_dram, xn, cst, f"{tag}_ln", lnstack)
    joins = [(lnev, ln_done)]

    evv_q = R.sem(pool="dve0")
    axs_q = R.sem(pool="act1")
    sst_q = [R.sem(pool=f"y{2 + i}") for i in range(2)]
    qpT3 = W["qpT_out"].rearrange("(h p) t -> p h t", p=P)

    def qp_cb(ft, tc, pp, es, evs):
        m = elu_sb[:, (2 * pp) * STRIPE:(2 * pp + 1) * STRIPE]
        r_ = elu_sb[:, (2 * pp + 1) * STRIPE:(2 * pp + 2) * STRIPE]
        ob = qp_sb[:, pp * STRIPE:(pp + 1) * STRIPE]
        R.v.wait(evs)
        R.v.wait(evv_q)
        R.v.tensor_scalar_min(m, es, 0.0).inc(evv_q, 1)
        R.s.wait(evv_q)
        R.s.activation(m, m, AF.Exp).inc(axs_q, 1)
        R.v.tensor_scalar_max(r_, es, 0.0).inc(evv_q, 1)
        R.v.wait(axs_q)
        R.v.wait(evv_q)
        R.v.wait(sst_q[pp])
        R.v.tensor_tensor(ob, m, r_, ALU.add).inc(evv_q, 1)
        R.y.wait(evv_q)
        R.y.dma_start(out=qpT3[:, ft, tc * STRIPE:(tc + 1) * STRIPE],
                      in_=ob).inc(sst_q[pp], 16)
        return [(evv_q, R.val(evv_q)), (axs_q, R.val(axs_q))]

    joins = emit_proj_ch(R, xn, W["qw_t"], bq_sb, qp_cb, cst, f"{tag}_q", joins)
    joins += [(sst_q[0], R.val(sst_q[0])), (sst_q[1], R.val(sst_q[1]))]

    with contextlib.ExitStack() as ps:
        ld = [R.sem(pool=f"g{i}") for i in range(2)]
        mm = R.sem(pool="pe0")
        axs = R.sem(pool="act0")
        evv = R.sem(pool="dve0")
        kst = R.sem(pool="y4")
        wkb = [R.sbuf(f"{tag}_wk{i}", [P, NE * 512], BF16, ps) for i in range(2)]
        wvb = [R.sbuf(f"{tag}_wv{i}", [P, NE * 512], BF16, ps) for i in range(2)]
        kpt = [R.sbuf(f"{tag}_kp{i}", [P, 512], BF16, ps) for i in range(2)]
        vt = [R.sbuf(f"{tag}_vt{i}", [P, 512], BF16, ps) for i in range(2)]
        kvo = R.sbuf(f"{tag}_kvo", [P, 2 * 512], F32, ps)
        kso = R.sbuf(f"{tag}_kso", [1, 2 * 512], F32, ps)
        p_k = [R.psum(f"{tag}_pk{i}", [P, 512], stack=ps) for i in range(2)]
        p_v = [R.psum(f"{tag}_pv{i}", [P, 512], stack=ps) for i in range(2)]
        p_kv = [R.psum(f"{tag}_pkv{i}", [P, 512], stack=ps) for i in range(2)]
        p_ks = [R.psum(f"{tag}_pks{i}", [1, 512], stack=ps) for i in range(2)]
        wk_src = W["kw_t"].rearrange("(k p) n -> p k n", p=P)
        wv_src = W["vw_t"].rearrange("(k p) n -> p k n", p=P)

        for e in (R.g, R.t, R.s, R.v, R.y):
            for sem, valv in joins:
                e.wait(sem, valv)

        pipe = 0
        tdone = []
        fc_mm = []
        kvd = []
        for fc in range(4):
            par = fc % 2
            if fc >= 2:
                R.g.wait(mm, fc_mm[fc - 2])
            n0 = fc * 512
            for kq in range(4):
                R.g.dma_start(
                    out=wkb[par][:].rearrange("p (k n) -> p k n", k=NE)
                    [:, 4 * kq:4 * kq + 4, :],
                    in_=wk_src[:, 4 * kq:4 * kq + 4, n0:n0 + 512]).inc(ld[par], 16)
                R.g.dma_start(
                    out=wvb[par][:].rearrange("p (k n) -> p k n", k=NE)
                    [:, 4 * kq:4 * kq + 4, :],
                    in_=wv_src[:, 4 * kq:4 * kq + 4, n0:n0 + 512]).inc(ld[par], 16)
            R.t.wait(ld[par])
            for tt in range(NE):
                b = tt // 8
                pp = pipe % 2
                if tt == 0 and fc >= 1:
                    R.t.wait(evv, kvd[fc - 1])
                if pipe >= 2:
                    R.t.wait(evv, tdone[pipe - 2][0])
                    R.t.wait(axs, tdone[pipe - 2][1])
                for k in range(NE):
                    R.t.matmul(p_k[pp][:],
                               xn[:, k * TPC + tt * P: k * TPC + (tt + 1) * P],
                               wkb[par][:, k * 512:(k + 1) * 512],
                               start=(k == 0), stop=False)
                R.t.matmul(p_k[pp][:], cst["ones_row_bf"][:],
                           bkrow[0:1, n0:n0 + 512], start=False, stop=True)
                for k in range(NE):
                    R.t.matmul(p_v[pp][:],
                               xn[:, k * TPC + tt * P: k * TPC + (tt + 1) * P],
                               wvb[par][:, k * 512:(k + 1) * 512],
                               start=(k == 0), stop=False)
                R.t.matmul(p_v[pp][:], cst["ones_row_bf"][:],
                           bvrow[0:1, n0:n0 + 512], start=False,
                           stop=True).inc(mm, 1)
                R.s.wait(mm)
                m = elu_sb[:, (2 * pp) * STRIPE:(2 * pp + 1) * STRIPE]
                r_ = elu_sb[:, (2 * pp + 1) * STRIPE:(2 * pp + 2) * STRIPE]
                R.v.wait(mm)
                if pipe >= 2:
                    R.v.wait(mm, tdone[pipe - 2][2])
                    R.v.wait(evv, tdone[pipe - 2][0])
                R.v.tensor_scalar_min(m, p_k[pp][:], 0.0).inc(evv, 1)
                R.s.wait(evv)
                R.s.activation(m, m, AF.Exp).inc(axs, 1)
                R.v.tensor_scalar_max(r_, p_k[pp][:], 0.0).inc(evv, 1)
                R.v.wait(axs)
                R.v.wait(evv)
                R.v.tensor_tensor(kpt[pp][:], m, r_, ALU.add).inc(evv, 1)
                R.s.wait(evv)
                R.s.activation(vt[pp][:], p_v[pp][:], AF.Copy).inc(axs, 1)
                R.t.wait(evv)
                R.t.wait(axs)
                first = (tt % 8) == 0
                last = (tt % 8) == 7
                for hd in range(4):
                    R.t.matmul(p_kv[b][:, hd * P:(hd + 1) * P],
                               kpt[pp][:, hd * P:(hd + 1) * P],
                               vt[pp][:, hd * P:(hd + 1) * P],
                               start=(first and hd == 0),
                               stop=(last and hd == 3))
                R.t.matmul(p_ks[b][:], cst["ones_col_bf"][:], kpt[pp][:],
                           start=first, stop=last).inc(mm, 1)
                tdone.append((R.val(evv), R.val(axs), R.val(mm)))
                pipe += 1
            fc_mm.append(R.val(mm))
            R.v.wait(mm)
            if fc >= 1:
                R.v.wait(kst)
            for b in range(2):
                R.v.tensor_copy(kvo[:, b * 512:(b + 1) * 512],
                                p_kv[b][:]).inc(evv, 1)
                R.v.tensor_copy(kso[0:1, b * 512:(b + 1) * 512],
                                p_ks[b][:]).inc(evv, 1)
            kvd.append(R.val(evv))
            R.y.wait(evv)
            R.y.dma_start(out=W["kv_out"].rearrange("(b p) n -> p b n", p=P)
                          [:, :, n0:n0 + 512],
                          in_=kvo[:].rearrange("p (b n) -> p b n", b=2)
                          ).inc(kst, 16)
            for b in range(2):
                R.y.dma_start(out=W["ks_out"][b:b + 1, n0:n0 + 512],
                              in_=kso[0:1, b * 512:(b + 1) * 512]
                              ).inc(kst, 16)
        R.y.wait(kst)
        R.y.wait(sst_q[0])
        R.y.wait(sst_q[1])


def emit_attn1b(R, W, cst, tag):
    nc = R.nc
    EPS1 = 1e-6
    acc = R.sem(pool="g5")
    evv = R.sem(pool="dve0")
    axs = R.sem(pool="act0")
    mm = R.sem(pool="pe0")
    kv_f = R.sbuf(f"{tag}_kvf", [P, 2 * D], F32)
    kv_bf = R.sbuf(f"{tag}_kvb", [P, 2 * D], BF16)
    ks_f = R.sbuf(f"{tag}_ksf", [2, D], F32)
    ks_bf = R.sbuf(f"{tag}_ksb", [2, D], BF16)
    ksc = R.sbuf(f"{tag}_ksc", [P, 32], BF16)
    ident = R.sbuf(f"{tag}_id", [P, P], BF16)
    p_tr = R.psum(f"{tag}_ptr", [P, 32], BF16)
    idl = R.sem(pool="y6")
    R.y.dma_start(out=ident[:], in_=W["ident"]).inc(idl, 16)
    if "kv_sum" in W:
        R.g.dma_start(out=kv_f[:].rearrange("p (b n) -> p b n", b=2),
                      in_=W["kv_sum"].rearrange("(b p) n -> p b n", p=P)
                      ).inc(acc, 16)
        R.g.dma_start(out=ks_f[:], in_=W["ks_sum"]).inc(acc, 16)
    else:
        R.g.dma_start(out=kv_f[:].rearrange("p (b n) -> p b n", b=2),
                      in_=W["kv_all"].rearrange("(r b p) n -> p r b n", p=P, b=2)
                      [:, 0, :, :]).inc(acc, 16)
        for r in range(1, 8):
            R.g.wait(acc)
            R.g.dma_start(out=kv_f[:].rearrange("p (b n) -> p b n", b=2),
                          in_=W["kv_all"].rearrange("(r b p) n -> p r b n", p=P, b=2)
                          [:, r, :, :], accum_op=ALU.add).inc(acc, 16)
        R.g.wait(acc)
        R.g.dma_start(out=ks_f[:], in_=W["ks_all"][0:2, :]).inc(acc, 16)
        for r in range(1, 8):
            R.g.wait(acc)
            R.g.dma_start(out=ks_f[:], in_=W["ks_all"][2 * r:2 * r + 2, :],
                          accum_op=ALU.add).inc(acc, 16)
    R.v.wait(acc)
    R.v.tensor_copy(kv_bf[:], kv_f[:]).inc(evv, 1)
    R.v.tensor_copy(ks_bf[:], ks_f[:]).inc(evv, 1)
    R.t.wait(evv)
    R.t.wait(idl)
    h = None
    for k in range(NE):
        h = R.t.transpose(p_tr[:, 2 * k:2 * k + 2],
                          ks_bf[0:2, k * P:(k + 1) * P], ident[0:2, 0:2])
    h.inc(mm, 1)
    R.s.wait(mm)
    R.s.activation(ksc[:].rearrange("p (b k) -> p b k", b=2),
                   p_tr[:].rearrange("p (k b) -> p k b", b=2)
                   .transpose([0, 2, 1]), AF.Copy).inc(axs, 1)

    att = [R.sbuf(f"{tag}_at{i}", [P, NE * STRIPE], BF16) for i in range(2)]
    qpb = [R.sbuf(f"{tag}_qb{i}", [P, NE * STRIPE], BF16) for i in range(2)]
    zrow = R.sbuf(f"{tag}_z", [1, 2 * STRIPE], F32)
    zbc = R.sbuf(f"{tag}_zb", [P, STRIPE], F32)
    p_a = [R.psum(f"{tag}_pa{i}", [P, STRIPE]) for i in range(2)]
    p_z = R.psum(f"{tag}_pz", [1, STRIPE])
    p_zb = R.psum(f"{tag}_pzb", [P, STRIPE])
    qld = [R.sem(pool=f"s{1 + i}") for i in range(2)]
    qpT3 = W["qpT"].rearrange("(h p) t -> p h t", p=P)

    wo_ld = [R.sem(pool=f"g{2 + i}") for i in range(2)]
    xld = [R.sem(pool=["s0", "s3"][i]) for i in range(2)]
    sst = [R.sem(pool=f"y{4 + i}") for i in range(2)]
    wo_blk = [R.sbuf(f"{tag}_wo{i}", [P, NE * 512], BF16) for i in range(2)]
    xrb = R.sbuf(f"{tag}_xr", [P, 2 * 4 * STRIPE], F32)
    o_sb = R.sbuf(f"{tag}_osb", [P, 2 * 2 * STRIPE], F32)
    p_op = [R.psum(f"{tag}_pop0", [P, STRIPE])]
    wo_src = W["ow_t"].rearrange("(k p) n -> p k n", p=P)
    bo_sb = R.sbuf(f"{tag}_bo", [P, NE], F32)
    bol = R.sem(pool="y7")
    R.y.dma_start(out=bo_sb[:], in_=W["ob"]).inc(bol, 16)
    R.v.wait(bol)
    x3 = W["x_in"].rearrange("(k p) t -> p k t", p=P)
    xo3 = W["x_out"].rearrange("(k p) t -> p k t", p=P)
    ost = dict(wp=0, wo_mm=[], nft=0, evac=[], nsup=0, sup_evac=[])

    ci_mm = {}
    att_done = []
    for ci in range(NTC):
        b = ci // 2
        cp = ci % 2
        if ci >= 2:
            R.s.wait(mm, ci_mm[ci - 2])
        R.s.dma_start(out=qpb[cp][:].rearrange("p (k t) -> p k t", k=NE),
                      in_=qpT3[:, :, ci * STRIPE:(ci + 1) * STRIPE]
                      ).inc(qld[cp], 16)
        R.t.wait(qld[cp])
        R.t.wait(axs)
        for h2 in range(NE):
            bh = b * NE + h2
            pa = h2 % 2
            qh = qpb[cp][:, h2 * STRIPE:(h2 + 1) * STRIPE]
            R.t.wait(evv)
            if ci >= 2 and h2 == 0:
                R.t.wait(mm, ci_mm[ci - 2])
            R.t.matmul(p_a[pa][:],
                       kv_bf[:, b * D + h2 * P: b * D + (h2 + 1) * P],
                       qh, start=True, stop=True)
            R.t.matmul(p_z[:], ksc[:, bh:bh + 1], qh,
                       start=True, stop=True).inc(mm, 1)
            R.v.wait(mm)
            zr = zrow[0:1, (h2 % 2) * STRIPE:(h2 % 2 + 1) * STRIPE]
            R.v.tensor_single_scalar(zr, p_z[:], EPS1, ALU.add).inc(evv, 1)
            R.v.wait(evv)
            R.v.reciprocal(zr, zr).inc(evv, 1)
            R.t.wait(evv)
            R.t.matmul(p_zb[:], cst["ones_row_f32"][:], zr,
                       start=True, stop=True).inc(mm, 1)
            R.s.wait(mm)
            R.s.wait(evv)
            R.s.activation(zbc[:], p_zb[:], AF.Copy).inc(axs, 1)
            R.v.wait(axs)
            if ci >= 2 and h2 == 0:
                R.v.wait(mm, ci_mm[ci - 2])
            R.v.tensor_tensor(att[ci % 2][:, h2 * STRIPE:(h2 + 1) * STRIPE],
                              p_a[pa][:], zbc[:], ALU.mult).inc(evv, 1)
            att_done.append(R.val(evv))
        ci_mm[ci] = R.val(mm)
        for fsup in range(4):
            xp = ost["nsup"] % 2
            if ost["nsup"] >= 2:
                R.s.wait(evv, ost["sup_evac"][ost["nsup"] - 2])
            R.s.dma_start(
                out=xrb[:, xp * 4 * STRIPE:(xp + 1) * 4 * STRIPE]
                .rearrange("p (k t) -> p k t", k=4),
                in_=x3[:, fsup * 4:(fsup + 1) * 4,
                       ci * STRIPE:(ci + 1) * STRIPE]).inc(xld[xp], 16)
            wp = ost["wp"] % 2
            if ost["wp"] >= 2:
                R.g.wait(mm, ost["wo_mm"][ost["wp"] - 2])
            n0 = fsup * 512
            for kq in range(4):
                R.g.dma_start(
                    out=wo_blk[wp][:].rearrange("p (k n) -> p k n", k=NE)
                    [:, 4 * kq:4 * kq + 4, :],
                    in_=wo_src[:, 4 * kq:4 * kq + 4, n0:n0 + 512]
                ).inc(wo_ld[wp], 16)
            R.t.wait(wo_ld[wp])
            R.t.wait(evv)
            for ft4 in range(4):
                ft = fsup * 4 + ft4
                nft = ost["nft"]
                if nft >= 1:
                    R.t.wait(evv, ost["evac"][nft - 1])
                for k in range(NE):
                    R.t.matmul(p_op[0][:],
                               wo_blk[wp][:, k * 512 + ft4 * 128:
                                          k * 512 + (ft4 + 1) * 128],
                               att[ci % 2][:, k * STRIPE:(k + 1) * STRIPE],
                               start=(k == 0),
                               stop=(k == NE - 1)).inc(
                                   mm, 1 if k == NE - 1 else 0)
                R.v.wait(mm)
                R.v.wait(xld[xp])
                opair = ost["nft"] // 2
                op = opair % 2
                ot = o_sb[:, (op * 2 + ft4 % 2) * STRIPE:
                          (op * 2 + ft4 % 2 + 1) * STRIPE]
                if opair >= 2:
                    R.v.wait(sst[op])
                R.v.scalar_tensor_tensor(
                    ot, p_op[0][:], bo_sb[:, ft:ft + 1],
                    xrb[:, (xp * 4 + ft4) * STRIPE:(xp * 4 + ft4 + 1) * STRIPE],
                    ALU.add, ALU.add).inc(evv, 1)
                ost["evac"].append(R.val(evv))
                ost["nft"] += 1
                if ft4 % 2 == 1:
                    R.y.wait(evv)
                    R.y.dma_start(
                        out=xo3[:, fsup * 4 + ft4 - 1: fsup * 4 + ft4 + 1,
                                ci * STRIPE:(ci + 1) * STRIPE],
                        in_=o_sb[:, op * 2 * STRIPE:(op + 1) * 2 * STRIPE]
                        .rearrange("p (k t) -> p k t", k=2)).inc(sst[op], 16)
            ost["wo_mm"].append(R.val(mm))
            ost["wp"] += 1
            ost["sup_evac"].append(R.val(evv))
            ost["nsup"] += 1
    R.y.wait(sst[0])
    R.y.wait(sst[1])


def build_attn1a():
    nc = bass.Bass("TRN2", target_bir_lowering=False)
    with contextlib.ExitStack() as stack:
        R = Rec(nc, stack)
        W = {}
        x_in = nc.dram_tensor("x_in", [D, TPC], F32, kind="ExternalInput").ap()
        for nm, shp, dt in [("qw_t", [D, D], F32), ("kw_t", [D, D], F32),
                            ("vw_t", [D, D], F32), ("qb", [P, NE], F32),
                            ("kb", [1, D], F32), ("vb", [1, D], F32)]:
            W[nm] = nc.dram_tensor(nm, shp, dt, kind="ExternalInput").ap()
        W["qpT_out"] = nc.dram_tensor("qpT_out", [D, TPC], BF16, kind="ExternalOutput").ap()
        W["kv_out"] = nc.dram_tensor("kv_out", [2 * P, D], F32, kind="ExternalOutput").ap()
        W["ks_out"] = nc.dram_tensor("ks_out", [2, D], F32, kind="ExternalOutput").ap()
        cst = _mk_consts(R)
        emit_attn1a(R, x_in, W, cst, "a1")
        R.emit()
    return nc


def build_attn1b():
    nc = bass.Bass("TRN2", target_bir_lowering=False)
    with contextlib.ExitStack() as stack:
        R = Rec(nc, stack)
        W = {}
        W["x_in"] = nc.dram_tensor("x_in", [D, TPC], F32, kind="ExternalInput").ap()
        for nm, shp, dt in [("qpT", [D, TPC], BF16),
                            ("kv_all", [8 * 2 * P, D], F32),
                            ("ks_all", [8 * 2, D], F32),
                            ("ident", [P, P], BF16),
                            ("ow_t", [D, D], F32), ("ob", [P, NE], F32)]:
            W[nm] = nc.dram_tensor(nm, shp, dt, kind="ExternalInput").ap()
        W["x_out"] = nc.dram_tensor("x_out", [D, TPC], F32, kind="ExternalOutput").ap()
        cst = _mk_consts(R)
        emit_attn1b(R, W, cst, "b1")
        R.emit()
    return nc



def shard_x(x):
    out = []
    for r in range(NCORE):
        cols = [np.ascontiguousarray(x[b, s * STRIPE:(s + 1) * STRIPE, :].T)
                for (b, s) in chunks_of(r)]
        out.append(np.ascontiguousarray(np.concatenate(cols, axis=1)))
    return out


def unshard_x(slices):
    x = np.empty((B, S, D), np.float32)
    for r in range(NCORE):
        for i, (b, s) in enumerate(chunks_of(r)):
            x[b, s * STRIPE:(s + 1) * STRIPE, :] = \
                slices[r][:, i * STRIPE:(i + 1) * STRIPE].T
    return x


def scale_tile(s):
    return np.ascontiguousarray(np.asarray(s, np.float32).reshape(-1, P).T)


def build_fused():
    nc = bass.Bass("TRN2", target_bir_lowering=False)
    stack = contextlib.ExitStack()
    with stack:
        R = Rec(nc, stack)
        EI = lambda n, s, d: nc.dram_tensor(n, s, d, kind="ExternalInput").ap()
        IN = lambda n, s, d: nc.dram_tensor(n, s, d, kind="Internal").ap()

        x_in = EI("x_in", [D, TPC], F32)
        x_out = nc.dram_tensor("x_out", [D, TPC], F32, kind="ExternalOutput").ap()
        w_gate = EI("w_gate", [D, 8192], I8)
        w_up = EI("w_up", [D, 8192], I8)
        w_down = EI("w_down", [8192, D], I8)
        s_gate = EI("s_gate", [P, 64], F32)
        s_up = EI("s_up", [P, 64], F32)
        s_down = EI("s_down", [P, 16], F32)
        wq_t = EI("wq_t", [D, D], F32)
        wk_t = EI("wk_t", [D, D], F32)
        wv_t = EI("wv_t", [D, D], F32)
        bq = EI("bq", [P, NE], F32)
        bk = EI("bk", [P, NE], F32)
        bv = EI("bv", [1, D], F32)
        cs2 = EI("cs2", [P, 2 * TPC], F32)
        wo_t = EI("wo_t", [D, D], F32)
        bo = EI("bo", [P, NE], F32)
        masks = EI("masks", [P, 896], BF16)
        fut = EI("fut", [1, 128], F32)
        gWp = EI("gWp", [D, 1024], I32)
        gSc = EI("gSc", [D, 64], F32)
        uWp = EI("uWp", [D, 1024], I32)
        uSc = EI("uSc", [D, 64], F32)
        dWp = EI("dWp", [D, 1024], I32)
        dSc = EI("dSc", [D, 64], F32)
        ident = EI("ident", [P, P], BF16)
        qw_t = EI("qw_t", [D, D], F32)
        kw_t = EI("kw_t", [D, D], F32)
        vw_t = EI("vw_t", [D, D], F32)
        qb1 = EI("qb1", [P, NE], F32)
        kb1 = EI("kb1", [1, D], F32)
        vb1 = EI("vb1", [1, D], F32)
        ow_t = EI("ow_t", [D, D], F32)
        ob1 = EI("ob1", [P, NE], F32)

        x1 = IN("x1", [D, TPC], F32)
        x2 = IN("x2", [D, TPC], F32)
        x3 = IN("x3", [D, TPC], F32)
        x4 = IN("x4", [D, TPC], F32)
        x5 = IN("x5", [D, TPC], F32)
        qT = IN("qT", [D, TPC], BF16)
        kT = IN("kT", [D, TPC], BF16)
        v_hb = IN("v_hb", [NE * TPC, P], BF16)
        nrm = IN("nrm", [1, 64], F32)
        kn1 = IN("kn1", [1, 32], F32)
        kT_all = IN("kT_all", [8 * D, TPC], BF16)
        v_all = IN("v_all", [8 * NE * TPC, P], BF16)
        kn_all8 = IN("kn_all8", [8, 32], F32)
        dq_part = IN("dq_part", [3 * D, D], BF16)
        dq_all = IN("dq_all", [8 * 3 * D, D], BF16)
        qpT = IN("qpT", [D, TPC], BF16)
        kv_part = IN("kv_part", [2 * P, D], F32)
        ks_part = IN("ks_part", [2, D], F32)
        kv_sum = IN("kv_sum", [2 * P, D], F32)
        ks_sum = IN("ks_sum", [2, D], F32)

        ccs = [R.sem(f"cc{i}", barrier=False) for i in range(6)]
        RG = [list(range(NCORE))]
        cst = _mk_consts(R)

        sg_sb = R.sbuf("sg_sb", [P, 64], F32)
        su_sb = R.sbuf("su_sb", [P, 64], F32)
        sd_sb = R.sbuf("sd_sb", [P, 16], F32)
        sld = R.sem("sld")
        sini = R.sem("sini")
        R.g.dma_start(out=sg_sb[:], in_=s_gate).inc(sld, 16)
        R.g.dma_start(out=su_sb[:], in_=s_up).inc(sld, 16)
        R.g.dma_start(out=sd_sb[:], in_=s_down).inc(sld, 16)
        R.v.wait(sld)
        R.v.tensor_single_scalar(sd_sb[:], sd_sb[:], 0.5, ALU.mult).inc(sini, 1)
        for e in (R.t, R.s, R.y, R.g, R.v):
            e.wait(sld)
            e.wait(sini)

        W0 = dict(w_gate=w_gate, w_up=w_up, w_down=w_down,
                  sg_sb=sg_sb, su_sb=su_sb, sd_sb=sd_sb)

        def phase(fn, name="?", ccw=()):
            with contextlib.ExitStack() as ph:
                R.cur = ph
                R.phase = name
                R.phase_barrier()
                for eng, sem in ccw:
                    eng.wait(sem, 1)
                fn()
            R.cur = stack

        def f0a_body():
            run_jobs, njobs, finish = make_dequant_jobs(R, dict(
                gWp=gWp, gSc=gSc, uWp=uWp, uSc=uSc, dWp=dWp, dSc=dSc,
                ident=ident,
                wg_t_part=dq_part[0:D, :], wu_t_part=dq_part[D:2 * D, :],
                wd_t_part=dq_part[2 * D:3 * D, :]), cst, "dq")

            def mid():
                dq_joins = finish()
                for sem, valv in dq_joins:
                    R.g.wait(sem, valv)
                R.g.collective_compute("AllGather", ALU.bypass,
                                       replica_groups=RG,
                                       ins=[dq_part], outs=[dq_all]
                                       ).inc(ccs[0], 1)

            emit_ffn_pass(R, x_in, x1, W0, cst, "f0a", inter=8192, kind="l0",
                          co_job=lambda: run_jobs(3))
            R.dq_mid = mid
            R.dq_run = run_jobs
        phase(f0a_body, "f0a")

        kcp = R.sem("kcp")

        def after_k(R2, joins):
            for sem, valv in joins:
                R2.y.wait(sem, valv)
            R2.y.dma_start(out=kn1, in_=R2.qk_nmax[0:1, 32:64]).inc(kcp, 16)
            for sem, valv in joins:
                R2.g.wait(sem, valv)
            R2.g.wait(kcp)
            R2.g.collective_compute("AllGather", ALU.bypass, replica_groups=RG,
                                    ins=[kT], outs=[kT_all]).inc(ccs[1], 1)
            R2.g.collective_compute("AllGather", ALU.bypass, replica_groups=RG,
                                    ins=[kn1], outs=[kn_all8]).inc(ccs[3], 1)

        def after_v(R2, joins):
            for sem, valv in joins:
                R2.g.wait(sem, valv)
            R2.g.collective_compute("AllGather", ALU.bypass, replica_groups=RG,
                                    ins=[v_hb], outs=[v_all]).inc(ccs[2], 1)

        phase(lambda: emit_qkv0(R, x1, dict(
            wq_t=wq_t, wk_t=wk_t, wv_t=wv_t, bq=bq, bk=bk, bv=bv, cs2=cs2,
            qT_out=qT, kT_out=kT, v_out=v_hb, nrm_out=nrm), cst, "qk",
            after_k=after_k, after_v=after_v), "qk")

        kn_all_view = bass.AP(kn_all8.tensor, 0, [[256, 1], [1, 256]])
        phase(lambda: emit_attn0(R, dict(
            x_in=x1, qT=qT, kT=kT, v_hb=v_hb, kT_all=kT_all, v_all=v_all,
            qn=nrm[0:1, 0:32], kn_all=kn_all_view, fut=fut, masks=masks,
            wo_t=wo_t, bo=bo, x_out=x2), cst, "a0"), "a0",
              ccw=((R.y, ccs[1]), (R.g, ccs[2]), (R.y, ccs[3])))

        phase(lambda: emit_ffn_pass(R, x2, x3, W0, cst, "f0b",
                                    inter=8192, kind="l0",
                                    co_job=lambda: R.dq_run(3),
                                    mid_hook=R.dq_mid), "f0b")

        def wg_slice(n0):
            rk = n0 // D
            nl = n0 % D
            return dq_all[rk * 3 * D: rk * 3 * D + D,
                          nl:nl + 512].rearrange("(k p) n -> p k n", p=P)

        def wu_slice(n0):
            rk = n0 // D
            nl = n0 % D
            return dq_all[rk * 3 * D + D: rk * 3 * D + 2 * D,
                          nl:nl + 512].rearrange("(k p) n -> p k n", p=P)

        def wd_slice(n0, h0, hl):
            rk = n0 // D
            nl = n0 % D
            base = rk * 3 * D + 2 * D
            return dq_all[base + nl: base + nl + P, h0:h0 + hl]

        W1 = dict(wg_slice=wg_slice, wu_slice=wu_slice, wd_slice=wd_slice)
        phase(lambda: emit_ffn_pass(R, x3, x4, W1, cst, "f1a",
                                    inter=16384, kind="l1"), "f1a",
              ccw=((R.g, ccs[0]),))

        phase(lambda: emit_attn1a(R, x4, dict(
            qw_t=qw_t, kw_t=kw_t, vw_t=vw_t, qb=qb1, kb=kb1, vb=vb1,
            qpT_out=qpT, kv_out=kv_part, ks_out=ks_part), cst, "a1"), "a1")
        for s in [s for s in R.semval if s not in getattr(R, "nobar", set())]:
            R.g.wait(s)
        R.g.collective_compute("AllReduce", ALU.add, replica_groups=RG,
                               ins=[kv_part], outs=[kv_sum]).inc(ccs[4], 1)
        R.g.collective_compute("AllReduce", ALU.add, replica_groups=RG,
                               ins=[ks_part], outs=[ks_sum]).inc(ccs[5], 1)

        phase(lambda: emit_attn1b(R, dict(
            x_in=x4, qpT=qpT, kv_sum=kv_sum, ks_sum=ks_sum, ident=ident,
            ow_t=ow_t, ob=ob1, x_out=x5), cst, "b1"), "b1",
              ccw=((R.g, ccs[4]), (R.g, ccs[5])))

        phase(lambda: emit_ffn_pass(R, x5, x_out, W1, cst, "f1b",
                                    inter=16384, kind="l1"), "f1b")
        R.phase_barrier()
        R.emit()
    return nc


def kernel_fused(**ins):
    cores = list(range(NCORE))
    PM = ROPE_PERM
    base = dict(
        w_gate=np.asarray(ins["L0_ffn_w_gate"]),
        w_up=np.asarray(ins["L0_ffn_w_up"]),
        w_down=np.asarray(ins["L0_ffn_w_down"]),
        s_gate=scale_tile(ins["L0_ffn_s_gate"]),
        s_up=scale_tile(ins["L0_ffn_s_up"]),
        s_down=scale_tile(ins["L0_ffn_s_down"]),
        wq_t=np.ascontiguousarray(np.asarray(ins["L0_attn_Wq"], np.float32)[PM].T),
        wk_t=np.ascontiguousarray(np.asarray(ins["L0_attn_Wk"], np.float32)[PM].T),
        wv_t=np.ascontiguousarray(np.asarray(ins["L0_attn_Wv"], np.float32).T),
        bq=bias_tile(ins["L0_attn_bq"], PM),
        bk=bias_tile(ins["L0_attn_bk"], PM),
        bv=np.asarray(ins["L0_attn_bv"], np.float32).reshape(1, D),
        wo_t=np.ascontiguousarray(np.asarray(ins["L0_attn_Wo"], np.float32).T),
        bo=bias_tile(ins["L0_attn_bo"]),
        masks=host_masks(),
        ident=np.eye(P).astype(BF),
        qw_t=np.ascontiguousarray(np.asarray(ins["L1_attn_q_w"], np.float32).T),
        kw_t=np.ascontiguousarray(np.asarray(ins["L1_attn_k_w"], np.float32).T),
        vw_t=np.ascontiguousarray(np.asarray(ins["L1_attn_v_w"], np.float32).T),
        qb1=bias_tile(ins["L1_attn_q_b"]),
        kb1=np.asarray(ins["L1_attn_k_b"], np.float32).reshape(1, D),
        vb1=np.asarray(ins["L1_attn_v_b"], np.float32).reshape(1, D),
        ow_t=np.ascontiguousarray(np.asarray(ins["L1_attn_o_w"], np.float32).T),
        ob1=bias_tile(ins["L1_attn_o_b"]),
    )
    x = np.asarray(ins["L0_attn_x"], np.float32)
    xsh = shard_x(x)
    im = []
    for r in cores:
        im.append(dict(
            x_in=xsh[r], cs2=host_rope_tables(r), fut=host_fut(r),
            gWp=np.asarray(ins["L1_ffn_gate_W_packed"])[r * D:(r + 1) * D],
            gSc=np.asarray(ins["L1_ffn_gate_scales"])[r * D:(r + 1) * D],
            uWp=np.asarray(ins["L1_ffn_up_W_packed"])[r * D:(r + 1) * D],
            uSc=np.asarray(ins["L1_ffn_up_scales"])[r * D:(r + 1) * D],
            dWp=np.ascontiguousarray(
                np.asarray(ins["L1_ffn_down_W_packed"])[:, r * 1024:(r + 1) * 1024]),
            dSc=np.ascontiguousarray(
                np.asarray(ins["L1_ffn_down_scales"])[:, r * 64:(r + 1) * 64]),
            **base))
    if "fused" not in _NC_CACHE:
        _NC_CACHE["fused"] = build_fused()
    res = run_bass_kernel_spmd(_NC_CACHE["fused"], im, core_ids=cores)
    return unshard_x([np.asarray(r["x_out"]) for r in res.results])


_NC_CACHE = {}


def _get_nc(name):
    if name not in _NC_CACHE:
        _NC_CACHE[name] = {"ffn0": build_ffn0, "qkv0": build_qkv0,
                           "attn0": build_attn0, "dequant": build_dequant,
                           "ffn1": build_ffn1, "attn1a": build_attn1a,
                           "attn1b": build_attn1b}[name]()
    return _NC_CACHE[name]


def _launch(name, in_maps, out_names):
    nc = _get_nc(name)
    res = run_bass_kernel_spmd(nc, in_maps, core_ids=list(range(NCORE)))
    return [{k: np.asarray(r[k]) for k in out_names} for r in res.results]


def kernel_unfused(**ins):
    cores = list(range(NCORE))
    ident = np.eye(P).astype(BF)
    masks = host_masks()

    x = np.asarray(ins["L0_attn_x"], np.float32)
    xsh = shard_x(x)
    f0_in = dict(w_gate=np.asarray(ins["L0_ffn_w_gate"]),
                 w_up=np.asarray(ins["L0_ffn_w_up"]),
                 w_down=np.asarray(ins["L0_ffn_w_down"]),
                 s_gate=scale_tile(ins["L0_ffn_s_gate"]),
                 s_up=scale_tile(ins["L0_ffn_s_up"]),
                 s_down=scale_tile(ins["L0_ffn_s_down"]))
    o = _launch("ffn0", [dict(x_in=xsh[r], **f0_in) for r in cores], ["x_out"])
    x1sh = [o[r]["x_out"] for r in cores]

    PM = ROPE_PERM
    qkv_in = dict(
        wq_t=np.ascontiguousarray(np.asarray(ins["L0_attn_Wq"], np.float32)[PM].T),
        wk_t=np.ascontiguousarray(np.asarray(ins["L0_attn_Wk"], np.float32)[PM].T),
        wv_t=np.ascontiguousarray(np.asarray(ins["L0_attn_Wv"], np.float32).T),
        bq=bias_tile(ins["L0_attn_bq"], PM),
        bk=bias_tile(ins["L0_attn_bk"], PM),
        bv=np.asarray(ins["L0_attn_bv"], np.float32).reshape(1, D))
    im = []
    for r in cores:
        im.append(dict(x_in=x1sh[r], cs2=host_rope_tables(r), **qkv_in))
    o = _launch("qkv0", im, ["qT_out", "kT_out", "v_out", "nrm_out"])
    kT_all = np.ascontiguousarray(np.concatenate([o[r]["kT_out"] for r in cores], axis=0))
    v_all = np.ascontiguousarray(np.concatenate([o[r]["v_out"] for r in cores], axis=0))
    kn_all = np.ascontiguousarray(np.concatenate([o[r]["nrm_out"][:, 32:64] for r in cores], axis=1))

    a0_in = dict(kT_all=kT_all, v_all=v_all, kn_all=kn_all, masks=masks,
                 wo_t=np.ascontiguousarray(np.asarray(ins["L0_attn_Wo"], np.float32).T),
                 bo=bias_tile(ins["L0_attn_bo"]))
    im = []
    for r in cores:
        im.append(dict(x_in=x1sh[r], qT=o[r]["qT_out"], kT=o[r]["kT_out"],
                       v_hb=o[r]["v_out"],
                       qn=np.ascontiguousarray(o[r]["nrm_out"][:, 0:32]),
                       fut=host_fut(r), **a0_in))
    o = _launch("attn0", im, ["x_out"])
    x2sh = [o[r]["x_out"] for r in cores]

    o = _launch("ffn0", [dict(x_in=x2sh[r], **f0_in) for r in cores], ["x_out"])
    x3sh = [o[r]["x_out"] for r in cores]

    im = []
    for r in cores:
        im.append(dict(
            gWp=np.asarray(ins["L1_ffn_gate_W_packed"])[r * D:(r + 1) * D],
            gSc=np.asarray(ins["L1_ffn_gate_scales"])[r * D:(r + 1) * D],
            uWp=np.asarray(ins["L1_ffn_up_W_packed"])[r * D:(r + 1) * D],
            uSc=np.asarray(ins["L1_ffn_up_scales"])[r * D:(r + 1) * D],
            dWp=np.ascontiguousarray(
                np.asarray(ins["L1_ffn_down_W_packed"])[:, r * 1024:(r + 1) * 1024]),
            dSc=np.ascontiguousarray(
                np.asarray(ins["L1_ffn_down_scales"])[:, r * 64:(r + 1) * 64]),
            ident=ident))
    o = _launch("dequant", im, ["wg_t_part", "wu_t_part", "wd_t_part"])
    wg_t = np.ascontiguousarray(np.concatenate([o[r]["wg_t_part"] for r in cores], axis=1))
    wu_t = np.ascontiguousarray(np.concatenate([o[r]["wu_t_part"] for r in cores], axis=1))
    wd_t = np.ascontiguousarray(np.concatenate([o[r]["wd_t_part"] for r in cores], axis=0))

    f1_in = dict(wg_t=wg_t, wu_t=wu_t, wd_t=wd_t)
    o = _launch("ffn1", [dict(x_in=x3sh[r], **f1_in) for r in cores], ["x_out"])
    x4sh = [o[r]["x_out"] for r in cores]

    a1_in = dict(
        qw_t=np.ascontiguousarray(np.asarray(ins["L1_attn_q_w"], np.float32).T),
        kw_t=np.ascontiguousarray(np.asarray(ins["L1_attn_k_w"], np.float32).T),
        vw_t=np.ascontiguousarray(np.asarray(ins["L1_attn_v_w"], np.float32).T),
        qb=bias_tile(ins["L1_attn_q_b"]),
        kb=np.asarray(ins["L1_attn_k_b"], np.float32).reshape(1, D),
        vb=np.asarray(ins["L1_attn_v_b"], np.float32).reshape(1, D))
    o = _launch("attn1a", [dict(x_in=x4sh[r], **a1_in) for r in cores],
                ["qpT_out", "kv_out", "ks_out"])
    kv_all = np.ascontiguousarray(np.concatenate([o[r]["kv_out"] for r in cores], axis=0))
    ks_all = np.ascontiguousarray(np.concatenate([o[r]["ks_out"] for r in cores], axis=0))

    b1_in = dict(kv_all=kv_all, ks_all=ks_all, ident=ident,
                 ow_t=np.ascontiguousarray(np.asarray(ins["L1_attn_o_w"], np.float32).T),
                 ob=bias_tile(ins["L1_attn_o_b"]))
    im = [dict(x_in=x4sh[r], qpT=o[r]["qpT_out"], **b1_in) for r in cores]
    o = _launch("attn1b", im, ["x_out"])
    x5sh = [o[r]["x_out"] for r in cores]

    o = _launch("ffn1", [dict(x_in=x5sh[r], **f1_in) for r in cores], ["x_out"])
    return unshard_x([o[r]["x_out"] for r in cores])


def kernel(**ins):
    return kernel_fused(**ins)


# version 25
# speedup vs baseline: 1.0701x; 1.0701x over previous
import contextlib
import numpy as np
import ml_dtypes
import concourse.bass as bass
import concourse.mybir as mybir
from concourse.bass_utils import run_bass_kernel_spmd

F32 = mybir.dt.float32
BF16 = mybir.dt.bfloat16
I32 = mybir.dt.int32
I8 = mybir.dt.int8
AF = mybir.ActivationFunctionType
ALU = mybir.AluOpType
AX = mybir.AxisListType

P = 128
D = 2048
NE = D // P
B = 2
S = 8192
NCORE = 8
STRIPE = 512
NSTRIPE = S // STRIPE
TPC = 2048
NTC = TPC // STRIPE
LN_EPS = 1e-5
BF = ml_dtypes.bfloat16


def stripes_of(r):
    return [r, NSTRIPE - 1 - r]


def chunks_of(r):
    return [(b, s) for b in range(B) for s in stripes_of(r)]



class _OpHandle:
    def __init__(self, rec, entry):
        self.rec = rec
        self.entry = entry

    def inc(self, sem, n=1):
        self.entry[3].append((sem, n))
        self.rec.semval[sem] = self.rec.semval.get(sem, 0) + n
        return self


class _EngProxy:
    def __init__(self, rec, eng):
        object.__setattr__(self, "_rec", rec)
        object.__setattr__(self, "_eng", eng)

    def wait(self, sem, val=None):
        rec = self._rec
        if val is None:
            val = rec.semval.get(sem, 0)
        if val <= 0:
            return
        if val <= rec.lastwait[self._eng].get(sem, -1):
            return
        rec.lastwait[self._eng][sem] = val
        rec.streams[self._eng].append(("wait", sem, val, []))

    def __getattr__(self, m):
        if m.startswith("_"):
            raise AttributeError(m)
        rec, eng = self._rec, self._eng

        def call(*a, **k):
            entry = ["op", m, (a, k), [], rec.phase]
            rec.streams[eng].append(entry)
            return _OpHandle(rec, entry)
        return call


class Rec:
    def __init__(self, nc, stack):
        self.nc = nc
        self.stack = stack
        self.cur = stack
        self.streams = {e: [] for e in ("tensor", "vector", "scalar", "gpsimd", "sync")}
        self.semval = {}
        self.lastwait = {e: {} for e in self.streams}
        self._semnames = 0
        self.phase = "init"
        self.t = _EngProxy(self, "tensor")
        self.v = _EngProxy(self, "vector")
        self.s = _EngProxy(self, "scalar")
        self.g = _EngProxy(self, "gpsimd")
        self.y = _EngProxy(self, "sync")

    def sem(self, name=None, pool=None, barrier=True):
        if not barrier:
            s = self.stack.enter_context(self.nc.semaphore(name or "nb"))
            self.semval[s] = 0
            if not hasattr(self, "nobar"):
                self.nobar = set()
            self.nobar.add(s)
            return s
        if pool is not None:
            if not hasattr(self, "pools"):
                self.pools = {}
            if pool not in self.pools:
                s = self.stack.enter_context(self.nc.semaphore(f"pool_{pool}"))
                self.semval[s] = 0
                self.pools[pool] = s
            return self.pools[pool]
        if name is None:
            name = f"sem{self._semnames}"
            self._semnames += 1
        s = self.stack.enter_context(self.nc.semaphore(name))
        self.semval[s] = 0
        return s

    def phase_barrier(self):
        nob = getattr(self, "nobar", set())
        sems = [s for s in self.semval if s not in nob]
        for e in (self.t, self.v, self.s, self.g, self.y):
            for s in sems:
                e.wait(s)

    def val(self, sem):
        return self.semval.get(sem, 0)

    def emit(self):
        with self.nc.Block() as block:
            for eng_name in ("tensor", "vector", "scalar", "gpsimd", "sync"):
                stream = self.streams[eng_name]
                if not stream:
                    continue

                def make(stream):
                    def body(eng):
                        for entry in stream:
                            if entry[0] == "wait":
                                _, sem, valv, _ = entry
                                eng.wait_ge(sem, valv)
                            else:
                                _, m, (a, k), incs, ph = entry
                                inst = getattr(eng, m)(*a, **k)
                                try:
                                    inst.annotate(ph)
                                except Exception:
                                    pass
                                for sem, n in incs:
                                    inst = inst.then_inc(sem, n)
                    return body
                getattr(block, eng_name)(make(stream))

    def sbuf(self, name, shape, dtype, stack=None):
        return (stack or self.cur).enter_context(
            self.nc.sbuf_tensor(name, shape, dtype))

    def psum(self, name, shape, dtype=F32, stack=None):
        return (stack or self.cur).enter_context(
            self.nc.psum_tensor(name, shape, dtype))



def emit_ln(R, x_dram, xn_sb, cst, tag, lstack):
    ld = [R.sem(pool=f"y{i}") for i in range(2)]
    mm = R.sem(pool="pe0")
    evs = R.sem(pool="act0")
    evv = R.sem(pool="dve0")
    x3 = x_dram.rearrange("(k p) t -> p k t", p=P)

    xbuf = [R.sbuf(f"{tag}_x{i}", [P, NE * STRIPE], F32, lstack) for i in range(2)]
    xbf = R.sbuf(f"{tag}_xbf", [P, 8 * STRIPE], BF16, lstack)
    sqbuf = R.sbuf(f"{tag}_sq", [P, 2 * STRIPE], BF16, lstack)
    stat = R.sbuf(f"{tag}_stat", [1, 6 * STRIPE], F32, lstack)
    tmpbuf = R.sbuf(f"{tag}_tmp", [P, 2 * STRIPE], F32, lstack)
    bcast = R.sbuf(f"{tag}_bc", [P, 2 * STRIPE], F32, lstack)
    p_sum = R.psum(f"{tag}_ps", [1, STRIPE], stack=lstack)
    p_sq = R.psum(f"{tag}_pq", [1, STRIPE], stack=lstack)
    p_bc = R.psum(f"{tag}_pb", [P, 2 * STRIPE], stack=lstack)

    norm_done = {}
    sq_hist = []
    for tc in range(NTC):
        buf = xbuf[tc % 2]
        if tc >= 2:
            R.y.wait(evv, norm_done[tc - 2])
        for kq in range(4):
            R.y.dma_start(
                out=buf[:].rearrange("p (k t) -> p k t", k=NE)[:, 4 * kq:4 * kq + 4, :],
                in_=x3[:, 4 * kq:4 * kq + 4, tc * STRIPE:(tc + 1) * STRIPE],
            ).inc(ld[tc % 2], 16)
        R.t.wait(ld[tc % 2])
        R.s.wait(ld[tc % 2])
        R.v.wait(ld[tc % 2])
        for k in range(NE):
            if k % 8 == 0:
                if len(sq_hist) >= 1:
                    R.v.wait(mm, sq_hist[-1])
                R.v.wait(evs)
                R.v.tensor_copy(xbf[:],
                                buf[:, (k // 8) * 8 * STRIPE:
                                    (k // 8 + 1) * 8 * STRIPE]).inc(evv, 1)
            xt = xbf[:, (k % 8) * STRIPE:(k % 8 + 1) * STRIPE]
            sq = sqbuf[:, (k % 2) * STRIPE:(k % 2 + 1) * STRIPE]
            slot = tc * NE + k
            if slot >= 2:
                R.s.wait(mm, sq_hist[slot - 2])
            R.s.wait(evv)
            R.s.activation(sq, xt, AF.Square).inc(evs, 1)
            R.t.wait(evs)
            R.t.matmul(p_sum[:], cst["ones_col_bf"][:], xt,
                       start=(k == 0), stop=(k == NE - 1))
            R.t.matmul(p_sq[:], cst["ones_col_bf"][:], sq,
                       start=(k == 0), stop=(k == NE - 1)).inc(mm, 1)
            sq_hist.append(R.val(mm))
        mean = stat[0:1, 0:STRIPE]
        m2 = stat[0:1, STRIPE:2 * STRIPE]
        var = stat[0:1, 2 * STRIPE:3 * STRIPE]
        sd = stat[0:1, 3 * STRIPE:4 * STRIPE]
        rstd = stat[0:1, 4 * STRIPE:5 * STRIPE]
        mb = stat[0:1, 5 * STRIPE:6 * STRIPE]
        R.v.wait(mm)
        R.v.tensor_single_scalar(mean, p_sum[:], 1.0 / D, ALU.mult).inc(evv, 1)
        R.s.wait(evv)
        R.s.activation(m2, mean, AF.Square).inc(evs, 1)
        R.v.wait(evs)
        R.v.scalar_tensor_tensor(var, p_sq[:], 1.0 / D, m2,
                                 ALU.mult, ALU.subtract).inc(evv, 1)
        R.v.wait(evv)
        R.v.tensor_single_scalar(var, var, LN_EPS, ALU.add).inc(evv, 1)
        R.s.wait(evv)
        R.s.activation(sd, var, AF.Sqrt).inc(evs, 1)
        R.v.wait(evs)
        R.v.reciprocal(rstd, sd).inc(evv, 1)
        R.v.wait(evv)
        R.v.scalar_tensor_tensor(mb, mean, -1.0, rstd, ALU.mult, ALU.mult).inc(evv, 1)
        R.t.wait(evv)
        R.t.matmul(p_bc[:, 0:STRIPE], cst["ones_row_f32"][:], rstd,
                   start=True, stop=True)
        R.t.matmul(p_bc[:, STRIPE:2 * STRIPE], cst["ones_row_f32"][:], mb,
                   start=True, stop=True).inc(mm, 1)
        R.s.wait(mm)
        R.s.activation(bcast[:], p_bc[:], AF.Copy).inc(evs, 1)
        R.v.wait(evs)
        for k in range(NE):
            xt = buf[:, k * STRIPE:(k + 1) * STRIPE]
            tmp = tmpbuf[:, (k % 2) * STRIPE:(k % 2 + 1) * STRIPE]
            R.v.wait(evv)
            R.v.tensor_tensor(tmp, xt, bcast[:, 0:STRIPE], ALU.mult).inc(evv, 1)
            R.v.wait(evv)
            R.v.tensor_tensor(
                xn_sb[:, k * TPC + tc * STRIPE: k * TPC + (tc + 1) * STRIPE],
                tmp, bcast[:, STRIPE:2 * STRIPE], ALU.add).inc(evv, 1)
        norm_done[tc] = R.val(evv)
    return evv, R.val(evv)


def emit_ffn_pass(R, x_dram, xout_dram, W, cst, tag, *, inter, kind, stop_after=None, max_chunks=8, co_job=None, mid_hook=None):
    nc = R.nc
    NSUP = inter // 512
    NSTEP = inter // 128
    DEPTH = 3
    xn = R.sbuf(f"{tag}_xn", [P, NE * TPC], BF16)

    with contextlib.ExitStack() as lnstack:
        lnev, ln_done = emit_ln(R, x_dram, xn, cst, f"{tag}_ln", lnstack)

    ld = [R.sem(pool=f"g{i}") for i in range(2)]
    fse = [R.sem(pool=f"y{[2, 3, 8][i]}") for i in range(DEPTH)]
    mm = R.sem(pool="pe0")
    evs = R.sem(pool="act0")
    evv = R.sem(pool="dve0")

    fuse_dram = nc.dram_tensor(f"{tag}_fuse", [inter, TPC], BF16, kind="Internal").ap()

    with contextlib.ExitStack() as p1stack:
        wg_buf = [R.sbuf(f"{tag}_wg{i}", [P, NE * 512], BF16, p1stack) for i in range(2)]
        wu_buf = [R.sbuf(f"{tag}_wu{i}", [P, NE * 512], BF16, p1stack) for i in range(2)]
        g_sb = R.sbuf(f"{tag}_g", [P, DEPTH * STRIPE], F32, p1stack)
        u_sb = R.sbuf(f"{tag}_u", [P, DEPTH * STRIPE], F32, p1stack)
        f_sb = R.sbuf(f"{tag}_f", [P, DEPTH * STRIPE], BF16, p1stack)
        gel = R.sbuf(f"{tag}_gel", [P, DEPTH * 2 * STRIPE], F32, p1stack) \
            if kind == "l1" else None
        p_g = [R.psum(f"{tag}_pg{i}", [P, STRIPE], stack=p1stack) for i in range(DEPTH)]
        p_u = [R.psum(f"{tag}_pu{i}", [P, STRIPE], stack=p1stack) for i in range(DEPTH)]
        if co_job is not None:
            R.dq_pt = [R.psum(f"{tag}_dqpt{i}", [P, 1024], BF16, stack=p1stack)
                       for i in range(2)]

        if "wg_slice" in W:
            wg_slice = W["wg_slice"]
            wu_slice = W["wu_slice"]
        else:
            wg_full = (W["w_gate"] if kind == "l0" else W["wg_t"]).rearrange(
                "(k p) n -> p k n", p=P)
            wu_full = (W["w_up"] if kind == "l0" else W["wu_t"]).rearrange(
                "(k p) n -> p k n", p=P)
            wg_slice = lambda n0: wg_full[:, :, n0:n0 + 512]
            wu_slice = lambda n0: wu_full[:, :, n0:n0 + 512]

        R.g.wait(lnev, ln_done)
        sup_mm = []
        pipe = 0
        tile_evs = []
        tile_evv = []
        tile_fse = []
        for sup in range(NSUP):
            par = sup % 2
            if sup >= 2:
                R.g.wait(mm, sup_mm[sup - 2])
            n0 = sup * 512
            for kq in range(4):
                R.g.dma_start(
                    out=wg_buf[par][:].rearrange("p (k n) -> p k n", k=NE)
                    [:, 4 * kq:4 * kq + 4, :],
                    in_=wg_slice(n0)[:, 4 * kq:4 * kq + 4, :]).inc(ld[par], 16)
                R.g.dma_start(
                    out=wu_buf[par][:].rearrange("p (k n) -> p k n", k=NE)
                    [:, 4 * kq:4 * kq + 4, :],
                    in_=wu_slice(n0)[:, 4 * kq:4 * kq + 4, :]).inc(ld[par], 16)
            R.t.wait(ld[par])
            if sup == 0:
                R.t.wait(lnev, ln_done)
            if co_job is not None:
                co_job()
            for ns in range(4):
                for tc in range(NTC):
                    pp = pipe % DEPTH
                    if pipe >= DEPTH:
                        R.t.wait(evs, tile_evs[pipe - DEPTH])
                        if kind == "l1":
                            R.t.wait(evv, tile_evv[pipe - DEPTH])
                    for k in range(NE):
                        xnt = xn[:, k * TPC + tc * STRIPE: k * TPC + (tc + 1) * STRIPE]
                        R.t.matmul(p_g[pp][:],
                                   wg_buf[par][:, k * 512 + ns * 128:
                                               k * 512 + (ns + 1) * 128],
                                   xnt, start=(k == 0), stop=(k == NE - 1))
                    for k in range(NE):
                        xnt = xn[:, k * TPC + tc * STRIPE: k * TPC + (tc + 1) * STRIPE]
                        R.t.matmul(p_u[pp][:],
                                   wu_buf[par][:, k * 512 + ns * 128:
                                               k * 512 + (ns + 1) * 128],
                                   xnt, start=(k == 0),
                                   stop=(k == NE - 1)).inc(mm, 1 if k == NE - 1 else 0)
                    nt = sup * 4 + ns
                    gs = g_sb[:, pp * STRIPE:(pp + 1) * STRIPE]
                    us = u_sb[:, pp * STRIPE:(pp + 1) * STRIPE]
                    fs = f_sb[:, pp * STRIPE:(pp + 1) * STRIPE]
                    R.s.wait(mm)
                    if pipe >= DEPTH:
                        R.s.wait(evv, tile_evv[pipe - DEPTH])
                        R.s.wait(fse[pp], tile_fse[pipe - DEPTH])
                        R.v.wait(fse[pp], tile_fse[pipe - DEPTH])
                    if kind == "l0":
                        R.s.activation(gs, p_g[pp][:], AF.Silu,
                                       scale=W["sg_sb"][:, nt:nt + 1]).inc(evs, 1)
                        R.s.activation(us, p_u[pp][:], AF.Copy,
                                       scale=W["su_sb"][:, nt:nt + 1]).inc(evs, 1)
                        R.v.wait(evs)
                        R.v.tensor_tensor(fs, gs, us, ALU.mult).inc(evv, 1)
                    else:
                        emit_gelu_fuse(R, p_g[pp][:], p_u[pp][:], gs, us, fs,
                                       gel, pp, evs, evv,
                                       tile_evv[pipe - DEPTH] if pipe >= DEPTH else 0)
                    R.y.wait(evv)
                    R.y.dma_start(out=fuse_dram[nt * P:(nt + 1) * P,
                                                tc * STRIPE:(tc + 1) * STRIPE],
                                  in_=fs).inc(fse[pp], 16)
                    tile_evs.append(R.val(evs))
                    tile_evv.append(R.val(evv))
                    tile_fse.append(R.val(fse[pp]))
                    pipe += 1
            sup_mm.append(R.val(mm))
        phase1_evs = R.val(evs)
        phase1_evv = R.val(evv)
        phase1_fse = [R.val(s) for s in fse]

    if mid_hook is not None:
        mid_hook()

    HH = 1024
    with contextlib.ExitStack() as p2stack:
        wd_buf = [R.sbuf(f"{tag}_wd{i}", [P, 4 * HH], BF16, p2stack) for i in range(2)]
        fu_buf = [R.sbuf(f"{tag}_fu{i}", [P, 4 * STRIPE], BF16, p2stack) for i in range(2)]
        xr_buf = R.sbuf(f"{tag}_xr", [P, 8 * STRIPE], F32, p2stack)
        o_sb = R.sbuf(f"{tag}_o", [P, 8 * STRIPE], F32, p2stack)
        p_d = R.psum(f"{tag}_pd", [P, 8 * STRIPE], stack=p2stack)
        ld2g = [R.sem(pool=f"g{2 + i}") for i in range(2)]
        ld2y = [R.sem(pool=f"y{4 + i}") for i in range(2)]
        xld = R.sem(pool="s0")
        st = R.sem(pool="y6")
        x3 = x_dram.rearrange("(k p) t -> p k t", p=P)
        xo3 = xout_dram.rearrange("(k p) t -> p k t", p=P)
        if "wd_slice" in W:
            wd_slice = W["wd_slice"]
        else:
            wd_full = W["w_down"] if kind == "l0" else W["wd_t"]
            wd_slice = lambda n0, h0, hl: wd_full[n0:n0 + P, h0:h0 + hl]

        for e in (R.y, R.g, R.s, R.t):
            e.wait(evs, phase1_evs)
            e.wait(evv, phase1_evv)
            for i, s in enumerate(fse):
                e.wait(s, phase1_fse[i])

        step = 0
        step_mm = []
        chunk_evv = []
        chunk_st = []
        GRP = 4
        for hh in range(2):
            for tc in range(NTC):
                ci = hh * NTC + tc
                if ci >= 1:
                    R.s.wait(st, chunk_st[ci - 1])
                    R.s.wait(evv, chunk_evv[ci - 1])
                R.s.dma_start(
                    out=xr_buf[:].rearrange("p (k t) -> p k t", k=8),
                    in_=x3[:, hh * 8:hh * 8 + 8,
                           tc * STRIPE:(tc + 1) * STRIPE]).inc(xld, 16)
                for ng in range(NSTEP // GRP):
                    par = step % 2
                    if step >= 2:
                        R.g.wait(mm, step_mm[step - 2])
                        R.y.wait(mm, step_mm[step - 2])
                    for j in range(GRP):
                        R.g.dma_start(
                            out=wd_buf[par][:, j * HH:(j + 1) * HH],
                            in_=wd_slice((ng * GRP + j) * P, hh * HH, HH)
                            ).inc(ld2g[par], 16)
                    R.y.dma_start(
                        out=fu_buf[par][:].rearrange("p (j t) -> p j t", j=GRP),
                        in_=fuse_dram[ng * GRP * P:(ng + 1) * GRP * P,
                                      tc * STRIPE:(tc + 1) * STRIPE]
                        .rearrange("(j p) t -> p j t", p=P)).inc(ld2y[par], 16)
                    R.t.wait(ld2g[par])
                    R.t.wait(ld2y[par])
                    if ng == 0 and ci >= 1:
                        R.t.wait(evv, chunk_evv[ci - 1])
                    for j in range(GRP):
                        n = ng * GRP + j
                        for ht in range(8):
                            R.t.matmul(p_d[:, ht * STRIPE:(ht + 1) * STRIPE],
                                       wd_buf[par][:, j * HH + ht * P:
                                                    j * HH + (ht + 1) * P],
                                       fu_buf[par][:, j * STRIPE:(j + 1) * STRIPE],
                                       start=(n == 0),
                                       stop=(n == NSTEP - 1)).inc(
                                           mm, 1 if ht == 7 and j == GRP - 1 else 0)
                    step_mm.append(R.val(mm))
                    step += 1
                R.v.wait(mm)
                R.v.wait(xld)
                if ci >= 1:
                    R.v.wait(st, chunk_st[ci - 1])
                for ht in range(8):
                    k = hh * 8 + ht
                    ot = o_sb[:, ht * STRIPE:(ht + 1) * STRIPE]
                    xt = xr_buf[:, ht * STRIPE:(ht + 1) * STRIPE]
                    pd = p_d[:, ht * STRIPE:(ht + 1) * STRIPE]
                    if kind == "l0":
                        R.v.tensor_scalar(ot, pd, W["sd_sb"][:, k:k + 1], None,
                                          ALU.mult).inc(evv, 1)
                        R.v.wait(evv)
                        R.v.tensor_tensor(ot, ot, xt, ALU.add).inc(evv, 1)
                    else:
                        R.v.scalar_tensor_tensor(ot, pd, 0.5, xt,
                                                 ALU.mult, ALU.add).inc(evv, 1)
                chunk_evv.append(R.val(evv))
                R.y.wait(evv)
                R.y.dma_start(out=xo3[:, hh * 8:hh * 8 + 8,
                                      tc * STRIPE:(tc + 1) * STRIPE],
                              in_=o_sb[:].rearrange("p (k t) -> p k t", k=8)
                              ).inc(st, 16)
                chunk_st.append(R.val(st))
        R.y.wait(st)
    return st


def emit_gelu_fuse(R, p_g, p_u, gs, us, fs, gel, pp, evs, evv, prev_evv):
    s1 = gel[:, (2 * pp + 0) * STRIPE:(2 * pp + 1) * STRIPE]
    w = gel[:, (2 * pp + 1) * STRIPE:(2 * pp + 2) * STRIPE]
    CC = 0.7978845608028654
    R.s.activation(s1, p_g, AF.Square).inc(evs, 1)
    R.s.activation(us, p_u, AF.Copy).inc(evs, 1)
    R.v.wait(evs)
    R.v.wait(evv, prev_evv)
    R.v.tensor_scalar(s1, s1, CC * 0.044715, CC, ALU.mult, ALU.add).inc(evv, 1)
    R.v.wait(evv)
    R.v.tensor_tensor(w, s1, p_g, ALU.mult).inc(evv, 1)
    R.v.tensor_tensor(gs, us, p_g, ALU.mult).inc(evv, 1)
    R.s.wait(evv)
    R.s.activation(w, w, AF.Tanh).inc(evs, 1)
    R.v.wait(evs)
    R.v.tensor_scalar(w, w, 0.5, 0.5, ALU.mult, ALU.add).inc(evv, 1)
    R.v.wait(evv)
    R.v.tensor_tensor(fs, gs, w, ALU.mult).inc(evv, 1)


def _mk_consts(R):
    cst = {}
    cst["ones_col_f32"] = R.sbuf("c_ones_col", [P, 1], F32)
    cst["ones_row_f32"] = R.sbuf("c_ones_row", [1, P], F32)
    cst["ones_col_bf"] = R.sbuf("c_ones_col_bf", [P, 1], BF16)
    cst["ones_row_bf"] = R.sbuf("c_ones_row_bf", [1, P], BF16)
    csem = R.sem("c_init")
    R.v.memset(cst["ones_col_f32"][:], 1.0).inc(csem, 1)
    R.v.memset(cst["ones_row_f32"][:], 1.0).inc(csem, 1)
    R.v.memset(cst["ones_col_bf"][:], 1.0).inc(csem, 1)
    R.v.memset(cst["ones_row_bf"][:], 1.0).inc(csem, 1)
    for e in (R.t, R.s, R.y, R.g):
        e.wait(csem)
    return cst



def build_ffn0(stop_after=None, max_chunks=8):
    nc = bass.Bass("TRN2", target_bir_lowering=False)
    with contextlib.ExitStack() as stack:
        R = Rec(nc, stack)
        x_in = nc.dram_tensor("x_in", [D, TPC], F32, kind="ExternalInput").ap()
        w_gate = nc.dram_tensor("w_gate", [D, 8192], I8, kind="ExternalInput").ap()
        w_up = nc.dram_tensor("w_up", [D, 8192], I8, kind="ExternalInput").ap()
        w_down = nc.dram_tensor("w_down", [8192, D], I8, kind="ExternalInput").ap()
        s_gate = nc.dram_tensor("s_gate", [P, 64], F32, kind="ExternalInput").ap()
        s_up = nc.dram_tensor("s_up", [P, 64], F32, kind="ExternalInput").ap()
        s_down = nc.dram_tensor("s_down", [P, 16], F32, kind="ExternalInput").ap()
        x_out = nc.dram_tensor("x_out", [D, TPC], F32, kind="ExternalOutput").ap()

        cst = _mk_consts(R)
        sg_sb = R.sbuf("sg_sb", [P, 64], F32)
        su_sb = R.sbuf("su_sb", [P, 64], F32)
        sd_sb = R.sbuf("sd_sb", [P, 16], F32)
        sld = R.sem("sld")
        sini = R.sem("sini")
        R.g.dma_start(out=sg_sb[:], in_=s_gate).inc(sld, 16)
        R.g.dma_start(out=su_sb[:], in_=s_up).inc(sld, 16)
        R.g.dma_start(out=sd_sb[:], in_=s_down).inc(sld, 16)
        R.v.wait(sld)
        R.v.tensor_single_scalar(sd_sb[:], sd_sb[:], 0.5, ALU.mult).inc(sini, 1)
        for e in (R.t, R.s, R.y, R.g, R.v):
            e.wait(sld)
            e.wait(sini)

        W = dict(w_gate=w_gate, w_up=w_up, w_down=w_down,
                 sg_sb=sg_sb, su_sb=su_sb, sd_sb=sd_sb)
        emit_ffn_pass(R, x_in, x_out, W, cst, "f0", inter=8192, kind="l0",
                      stop_after=stop_after, max_chunks=max_chunks)
        R.emit()
    return nc


SC_ATTN = 0.08838834764831845


def emit_proj_ch(R, xn, w_dram, bias_sb, out_cb, cst, tag, joins):
    with contextlib.ExitStack() as pstack:
        ld = [R.sem(pool=f"g{i}") for i in range(2)]
        mm = R.sem(pool="pe0")
        evs = R.sem(pool="act0")
        wbuf = [R.sbuf(f"{tag}_w{i}", [P, NE * 512], BF16, pstack) for i in range(2)]
        e_sb = R.sbuf(f"{tag}_e", [P, 2 * STRIPE], F32, pstack)
        p_o = [R.psum(f"{tag}_po{i}", [P, STRIPE], stack=pstack) for i in range(2)]
        w_src = w_dram.rearrange("(k p) n -> p k n", p=P)

        for e in (R.g, R.t, R.s, R.v, R.y):
            for sem, valv in joins:
                e.wait(sem, valv)

        pipe = 0
        sup_mm = []
        tile_done = []
        tile_evs = []
        for sup in range(4):
            par = sup % 2
            if sup >= 2:
                R.g.wait(mm, sup_mm[sup - 2])
            n0 = sup * 512
            for kq in range(4):
                R.g.dma_start(
                    out=wbuf[par][:].rearrange("p (k n) -> p k n", k=NE)
                    [:, 4 * kq:4 * kq + 4, :],
                    in_=w_src[:, 4 * kq:4 * kq + 4, n0:n0 + 512]).inc(ld[par], 16)
            R.t.wait(ld[par])
            for ns in range(4):
                ft = sup * 4 + ns
                for tc in range(NTC):
                    pp = pipe % 2
                    if pipe >= 2:
                        for sem, valv in tile_done[pipe - 2]:
                            R.t.wait(sem, valv)
                        R.t.wait(evs, tile_evs[pipe - 2])
                    for k in range(NE):
                        xnt = xn[:, k * TPC + tc * STRIPE: k * TPC + (tc + 1) * STRIPE]
                        R.t.matmul(p_o[pp][:],
                                   wbuf[par][:, k * 512 + ns * 128:
                                             k * 512 + (ns + 1) * 128],
                                   xnt, start=(k == 0),
                                   stop=(k == NE - 1)).inc(
                                       mm, 1 if k == NE - 1 else 0)
                    R.s.wait(mm)
                    if pipe >= 2:
                        for sem, valv in tile_done[pipe - 2]:
                            R.s.wait(sem, valv)
                    es = e_sb[:, pp * STRIPE:(pp + 1) * STRIPE]
                    if bias_sb is not None:
                        R.s.activation(es, p_o[pp][:], AF.Identity,
                                       bias=bias_sb[:, ft:ft + 1]).inc(evs, 1)
                    else:
                        R.s.activation(es, p_o[pp][:], AF.Copy).inc(evs, 1)
                    tile_evs.append(R.val(evs))
                    tile_done.append(out_cb(ft, tc, pp, es, evs))
                    pipe += 1
            sup_mm.append(R.val(mm))
        out_joins = [(mm, R.val(mm)), (evs, R.val(evs))]
        for cond in tile_done[-2:]:
            out_joins += cond
    return out_joins


def emit_qkv0(R, x_dram, W, cst, tag, after_k=None, after_v=None):
    nc = R.nc
    xn = R.sbuf(f"{tag}_xn", [P, NE * TPC], BF16)
    nrm = R.sem(pool="pe1")
    nev_v = R.sem(pool="dve0")
    sq_sb = R.sbuf(f"{tag}_sqn", [P, 3 * STRIPE], F32)
    nmax = R.sbuf(f"{tag}_nmax", [1, 64], F32)
    R.qk_nmax = nmax
    p_n = R.psum(f"{tag}_pn", [1, TPC])
    out_sb = R.sbuf(f"{tag}_ob", [P, 3 * STRIPE], BF16)
    r_scr = R.sbuf(f"{tag}_rs", [64, 2 * STRIPE], F32)
    bq_sb = R.sbuf(f"{tag}_bq", [P, NE], F32)
    bk_sb = R.sbuf(f"{tag}_bk", [P, NE], F32)
    bvrow = R.sbuf(f"{tag}_bv", [1, D], BF16)
    cl = R.sem(pool="y6")
    bl = R.sem(pool="g4")
    R.y.dma_start(out=bq_sb[:], in_=W["bq"]).inc(cl, 16)
    R.y.dma_start(out=bk_sb[:], in_=W["bk"]).inc(cl, 16)
    R.g.dma_start(out=bvrow[0:1, :], in_=W["bv"]).inc(bl, 16)
    for e in (R.s, R.t, R.v):
        e.wait(cl)
        e.wait(bl)

    with contextlib.ExitStack() as lnstack:
        lnev, ln_done = emit_ln(R, x_dram, xn, cst, f"{tag}_ln", lnstack)
    R.g.wait(lnev, ln_done)
    cs2 = R.sbuf(f"{tag}_cs2", [P, 2 * TPC], F32)
    R.y.wait(lnev, ln_done)
    R.y.dma_start(out=cs2[:], in_=W["cs2"]).inc(cl, 16)
    for e in (R.s, R.t, R.v):
        e.wait(cl)
    QD = 3
    wbuf = [R.sbuf(f"{tag}_w{i}", [P, NE * 512], BF16) for i in range(2)]
    e_sb = R.sbuf(f"{tag}_e", [P, QD * STRIPE], F32)
    p_o = [R.psum(f"{tag}_po{i}", [P, STRIPE]) for i in range(QD)]

    ld = [R.sem(pool=f"g{i}") for i in range(2)]
    mm = R.sem(pool="pe0")
    evs = R.sem(pool="act0")
    sqs = R.sem(pool="act1")
    evv = R.sem(pool="dve0")
    sst = [R.sem(pool=f"y{[2, 3, 10][i]}") for i in range(QD)]

    qT3 = W["qT_out"].rearrange("(h p) t -> p h t", p=P)
    kT3 = W["kT_out"].rearrange("(h p) t -> p h t", p=P)
    vo = W["v_out"]
    srcs = dict(q=W["wq_t"].rearrange("(k p) n -> p k n", p=P),
                k=W["wk_t"].rearrange("(k p) n -> p k n", p=P),
                v=W["wv_t"].rearrange("(k p) n -> p k n", p=P))
    jobs = [("k", s) for s in range(4)] + [("v", s) for s in range(4)] + \
           [("q", s) for s in range(4)]

    pipe = 0
    job_mm = []
    tile_done = []
    tile_sst = []
    nrm_hist = []
    for ji, (proj, sup) in enumerate(jobs):
        par = ji % 2
        if ji >= 2:
            R.g.wait(mm, job_mm[ji - 2])
        n0 = sup * 512
        for kq in range(4):
            R.g.dma_start(
                out=wbuf[par][:].rearrange("p (k n) -> p k n", k=NE)
                [:, 4 * kq:4 * kq + 4, :],
                in_=srcs[proj][:, 4 * kq:4 * kq + 4, n0:n0 + 512]).inc(ld[par], 16)
        R.t.wait(ld[par])
        if ji == 0:
            R.t.wait(lnev, ln_done)
        for t16 in range(NE):
            pp = pipe % QD
            if pipe >= QD:
                for sem, valv in tile_done[pipe - QD]:
                    R.t.wait(sem, valv)
            if proj == "v":
                tt = t16
                for k in range(NE):
                    R.t.matmul(p_o[pp][:],
                               xn[:, k * TPC + tt * P: k * TPC + (tt + 1) * P],
                               wbuf[par][:, k * 512:(k + 1) * 512],
                               start=(k == 0), stop=False)
                R.t.matmul(p_o[pp][:], cst["ones_row_bf"][:],
                           bvrow[0:1, n0:n0 + 512],
                           start=False, stop=True).inc(mm, 1)
                R.s.wait(mm)
                if pipe >= QD:
                    for sem, valv in tile_done[pipe - QD]:
                        R.s.wait(sem, valv)
                    R.s.wait(sst[pp], tile_sst[pipe - QD])
                ob = out_sb[:, pp * STRIPE:(pp + 1) * STRIPE]
                R.s.activation(ob, p_o[pp][:], AF.Copy).inc(evs, 1)
                R.y.wait(evs)
                for hd in range(4):
                    h = sup * 4 + hd
                    R.y.dma_start(
                        out=vo[h * TPC + tt * P: h * TPC + (tt + 1) * P, :],
                        in_=ob[:, hd * P:(hd + 1) * P]).inc(sst[pp], 16)
                tile_done.append([(evs, R.val(evs))])
                tile_sst.append(R.val(sst[pp]))
            else:
                ns, tc = t16 // 4, t16 % 4
                ft = sup * 4 + ns
                for k in range(NE):
                    xnt = xn[:, k * TPC + tc * STRIPE: k * TPC + (tc + 1) * STRIPE]
                    R.t.matmul(p_o[pp][:],
                               wbuf[par][:, k * 512 + ns * 128:
                                         k * 512 + (ns + 1) * 128],
                               xnt, start=(k == 0),
                               stop=(k == NE - 1)).inc(mm, 1 if k == NE - 1 else 0)
                R.s.wait(mm)
                if pipe >= QD:
                    for sem, valv in tile_done[pipe - QD]:
                        R.s.wait(sem, valv)
                es = e_sb[:, pp * STRIPE:(pp + 1) * STRIPE]
                bias = bk_sb if proj == "k" else bq_sb
                R.s.activation(es, p_o[pp][:], AF.Identity,
                               bias=bias[:, ft:ft + 1]).inc(evs, 1)
                o3 = kT3 if proj == "k" else qT3
                qn_off = 32 if proj == "k" else 0
                qe = es[0:64, :]
                qo = es[64:128, :]
                ct0 = cs2[0:64, tc * STRIPE:(tc + 1) * STRIPE]
                sn64 = cs2[64:128, tc * STRIPE:(tc + 1) * STRIPE]
                sn0 = cs2[0:64, TPC + tc * STRIPE: TPC + (tc + 1) * STRIPE]
                ct64 = cs2[64:128, TPC + tc * STRIPE: TPC + (tc + 1) * STRIPE]
                r1 = r_scr[0:64, 0:STRIPE]
                r2 = r_scr[0:64, STRIPE:2 * STRIPE]
                ob = out_sb[:, pp * STRIPE:(pp + 1) * STRIPE]
                R.v.wait(evs)
                if pipe >= QD:
                    R.v.wait(sst[pp], tile_sst[pipe - QD])
                R.v.wait(evv)
                R.v.tensor_tensor(r1, qe, ct0, ALU.mult).inc(evv, 1)
                R.v.tensor_tensor(r2, qo, sn64, ALU.mult).inc(evv, 1)
                R.v.wait(evv)
                R.v.tensor_tensor(ob[0:64, :], r1, r2, ALU.subtract).inc(evv, 1)
                R.v.wait(evv)
                R.v.tensor_tensor(r1, qe, sn0, ALU.mult).inc(evv, 1)
                R.v.tensor_tensor(r2, qo, ct64, ALU.mult).inc(evv, 1)
                R.v.wait(evv)
                R.v.tensor_tensor(ob[64:128, :], r1, r2, ALU.add).inc(evv, 1)
                R.s.wait(evv)
                sqt = sq_sb[:, pp * STRIPE:(pp + 1) * STRIPE]
                if len(nrm_hist) >= 2:
                    R.s.wait(nrm, nrm_hist[-2])
                R.s.activation(sqt, ob, AF.Square).inc(sqs, 1)
                R.t.wait(sqs)
                if ft > 0 and tc == 0:
                    R.t.wait(nev_v)
                R.t.matmul(p_n[0:1, tc * STRIPE:(tc + 1) * STRIPE],
                           cst["ones_col_f32"][:], sqt,
                           start=True, stop=True).inc(nrm, 1)
                nrm_hist.append(R.val(nrm))
                R.y.wait(evv)
                R.y.dma_start(out=o3[:, ft, tc * STRIPE:(tc + 1) * STRIPE],
                              in_=ob).inc(sst[pp], 16)
                tile_sst.append(R.val(sst[pp]))
                if tc == NTC - 1:
                    R.v.wait(nrm)
                    for b in range(B):
                        R.v.tensor_reduce(
                            nmax[0:1, qn_off + b * 16 + ft:
                                 qn_off + b * 16 + ft + 1],
                            p_n[0:1, b * 1024:(b + 1) * 1024],
                            AX.X, ALU.max).inc(nev_v, 1)
                tile_done.append([(evv, R.val(evv)), (sqs, R.val(sqs))])
            pipe += 1
        job_mm.append(R.val(mm))
        if ji == 3 and after_k is not None:
            after_k(R, [(sst[0], R.val(sst[0])), (sst[1], R.val(sst[1])),
                        (nev_v, R.val(nev_v))])
        if ji == 7 and after_v is not None:
            after_v(R, [(sst[0], R.val(sst[0])), (sst[1], R.val(sst[1]))])

    fst = R.sem(pool="y7")
    R.y.wait(nev_v)
    R.y.dma_start(out=W["nrm_out"], in_=nmax[0:1, :]).inc(fst, 16)
    R.y.wait(fst)
    R.y.wait(sst[0])
    R.y.wait(sst[1])


def emit_v_tok(R, xn, W, cst, tag, bvrow, joins):
    with contextlib.ExitStack() as pstack:
        ld = [R.sem(pool=f"g{i}") for i in range(2)]
        mm = R.sem(pool="pe0")
        evs = R.sem(pool="act0")
        sst = [R.sem(pool=f"y{2 + i}") for i in range(2)]
        wbuf = [R.sbuf(f"{tag}_w{i}", [P, NE * 512], BF16, pstack) for i in range(2)]
        vb_sb = R.sbuf(f"{tag}_vb", [P, 2 * STRIPE], BF16, pstack)
        p_o = [R.psum(f"{tag}_po{i}", [P, STRIPE], stack=pstack) for i in range(2)]
        w_src = W["wv_t"].rearrange("(k p) n -> p k n", p=P)
        vo = W["v_out"]

        for e in (R.g, R.t, R.s, R.y):
            for sem, valv in joins:
                e.wait(sem, valv)

        pipe = 0
        sup_mm = []
        tile_done = []
        for sup in range(4):
            par = sup % 2
            if sup >= 2:
                R.g.wait(mm, sup_mm[sup - 2])
            n0 = sup * 512
            for kq in range(4):
                R.g.dma_start(
                    out=wbuf[par][:].rearrange("p (k n) -> p k n", k=NE)
                    [:, 4 * kq:4 * kq + 4, :],
                    in_=w_src[:, 4 * kq:4 * kq + 4, n0:n0 + 512]).inc(ld[par], 16)
            R.t.wait(ld[par])
            for tt in range(NE):
                pp = pipe % 2
                if pipe >= 2:
                    R.t.wait(evs, tile_done[pipe - 2])
                for k in range(NE):
                    R.t.matmul(p_o[pp][:],
                               xn[:, k * TPC + tt * P: k * TPC + (tt + 1) * P],
                               wbuf[par][:, k * 512:(k + 1) * 512],
                               start=(k == 0), stop=False)
                R.t.matmul(p_o[pp][:], cst["ones_row_bf"][:],
                           bvrow[0:1, n0:n0 + 512],
                           start=False, stop=True).inc(mm, 1)
                R.s.wait(mm)
                R.s.wait(sst[pp])
                vb = vb_sb[:, pp * STRIPE:(pp + 1) * STRIPE]
                R.s.activation(vb, p_o[pp][:], AF.Copy).inc(evs, 1)
                R.y.wait(evs)
                for hd in range(4):
                    h = sup * 4 + hd
                    R.y.dma_start(
                        out=vo[h * TPC + tt * P: h * TPC + (tt + 1) * P, :],
                        in_=vb[:, hd * P:(hd + 1) * P]).inc(sst[pp], 16)
                tile_done.append(R.val(evs))
                pipe += 1
            sup_mm.append(R.val(mm))
        out_joins = [(mm, R.val(mm)), (evs, R.val(evs)),
                     (sst[0], R.val(sst[0])), (sst[1], R.val(sst[1]))]
    return out_joins


def emit_attn0(R, W, cst, tag):
    nc = R.nc
    NPAIR = 32

    c0s = R.sem(pool="act0")
    c0v = R.sem(pool="dve0")
    c0m = R.sem(pool="pe0")
    cln = R.sem(pool="y6")
    qn_sb = R.sbuf(f"{tag}_qn", [1, 32], F32)
    kn_sb = R.sbuf(f"{tag}_kn", [1, 256], F32)
    c0_sb = R.sbuf(f"{tag}_c0", [1, 128], F32)
    fut_sb = R.sbuf(f"{tag}_fut", [1, 128], F32)
    mask_sb = R.sbuf(f"{tag}_mask", [P, 896], BF16)
    c0b = R.sbuf(f"{tag}_c0b", [P, 32], F32)
    futb = R.sbuf(f"{tag}_futb", [P, 128], F32)
    bias_sb = R.sbuf(f"{tag}_bias", [P, 2 * NPAIR], F32)
    c0stack = contextlib.ExitStack()
    p_c0 = R.psum(f"{tag}_pc0", [P, 160], stack=c0stack)
    R.y.dma_start(out=qn_sb[:], in_=W["qn"]).inc(cln, 16)
    R.y.dma_start(out=kn_sb[:], in_=W["kn_all"]).inc(cln, 16)
    R.y.dma_start(out=fut_sb[:], in_=W["fut"]).inc(cln, 16)
    R.y.dma_start(out=mask_sb[:], in_=W["masks"]).inc(cln, 16)
    R.v.wait(cln)
    km = c0_sb[0:1, 32:64]
    prod = c0_sb[0:1, 64:96]
    c0r = c0_sb[0:1, 0:32]
    for bh in range(32):
        R.v.tensor_reduce(km[0:1, bh:bh + 1],
                          bass.AP(kn_sb, bh, [[256, 1], [32, 8]]),
                          AX.X, ALU.max).inc(c0v, 1)
    R.v.wait(c0v)
    R.v.tensor_tensor(prod, qn_sb[0:1, :], km, ALU.mult).inc(c0v, 1)
    R.s.wait(c0v)
    R.s.activation(c0_sb[0:1, 96:128], prod, AF.Sqrt).inc(c0s, 1)
    R.v.wait(c0s)
    R.v.tensor_single_scalar(c0r, c0_sb[0:1, 96:128], -SC_ATTN,
                             ALU.mult).inc(c0v, 1)
    R.t.wait(c0v)
    R.t.matmul(p_c0[:, 0:32], cst["ones_row_f32"][:], c0r,
               start=True, stop=True)
    R.t.matmul(p_c0[:, 32:160], cst["ones_row_f32"][:], fut_sb[0:1, :],
               start=True, stop=True).inc(c0m, 1)
    R.s.wait(c0m)
    R.s.activation(c0b[:], p_c0[:, 0:32], AF.Copy).inc(c0s, 1)
    R.s.activation(futb[:], p_c0[:, 32:160], AF.Copy).inc(c0s, 1)
    c0_ready = R.val(c0s)
    c0stack.close()

    kbuf = [R.sbuf(f"{tag}_kb{i}", [P, 64 * P], BF16) for i in range(2)]
    vbuf = [R.sbuf(f"{tag}_vb{i}", [P, 64 * P], BF16) for i in range(2)]
    qbuf = [R.sbuf(f"{tag}_qb{i}", [P, NE * STRIPE], BF16) for i in range(2)]
    kdb = [R.sbuf(f"{tag}_kd{i}", [P, 4 * P], BF16) for i in range(2)]
    vdb = [R.sbuf(f"{tag}_vd{i}", [P, 4 * P], BF16) for i in range(2)]
    ptb = [R.sbuf(f"{tag}_pt{i}", [P, 1024], BF16) for i in range(2)]
    att = [R.sbuf(f"{tag}_at{i}", [P, NE * STRIPE], BF16) for i in range(2)]
    lsb = R.sbuf(f"{tag}_l", [1, 2 * STRIPE], F32)
    lbc = R.sbuf(f"{tag}_lbc", [P, STRIPE], F32)
    p_s = [R.psum(f"{tag}_ps{i}", [P, 1024]) for i in range(2)]
    p_o = [R.psum(f"{tag}_po{i}", [P, STRIPE]) for i in range(2)]
    p_l = R.psum(f"{tag}_pl", [1, STRIPE])
    p_b2 = R.psum(f"{tag}_pb2", [P, STRIPE])

    kld = [R.sem(pool=f"y{i}") for i in range(2)]
    vld = [R.sem(pool=f"g{i}") for i in range(2)]
    qld = [R.sem(pool=f"s{1 + i}") for i in range(2)]
    dld = [R.sem(pool=f"y{2 + i}") for i in range(2)]
    mm = R.sem(pool="pe0")
    exs = R.sem(pool="act0")
    evv = R.sem(pool="dve0")

    kT_all = W["kT_all"]
    v_all3 = W["v_all"].rearrange("(x p) d -> p x d", p=P)
    kT_own3 = W["kT"].rearrange("(h p) t -> p h t", p=P)
    qT3 = W["qT"].rearrange("(h p) t -> p h t", p=P)
    v_own3 = W["v_hb"].rearrange("(x p) d -> p x d", p=P)

    kvdone = []
    att_evv = []
    recip_done = []
    hist_mask = []
    diagdone = []
    expdone_slot = []
    hist_exp = []
    hist_ptb = []
    batch_mm = {}
    oproj_state = dict(wp=0, wo_mm=[], nft=0, evac=[], nsup=0, sup_evac=[])
    wo_ld = [R.sem(pool=f"g{2 + i}") for i in range(2)]
    xld = [R.sem(pool=["s0", "s3"][i]) for i in range(2)]
    sst = [R.sem(pool=f"y{4 + i}") for i in range(2)]
    wo_blk = [R.sbuf(f"{tag}_wo{i}", [P, NE * 256], BF16) for i in range(2)]
    xrb = R.sbuf(f"{tag}_xr", [P, 2 * 2 * STRIPE], F32)
    o_sb = R.sbuf(f"{tag}_osb", [P, 2 * 2 * STRIPE], F32)
    p_op = [p_b2]
    wo_src = W["wo_t"].rearrange("(k p) n -> p k n", p=P)
    bo_sb = R.sbuf(f"{tag}_bo", [P, NE], F32)
    bol = R.sem(pool="y7")
    R.y.dma_start(out=bo_sb[:], in_=W["bo"]).inc(bol, 16)
    R.v.wait(bol)

    x3 = W["x_in"].rearrange("(k p) t -> p k t", p=P)
    xo3 = W["x_out"].rearrange("(k p) t -> p k t", p=P)

    pipe_pair = 0
    hslot = 0
    for b in range(B):
        for h in range(NE):
            hp = hslot % 2
            if hslot >= 2:
                R.y.wait(mm, kvdone[hslot - 2])
                R.g.wait(mm, kvdone[hslot - 2])
            for rk in range(8):
                for j, s in enumerate([rk, 15 - rk]):
                    R.y.dma_start(
                        out=kbuf[hp][:, (4 * s) * P:(4 * s + 4) * P],
                        in_=kT_all[rk * D + h * P: rk * D + (h + 1) * P,
                                   (2 * b + j) * STRIPE:(2 * b + j + 1) * STRIPE]
                    ).inc(kld[hp], 16)
                    R.g.dma_start(
                        out=vbuf[hp][:].rearrange("p (kt d) -> p kt d", d=P)
                        [:, 4 * s:4 * s + 4, :],
                        in_=v_all3[:, ((rk * NE + h) * NE + (2 * b + j) * 4):
                                   ((rk * NE + h) * NE + (2 * b + j + 1) * 4), :]
                    ).inc(vld[hp], 16)
            R.t.wait(kld[hp])
            R.t.wait(vld[hp])
            for cil in range(2):
                ci = 2 * b + cil
                cp = ci % 2
                dslot = hslot * 2 + cil
                dp = dslot % 2
                bh = b * NE + h
                if h == 0:
                    if b > 0:
                        R.s.wait(mm, batch_mm[b - 1])
                    R.s.dma_start(
                        out=qbuf[cp][:].rearrange("p (k t) -> p k t", k=NE),
                        in_=qT3[:, :, ci * STRIPE:(ci + 1) * STRIPE]
                    ).inc(qld[cp], 16)
                if dslot >= 2:
                    R.y.wait(mm, diagdone[dslot - 2])
                R.y.dma_start(out=kdb[dp][:],
                              in_=kT_own3[:, h, ci * STRIPE:(ci + 1) * STRIPE]
                              ).inc(dld[dp], 16)
                R.y.dma_start(out=vdb[dp][:].rearrange("p (j d) -> p j d", d=P),
                              in_=v_own3[:, h * NE + ci * 4: h * NE + ci * 4 + 4, :]
                              ).inc(dld[dp], 16)
                R.v.wait(c0s, c0_ready)
                if dslot >= 2:
                    R.v.wait(exs, expdone_slot[dslot - 2])
                bt = bias_sb[:, dp * NPAIR:(dp + 1) * NPAIR]
                R.v.wait(evv)
                R.v.tensor_scalar(bt, futb[:, ci * NPAIR:(ci + 1) * NPAIR],
                                  -1e30, None, ALU.mult).inc(evv, 1)
                R.v.wait(evv)
                R.v.tensor_scalar(bt, bt, c0b[:, bh:bh + 1], None,
                                  ALU.add).inc(evv, 1)
                bias_ready = R.val(evv)
                R.t.wait(evv, bias_ready)
                R.t.wait(qld[cp])
                R.t.wait(dld[dp])
                qh = qbuf[cp][:, h * STRIPE:(h + 1) * STRIPE]
                npair_ci = 14 if cil == 0 else 30
                NPC = npair_ci + 2
                pob = (hslot * 2 + cil) % 2

                def pair_operands(p):
                    diag = p >= npair_ci
                    if diag:
                        j0 = (p - npair_ci) * 2
                        return (diag, j0,
                                kdb[dp][:, j0 * P:(j0 + 1) * P],
                                kdb[dp][:, (j0 + 1) * P:(j0 + 2) * P],
                                vdb[dp][:, j0 * P:(j0 + 1) * P],
                                vdb[dp][:, (j0 + 1) * P:(j0 + 2) * P])
                    return (diag, 0,
                            kbuf[hp][:, (2 * p) * P:(2 * p + 1) * P],
                            kbuf[hp][:, (2 * p + 1) * P:(2 * p + 2) * P],
                            vbuf[hp][:, (2 * p) * P:(2 * p + 1) * P],
                            vbuf[hp][:, (2 * p + 1) * P:(2 * p + 2) * P])

                for p in range(NPC + 1):
                    if p < NPC:
                        par = pipe_pair % 2
                        diag, j0, lk0, lk1, _, _ = pair_operands(p)
                        if pipe_pair >= 2:
                            R.t.wait(exs, hist_exp[pipe_pair - 2])
                        R.t.matmul(p_s[par][:, 0:512], lk0, qh,
                                   start=True, stop=True)
                        R.t.matmul(p_s[par][:, 512:1024], lk1, qh,
                                   start=True, stop=True).inc(mm, 1)
                        R.s.wait(mm)
                        if pipe_pair >= 2:
                            R.s.wait(mm, hist_ptb[pipe_pair - 2])
                        bias_ap = (c0b[:, bh:bh + 1] if diag
                                   else bt[:, p:p + 1])
                        R.s.activation(ptb[par][:], p_s[par][:], AF.Exp,
                                       bias=bias_ap, scale=SC_ATTN).inc(exs, 1)
                        hist_exp.append(R.val(exs))
                        if diag:
                            R.v.wait(exs)
                            R.v.tensor_tensor(
                                ptb[par][:, 0:512], ptb[par][:, 0:512],
                                mask_sb[:, (3 - j0) * 128:(3 - j0) * 128 + 512],
                                ALU.mult)
                            R.v.tensor_tensor(
                                ptb[par][:, 512:1024], ptb[par][:, 512:1024],
                                mask_sb[:, (2 - j0) * 128:(2 - j0) * 128 + 512],
                                ALU.mult).inc(evv, 1)
                        hist_mask.append(R.val(evv))
                        pipe_pair += 1
                    if p >= 1:
                        pv = p - 1
                        gp = pipe_pair - 2 if p < NPC else pipe_pair - 1
                        par2 = gp % 2
                        diag2, j02, _, _, lv0, lv1 = pair_operands(pv)
                        if diag2:
                            R.t.wait(evv, hist_mask[gp])
                        else:
                            R.t.wait(exs, hist_exp[gp])
                        first = (pv == 0)
                        last = (pv == NPC - 1)
                        if first:
                            ds = hslot * 2 + cil
                            if ds >= 2:
                                R.t.wait(evv, att_evv[ds - 2])
                            if ds >= 1:
                                R.t.wait(evv, recip_done[ds - 1])
                        R.t.matmul(p_o[pob][:], lv0, ptb[par2][:, 0:512],
                                   start=first, stop=False)
                        R.t.matmul(p_o[pob][:], lv1, ptb[par2][:, 512:1024],
                                   start=False, stop=last)
                        R.t.matmul(p_l[:], cst["ones_col_bf"][:],
                                   ptb[par2][:, 0:512],
                                   start=first, stop=False)
                        R.t.matmul(p_l[:], cst["ones_col_bf"][:],
                                   ptb[par2][:, 512:1024],
                                   start=False, stop=last).inc(mm, 1)
                        hist_ptb.append(R.val(mm))
                expdone_slot.append(R.val(exs))
                diagdone.append(R.val(mm))
                R.v.wait(mm)
                lrow = lsb[0:1, dp * STRIPE:(dp + 1) * STRIPE]
                R.v.reciprocal(lrow, p_l[:]).inc(evv, 1)
                recip_done.append(R.val(evv))
                R.t.wait(evv)
                R.t.wait(exs)
                R.t.matmul(p_b2[:], cst["ones_row_f32"][:], lrow,
                           start=True, stop=True).inc(mm, 1)
                R.s.wait(mm)
                R.s.wait(evv)
                R.s.activation(lbc[:], p_b2[:], AF.Copy).inc(exs, 1)
                R.v.wait(exs)
                if b > 0:
                    R.v.wait(mm, batch_mm[b - 1])
                R.v.tensor_tensor(att[cil][:, h * STRIPE:(h + 1) * STRIPE],
                                  p_o[pob][:], lbc[:], ALU.mult).inc(evv, 1)
                att_evv.append(R.val(evv))
            kvdone.append(R.val(mm))
            hslot += 1
        for cil in range(2):
            ci = 2 * b + cil
            for fsup in range(8):
                xp = oproj_state["nsup"] % 2
                if oproj_state["nsup"] >= 2:
                    R.s.wait(evv, oproj_state["sup_evac"][oproj_state["nsup"] - 2])
                R.s.dma_start(
                    out=xrb[:, xp * 2 * STRIPE:(xp + 1) * 2 * STRIPE]
                    .rearrange("p (k t) -> p k t", k=2),
                    in_=x3[:, fsup * 2:(fsup + 1) * 2,
                           ci * STRIPE:(ci + 1) * STRIPE]).inc(xld[xp], 16)
                wp = oproj_state["wp"] % 2
                if oproj_state["wp"] >= 2:
                    R.g.wait(mm, oproj_state["wo_mm"][oproj_state["wp"] - 2])
                n0 = fsup * 256
                for kq in range(4):
                    R.g.dma_start(
                        out=wo_blk[wp][:].rearrange("p (k n) -> p k n", k=NE)
                        [:, 4 * kq:4 * kq + 4, :],
                        in_=wo_src[:, 4 * kq:4 * kq + 4, n0:n0 + 256]
                    ).inc(wo_ld[wp], 16)
                R.t.wait(wo_ld[wp])
                R.t.wait(evv)
                for ft4 in range(2):
                    ft = fsup * 2 + ft4
                    nft = oproj_state["nft"]
                    pq = 0
                    if nft >= 1:
                        R.t.wait(evv, oproj_state["evac"][nft - 1])
                    for k in range(NE):
                        R.t.matmul(p_op[pq][:, 0:STRIPE],
                                   wo_blk[wp][:, k * 256 + ft4 * 128:
                                              k * 256 + (ft4 + 1) * 128],
                                   att[cil][:, k * STRIPE:(k + 1) * STRIPE],
                                   start=(k == 0),
                                   stop=(k == NE - 1)).inc(
                                       mm, 1 if k == NE - 1 else 0)
                    R.v.wait(mm)
                    R.v.wait(xld[xp])
                    opair = oproj_state["nft"] // 2
                    op = opair % 2
                    ot = o_sb[:, (op * 2 + ft4 % 2) * STRIPE:
                              (op * 2 + ft4 % 2 + 1) * STRIPE]
                    if opair >= 2:
                        R.v.wait(sst[op])
                    R.v.scalar_tensor_tensor(
                        ot, p_op[pq][:, 0:STRIPE], bo_sb[:, ft:ft + 1],
                        xrb[:, (xp * 2 + ft4) * STRIPE:(xp * 2 + ft4 + 1) * STRIPE],
                        ALU.add, ALU.add).inc(evv, 1)
                    oproj_state["evac"].append(R.val(evv))
                    oproj_state["nft"] += 1
                    if ft4 % 2 == 1:
                        R.y.wait(evv)
                        R.y.dma_start(
                            out=xo3[:, fsup * 2: fsup * 2 + 2,
                                    ci * STRIPE:(ci + 1) * STRIPE],
                            in_=o_sb[:, op * 2 * STRIPE:(op + 1) * 2 * STRIPE]
                            .rearrange("p (k t) -> p k t", k=2)).inc(sst[op], 16)
                oproj_state["wo_mm"].append(R.val(mm))
                oproj_state["wp"] += 1
                oproj_state["sup_evac"].append(R.val(evv))
                oproj_state["nsup"] += 1
        batch_mm[b] = R.val(mm)
    R.y.wait(sst[0])
    R.y.wait(sst[1])
    return


def build_qkv0():
    nc = bass.Bass("TRN2", target_bir_lowering=False)
    with contextlib.ExitStack() as stack:
        R = Rec(nc, stack)
        W = {}
        x_in = nc.dram_tensor("x_in", [D, TPC], F32, kind="ExternalInput").ap()
        for nm, shp, dt in [("wq_t", [D, D], F32), ("wk_t", [D, D], F32),
                            ("wv_t", [D, D], F32), ("bq", [P, NE], F32),
                            ("bk", [P, NE], F32), ("bv", [1, D], F32),
                            ("cs2", [P, 2 * TPC], F32)]:
            W[nm] = nc.dram_tensor(nm, shp, dt, kind="ExternalInput").ap()
        W["qT_out"] = nc.dram_tensor("qT_out", [D, TPC], BF16, kind="ExternalOutput").ap()
        W["kT_out"] = nc.dram_tensor("kT_out", [D, TPC], BF16, kind="ExternalOutput").ap()
        W["v_out"] = nc.dram_tensor("v_out", [NE * TPC, P], BF16, kind="ExternalOutput").ap()
        W["nrm_out"] = nc.dram_tensor("nrm_out", [1, 64], F32, kind="ExternalOutput").ap()
        cst = _mk_consts(R)
        emit_qkv0(R, x_in, W, cst, "qk")
        R.emit()
    return nc


def build_attn0():
    nc = bass.Bass("TRN2", target_bir_lowering=False)
    with contextlib.ExitStack() as stack:
        R = Rec(nc, stack)
        W = {}
        W["x_in"] = nc.dram_tensor("x_in", [D, TPC], F32, kind="ExternalInput").ap()
        for nm, shp, dt in [("qT", [D, TPC], BF16), ("kT", [D, TPC], BF16),
                            ("v_hb", [NE * TPC, P], BF16),
                            ("kT_all", [8 * D, TPC], BF16),
                            ("v_all", [8 * NE * TPC, P], BF16),
                            ("qn", [1, 32], F32), ("kn_all", [1, 256], F32),
                            ("fut", [1, 128], F32), ("masks", [P, 896], BF16),
                            ("wo_t", [D, D], F32), ("bo", [P, NE], F32)]:
            W[nm] = nc.dram_tensor(nm, shp, dt, kind="ExternalInput").ap()
        W["x_out"] = nc.dram_tensor("x_out", [D, TPC], F32, kind="ExternalOutput").ap()
        cst = _mk_consts(R)
        emit_attn0(R, W, cst, "a0")
        R.emit()
    return nc



ROPE_PERM = np.concatenate(
    [np.concatenate([h * 128 + 2 * np.arange(64),
                     h * 128 + 2 * np.arange(64) + 1]) for h in range(16)])


def host_rope_tables(r):
    inv = (10000.0 ** (-np.arange(0, 128, 2) / 128.0)).astype(np.float64)
    cols = []
    for (b, s) in chunks_of(r):
        pos = np.arange(s * STRIPE, (s + 1) * STRIPE, dtype=np.float64)
        cols.append(pos)
    pos = np.concatenate(cols)
    ang = pos[None, :] * inv[:, None]
    c = np.cos(ang).astype(np.float32)
    s_ = np.sin(ang).astype(np.float32)
    return np.ascontiguousarray(np.block([[c, s_], [s_, c]]))


def host_masks():
    k = np.arange(128)[:, None]
    c = np.arange(896)[None, :]
    return (c >= k + 384).astype(np.float32).astype(BF)


def host_fut(r):
    out = np.zeros((1, 128), np.float32)
    for ci, (b, s) in enumerate(chunks_of(r)):
        for pr in range(32):
            if 2 * pr >= 4 * s:
                out[0, ci * 32 + pr] = 1.0
    return out


def bias_tile(bvec, perm=None):
    v = np.asarray(bvec, np.float32)
    if perm is not None:
        v = v[perm]
    return np.ascontiguousarray(v.reshape(NE, P).T)


def emit_nvfp4_tile(R, sp, sc_view, out_bf, scr, evv):
    W2 = sp.shape[1] * 2
    c = scr["c"]
    sg = scr["sg"]
    g2 = scr["g2"]
    g1 = scr["g1"]
    def op(fn):
        R.v.wait(evv)
        fn().inc(evv, 1)
    op(lambda: R.v.tensor_single_scalar(c[:, 0:W2:2], sp, 15, ALU.bitwise_and))
    op(lambda: R.v.tensor_scalar(c[:, 1:W2:2], sp, 4, 15,
                                 ALU.logical_shift_right, ALU.bitwise_and))
    op(lambda: R.v.tensor_scalar(sg[:], c[:], 8, 28,
                                 ALU.bitwise_and, ALU.logical_shift_left))
    op(lambda: R.v.tensor_single_scalar(c[:], c[:], 7, ALU.bitwise_and))
    op(lambda: R.v.tensor_single_scalar(g2[:], c[:], 2, ALU.is_ge))
    op(lambda: R.v.tensor_single_scalar(g1[:], c[:], 1, ALU.is_ge))
    op(lambda: R.v.scalar_tensor_tensor(g2[:], c[:], 251, g2[:], ALU.add, ALU.add))
    op(lambda: R.v.tensor_tensor(g2[:], g2[:], g1[:], ALU.mult))
    op(lambda: R.v.tensor_single_scalar(g2[:], g2[:], 22, ALU.logical_shift_left))
    op(lambda: R.v.tensor_tensor(g2[:], g2[:], sg[:], ALU.bitwise_or))
    op(lambda: R.v.tensor_tensor(
        out_bf.rearrange("p (b i) -> p b i", i=32),
        g2[:].bitcast(F32).rearrange("p (b i) -> p b i", i=32),
        sc_view, ALU.mult))


def make_dequant_jobs(R, W, cst, tag):
    nc = R.nc
    ld = [R.sem(pool=f"g{4 + i}") for i in range(2)]
    mm = R.sem(pool="pe0")
    evs = R.sem(pool="act0")
    evv = R.sem(pool="dve0")
    sst = [R.sem(pool=["y7", "y9"][i]) for i in range(2)]

    sp_b = [R.sbuf(f"{tag}_sp{i}", [P, 512], I32, stack=R.stack) for i in range(2)]
    sc_b = [R.sbuf(f"{tag}_sc{i}", [P, 32], F32, stack=R.stack) for i in range(2)]
    scr = {k: R.sbuf(f"{tag}_{k}", [P, 1024], I32, stack=R.stack)
           for k in ("c", "sg", "g2", "g1")}
    vals = R.sbuf(f"{tag}_vals", [P, 1024], BF16, stack=R.stack)
    tv_b = [R.sbuf(f"{tag}_tv{i}", [P, 1024], BF16, stack=R.stack) for i in range(2)]
    ident = R.sbuf(f"{tag}_id", [P, P], BF16, stack=R.stack)
    idl = R.sem(pool="y6")
    R.y.dma_start(out=ident[:], in_=W["ident"]).inc(idl, 16)
    R.t.wait(idl)

    jobs = []
    for mat, wp, sc, od in (("g", W["gWp"], W["gSc"], W["wg_t_part"]),
                            ("u", W["uWp"], W["uSc"], W["wu_t_part"]),
                            ("d", W["dWp"], W["dSc"], W["wd_t_part"])):
        for rt in range(16):
            for half in range(2):
                jobs.append((wp[rt * P:(rt + 1) * P, half * 512:(half + 1) * 512],
                             sc[rt * P:(rt + 1) * P, half * 32:(half + 1) * 32],
                             od, rt, half))

    state = dict(ji=0, tile_evv=[], tile_mm=[], tile_evs=[])

    def run_jobs(kk):
        for _ in range(kk):
            ji = state["ji"]
            if ji >= len(jobs):
                return
            src_, scsrc, outspec, rblk, half = jobs[ji]
            par = ji % 2
            if ji >= 2:
                R.g.wait(evv, state["tile_evv"][ji - 2])
            R.g.dma_start(out=sp_b[par][:], in_=src_).inc(ld[par], 16)
            R.g.dma_start(out=sc_b[par][:], in_=scsrc).inc(ld[par], 16)
            R.v.wait(ld[par])
            if ji >= 1:
                R.v.wait(mm, state["tile_mm"][ji - 1])
            sc_view = bass.AP(sc_b[par], 0, [[32, P], [1, 32], [0, 32]])
            emit_nvfp4_tile(R, sp_b[par][:], sc_view, vals[:], scr, evv)
            state["tile_evv"].append(R.val(evv))
            p_t = R.dq_pt
            R.t.wait(evv)
            if ji >= 2:
                R.t.wait(evs, state["tile_evs"][ji - 2])
            for k in range(8):
                h = R.t.transpose(p_t[par][:, k * P:(k + 1) * P],
                                  vals[:, k * P:(k + 1) * P], ident[:])
            h.inc(mm, 1)
            state["tile_mm"].append(R.val(mm))
            R.s.wait(mm)
            if ji >= 2:
                R.s.wait(sst[par])
            R.s.activation(tv_b[par][:], p_t[par][:], AF.Copy).inc(evs, 1)
            state["tile_evs"].append(R.val(evs))
            R.y.wait(evs)
            o3 = outspec.rearrange("(k p) n -> p k n", p=P)
            for kq in range(2):
                R.y.dma_start(
                    out=o3[:, half * 8 + 4 * kq: half * 8 + 4 * kq + 4,
                           rblk * P:(rblk + 1) * P],
                    in_=tv_b[par][:].rearrange("p (k c) -> p k c", k=8)
                    [:, 4 * kq:4 * kq + 4, :]).inc(sst[par], 16)
            state["ji"] = ji + 1

    def finish():
        run_jobs(len(jobs) - state["ji"])
        return [(sst[0], R.val(sst[0])), (sst[1], R.val(sst[1])),
                (evs, R.val(evs))]

    return run_jobs, len(jobs), finish


def emit_dequant(R, W, cst, tag):
    run_jobs, n, finish = make_dequant_jobs(R, W, cst, tag)
    finish()


def build_dequant():
    nc = bass.Bass("TRN2", target_bir_lowering=False)
    with contextlib.ExitStack() as stack:
        R = Rec(nc, stack)
        W = {}
        W["gWp"] = nc.dram_tensor("gWp", [D, 1024], I32, kind="ExternalInput").ap()
        W["gSc"] = nc.dram_tensor("gSc", [D, 64], F32, kind="ExternalInput").ap()
        W["uWp"] = nc.dram_tensor("uWp", [D, 1024], I32, kind="ExternalInput").ap()
        W["uSc"] = nc.dram_tensor("uSc", [D, 64], F32, kind="ExternalInput").ap()
        W["dWp"] = nc.dram_tensor("dWp", [D, 1024], I32, kind="ExternalInput").ap()
        W["dSc"] = nc.dram_tensor("dSc", [D, 64], F32, kind="ExternalInput").ap()
        W["ident"] = nc.dram_tensor("ident", [P, P], BF16, kind="ExternalInput").ap()
        W["wg_t_part"] = nc.dram_tensor("wg_t_part", [D, D], BF16, kind="ExternalOutput").ap()
        W["wu_t_part"] = nc.dram_tensor("wu_t_part", [D, D], BF16, kind="ExternalOutput").ap()
        W["wd_t_part"] = nc.dram_tensor("wd_t_part", [D, D], BF16, kind="ExternalOutput").ap()
        cst = _mk_consts(R)
        emit_dequant(R, W, cst, "dq")
        R.emit()
    return nc


def build_ffn1():
    nc = bass.Bass("TRN2", target_bir_lowering=False)
    with contextlib.ExitStack() as stack:
        R = Rec(nc, stack)
        x_in = nc.dram_tensor("x_in", [D, TPC], F32, kind="ExternalInput").ap()
        wg_t = nc.dram_tensor("wg_t", [D, 16384], BF16, kind="ExternalInput").ap()
        wu_t = nc.dram_tensor("wu_t", [D, 16384], BF16, kind="ExternalInput").ap()
        wd_t = nc.dram_tensor("wd_t", [16384, D], BF16, kind="ExternalInput").ap()
        x_out = nc.dram_tensor("x_out", [D, TPC], F32, kind="ExternalOutput").ap()
        cst = _mk_consts(R)
        W = dict(wg_t=wg_t, wu_t=wu_t, wd_t=wd_t)
        emit_ffn_pass(R, x_in, x_out, W, cst, "f1", inter=16384, kind="l1")
        R.emit()
    return nc


def emit_attn1a(R, x_dram, W, cst, tag):
    nc = R.nc
    xn = R.sbuf(f"{tag}_xn", [P, NE * TPC], BF16)
    bq_sb = R.sbuf(f"{tag}_bq", [P, NE], F32)
    bkrow = R.sbuf(f"{tag}_bk", [1, D], BF16)
    bvrow = R.sbuf(f"{tag}_bv", [1, D], BF16)
    elu_sb = R.sbuf(f"{tag}_elu", [P, 4 * STRIPE], F32)
    qp_sb = R.sbuf(f"{tag}_qp", [P, 2 * STRIPE], BF16)
    bl = R.sem(pool="y6")
    blg = R.sem(pool="g4")
    R.y.dma_start(out=bq_sb[:], in_=W["qb"]).inc(bl, 16)
    R.g.dma_start(out=bkrow[0:1, :], in_=W["kb"]).inc(blg, 16)
    R.g.dma_start(out=bvrow[0:1, :], in_=W["vb"]).inc(blg, 16)
    for e in (R.s, R.t, R.v):
        e.wait(bl)
        e.wait(blg)

    with contextlib.ExitStack() as lnstack:
        lnev, ln_done = emit_ln(R, x_dram, xn, cst, f"{tag}_ln", lnstack)
    joins = [(lnev, ln_done)]

    evv_q = R.sem(pool="dve0")
    axs_q = R.sem(pool="act1")
    sst_q = [R.sem(pool=f"y{2 + i}") for i in range(2)]
    qpT3 = W["qpT_out"].rearrange("(h p) t -> p h t", p=P)

    def qp_cb(ft, tc, pp, es, evs):
        m = elu_sb[:, (2 * pp) * STRIPE:(2 * pp + 1) * STRIPE]
        r_ = elu_sb[:, (2 * pp + 1) * STRIPE:(2 * pp + 2) * STRIPE]
        ob = qp_sb[:, pp * STRIPE:(pp + 1) * STRIPE]
        R.v.wait(evs)
        R.v.wait(evv_q)
        R.v.tensor_scalar_min(m, es, 0.0).inc(evv_q, 1)
        R.s.wait(evv_q)
        R.s.activation(m, m, AF.Exp).inc(axs_q, 1)
        R.v.tensor_scalar_max(r_, es, 0.0).inc(evv_q, 1)
        R.v.wait(axs_q)
        R.v.wait(evv_q)
        R.v.wait(sst_q[pp])
        R.v.tensor_tensor(ob, m, r_, ALU.add).inc(evv_q, 1)
        R.y.wait(evv_q)
        R.y.dma_start(out=qpT3[:, ft, tc * STRIPE:(tc + 1) * STRIPE],
                      in_=ob).inc(sst_q[pp], 16)
        return [(evv_q, R.val(evv_q)), (axs_q, R.val(axs_q))]

    joins = emit_proj_ch(R, xn, W["qw_t"], bq_sb, qp_cb, cst, f"{tag}_q", joins)
    joins += [(sst_q[0], R.val(sst_q[0])), (sst_q[1], R.val(sst_q[1]))]

    with contextlib.ExitStack() as ps:
        ld = [R.sem(pool=f"g{i}") for i in range(2)]
        mm = R.sem(pool="pe0")
        axs = R.sem(pool="act0")
        evv = R.sem(pool="dve0")
        kst = R.sem(pool="y4")
        wkb = [R.sbuf(f"{tag}_wk{i}", [P, NE * 512], BF16, ps) for i in range(2)]
        wvb = [R.sbuf(f"{tag}_wv{i}", [P, NE * 512], BF16, ps) for i in range(2)]
        kpt = [R.sbuf(f"{tag}_kp{i}", [P, 512], BF16, ps) for i in range(2)]
        vt = [R.sbuf(f"{tag}_vt{i}", [P, 512], BF16, ps) for i in range(2)]
        kvo = R.sbuf(f"{tag}_kvo", [P, 2 * 512], F32, ps)
        kso = R.sbuf(f"{tag}_kso", [1, 2 * 512], F32, ps)
        p_k = [R.psum(f"{tag}_pk{i}", [P, 512], stack=ps) for i in range(2)]
        p_v = [R.psum(f"{tag}_pv{i}", [P, 512], stack=ps) for i in range(2)]
        p_kv = [R.psum(f"{tag}_pkv{i}", [P, 512], stack=ps) for i in range(2)]
        p_ks = [R.psum(f"{tag}_pks{i}", [1, 512], stack=ps) for i in range(2)]
        wk_src = W["kw_t"].rearrange("(k p) n -> p k n", p=P)
        wv_src = W["vw_t"].rearrange("(k p) n -> p k n", p=P)

        for e in (R.g, R.t, R.s, R.v, R.y):
            for sem, valv in joins:
                e.wait(sem, valv)

        pipe = 0
        tdone = []
        fc_mm = []
        kvd = []
        for fc in range(4):
            par = fc % 2
            if fc >= 2:
                R.g.wait(mm, fc_mm[fc - 2])
            n0 = fc * 512
            for kq in range(4):
                R.g.dma_start(
                    out=wkb[par][:].rearrange("p (k n) -> p k n", k=NE)
                    [:, 4 * kq:4 * kq + 4, :],
                    in_=wk_src[:, 4 * kq:4 * kq + 4, n0:n0 + 512]).inc(ld[par], 16)
                R.g.dma_start(
                    out=wvb[par][:].rearrange("p (k n) -> p k n", k=NE)
                    [:, 4 * kq:4 * kq + 4, :],
                    in_=wv_src[:, 4 * kq:4 * kq + 4, n0:n0 + 512]).inc(ld[par], 16)
            R.t.wait(ld[par])
            for tt in range(NE):
                b = tt // 8
                pp = pipe % 2
                if tt == 0 and fc >= 1:
                    R.t.wait(evv, kvd[fc - 1])
                if pipe >= 2:
                    R.t.wait(evv, tdone[pipe - 2][0])
                    R.t.wait(axs, tdone[pipe - 2][1])
                for k in range(NE):
                    R.t.matmul(p_k[pp][:],
                               xn[:, k * TPC + tt * P: k * TPC + (tt + 1) * P],
                               wkb[par][:, k * 512:(k + 1) * 512],
                               start=(k == 0), stop=False)
                R.t.matmul(p_k[pp][:], cst["ones_row_bf"][:],
                           bkrow[0:1, n0:n0 + 512], start=False, stop=True)
                for k in range(NE):
                    R.t.matmul(p_v[pp][:],
                               xn[:, k * TPC + tt * P: k * TPC + (tt + 1) * P],
                               wvb[par][:, k * 512:(k + 1) * 512],
                               start=(k == 0), stop=False)
                R.t.matmul(p_v[pp][:], cst["ones_row_bf"][:],
                           bvrow[0:1, n0:n0 + 512], start=False,
                           stop=True).inc(mm, 1)
                R.s.wait(mm)
                m = elu_sb[:, (2 * pp) * STRIPE:(2 * pp + 1) * STRIPE]
                r_ = elu_sb[:, (2 * pp + 1) * STRIPE:(2 * pp + 2) * STRIPE]
                R.v.wait(mm)
                if pipe >= 2:
                    R.v.wait(mm, tdone[pipe - 2][2])
                    R.v.wait(evv, tdone[pipe - 2][0])
                R.v.tensor_scalar_min(m, p_k[pp][:], 0.0).inc(evv, 1)
                R.s.wait(evv)
                R.s.activation(m, m, AF.Exp).inc(axs, 1)
                R.v.tensor_scalar_max(r_, p_k[pp][:], 0.0).inc(evv, 1)
                R.v.wait(axs)
                R.v.wait(evv)
                R.v.tensor_tensor(kpt[pp][:], m, r_, ALU.add).inc(evv, 1)
                R.s.wait(evv)
                R.s.activation(vt[pp][:], p_v[pp][:], AF.Copy).inc(axs, 1)
                R.t.wait(evv)
                R.t.wait(axs)
                first = (tt % 8) == 0
                last = (tt % 8) == 7
                for hd in range(4):
                    R.t.matmul(p_kv[b][:, hd * P:(hd + 1) * P],
                               kpt[pp][:, hd * P:(hd + 1) * P],
                               vt[pp][:, hd * P:(hd + 1) * P],
                               start=(first and hd == 0),
                               stop=(last and hd == 3))
                R.t.matmul(p_ks[b][:], cst["ones_col_bf"][:], kpt[pp][:],
                           start=first, stop=last).inc(mm, 1)
                tdone.append((R.val(evv), R.val(axs), R.val(mm)))
                pipe += 1
            fc_mm.append(R.val(mm))
            R.v.wait(mm)
            if fc >= 1:
                R.v.wait(kst)
            for b in range(2):
                R.v.tensor_copy(kvo[:, b * 512:(b + 1) * 512],
                                p_kv[b][:]).inc(evv, 1)
                R.v.tensor_copy(kso[0:1, b * 512:(b + 1) * 512],
                                p_ks[b][:]).inc(evv, 1)
            kvd.append(R.val(evv))
            R.y.wait(evv)
            R.y.dma_start(out=W["kv_out"].rearrange("(b p) n -> p b n", p=P)
                          [:, :, n0:n0 + 512],
                          in_=kvo[:].rearrange("p (b n) -> p b n", b=2)
                          ).inc(kst, 16)
            for b in range(2):
                R.y.dma_start(out=W["ks_out"][b:b + 1, n0:n0 + 512],
                              in_=kso[0:1, b * 512:(b + 1) * 512]
                              ).inc(kst, 16)
        R.y.wait(kst)
        R.y.wait(sst_q[0])
        R.y.wait(sst_q[1])


def emit_attn1b(R, W, cst, tag):
    nc = R.nc
    EPS1 = 1e-6
    acc = R.sem(pool="g5")
    evv = R.sem(pool="dve0")
    axs = R.sem(pool="act0")
    mm = R.sem(pool="pe0")
    kv_f = R.sbuf(f"{tag}_kvf", [P, 2 * D], F32)
    kv_bf = R.sbuf(f"{tag}_kvb", [P, 2 * D], BF16)
    ks_f = R.sbuf(f"{tag}_ksf", [2, D], F32)
    ks_bf = R.sbuf(f"{tag}_ksb", [2, D], BF16)
    ksc = R.sbuf(f"{tag}_ksc", [P, 32], BF16)
    ident = R.sbuf(f"{tag}_id", [P, P], BF16)
    p_tr = R.psum(f"{tag}_ptr", [P, 32], BF16)
    idl = R.sem(pool="y6")
    R.y.dma_start(out=ident[:], in_=W["ident"]).inc(idl, 16)
    if "kv_sum" in W:
        R.g.dma_start(out=kv_f[:].rearrange("p (b n) -> p b n", b=2),
                      in_=W["kv_sum"].rearrange("(b p) n -> p b n", p=P)
                      ).inc(acc, 16)
        R.g.dma_start(out=ks_f[:], in_=W["ks_sum"]).inc(acc, 16)
    else:
        R.g.dma_start(out=kv_f[:].rearrange("p (b n) -> p b n", b=2),
                      in_=W["kv_all"].rearrange("(r b p) n -> p r b n", p=P, b=2)
                      [:, 0, :, :]).inc(acc, 16)
        for r in range(1, 8):
            R.g.wait(acc)
            R.g.dma_start(out=kv_f[:].rearrange("p (b n) -> p b n", b=2),
                          in_=W["kv_all"].rearrange("(r b p) n -> p r b n", p=P, b=2)
                          [:, r, :, :], accum_op=ALU.add).inc(acc, 16)
        R.g.wait(acc)
        R.g.dma_start(out=ks_f[:], in_=W["ks_all"][0:2, :]).inc(acc, 16)
        for r in range(1, 8):
            R.g.wait(acc)
            R.g.dma_start(out=ks_f[:], in_=W["ks_all"][2 * r:2 * r + 2, :],
                          accum_op=ALU.add).inc(acc, 16)
    R.v.wait(acc)
    R.v.tensor_copy(kv_bf[:], kv_f[:]).inc(evv, 1)
    R.v.tensor_copy(ks_bf[:], ks_f[:]).inc(evv, 1)
    R.t.wait(evv)
    R.t.wait(idl)
    h = None
    for k in range(NE):
        h = R.t.transpose(p_tr[:, 2 * k:2 * k + 2],
                          ks_bf[0:2, k * P:(k + 1) * P], ident[0:2, 0:2])
    h.inc(mm, 1)
    R.s.wait(mm)
    R.s.activation(ksc[:].rearrange("p (b k) -> p b k", b=2),
                   p_tr[:].rearrange("p (k b) -> p k b", b=2)
                   .transpose([0, 2, 1]), AF.Copy).inc(axs, 1)

    att = [R.sbuf(f"{tag}_at{i}", [P, NE * STRIPE], BF16) for i in range(2)]
    qpb = [R.sbuf(f"{tag}_qb{i}", [P, NE * STRIPE], BF16) for i in range(2)]
    zrow = R.sbuf(f"{tag}_z", [1, 2 * STRIPE], F32)
    zbc = R.sbuf(f"{tag}_zb", [P, STRIPE], F32)
    p_a = [R.psum(f"{tag}_pa{i}", [P, STRIPE]) for i in range(2)]
    p_z = R.psum(f"{tag}_pz", [1, STRIPE])
    p_zb = R.psum(f"{tag}_pzb", [P, STRIPE])
    qld = [R.sem(pool=f"s{1 + i}") for i in range(2)]
    qpT3 = W["qpT"].rearrange("(h p) t -> p h t", p=P)

    wo_ld = [R.sem(pool=f"g{2 + i}") for i in range(2)]
    xld = [R.sem(pool=["s0", "s3"][i]) for i in range(2)]
    sst = [R.sem(pool=f"y{4 + i}") for i in range(2)]
    wo_blk = [R.sbuf(f"{tag}_wo{i}", [P, NE * 512], BF16) for i in range(2)]
    xrb = R.sbuf(f"{tag}_xr", [P, 2 * 4 * STRIPE], F32)
    o_sb = R.sbuf(f"{tag}_osb", [P, 2 * 2 * STRIPE], F32)
    p_op = [R.psum(f"{tag}_pop0", [P, STRIPE])]
    wo_src = W["ow_t"].rearrange("(k p) n -> p k n", p=P)
    bo_sb = R.sbuf(f"{tag}_bo", [P, NE], F32)
    bol = R.sem(pool="y7")
    R.y.dma_start(out=bo_sb[:], in_=W["ob"]).inc(bol, 16)
    R.v.wait(bol)
    x3 = W["x_in"].rearrange("(k p) t -> p k t", p=P)
    xo3 = W["x_out"].rearrange("(k p) t -> p k t", p=P)
    ost = dict(wp=0, wo_mm=[], nft=0, evac=[], nsup=0, sup_evac=[])

    ci_mm = {}
    att_done = []
    for ci in range(NTC):
        b = ci // 2
        cp = ci % 2
        if ci >= 2:
            R.s.wait(mm, ci_mm[ci - 2])
        R.s.dma_start(out=qpb[cp][:].rearrange("p (k t) -> p k t", k=NE),
                      in_=qpT3[:, :, ci * STRIPE:(ci + 1) * STRIPE]
                      ).inc(qld[cp], 16)
        R.t.wait(qld[cp])
        R.t.wait(axs)
        for h2 in range(NE):
            bh = b * NE + h2
            pa = h2 % 2
            qh = qpb[cp][:, h2 * STRIPE:(h2 + 1) * STRIPE]
            R.t.wait(evv)
            if ci >= 2 and h2 == 0:
                R.t.wait(mm, ci_mm[ci - 2])
            R.t.matmul(p_a[pa][:],
                       kv_bf[:, b * D + h2 * P: b * D + (h2 + 1) * P],
                       qh, start=True, stop=True)
            R.t.matmul(p_z[:], ksc[:, bh:bh + 1], qh,
                       start=True, stop=True).inc(mm, 1)
            R.v.wait(mm)
            zr = zrow[0:1, (h2 % 2) * STRIPE:(h2 % 2 + 1) * STRIPE]
            R.v.tensor_single_scalar(zr, p_z[:], EPS1, ALU.add).inc(evv, 1)
            R.v.wait(evv)
            R.v.reciprocal(zr, zr).inc(evv, 1)
            R.t.wait(evv)
            R.t.matmul(p_zb[:], cst["ones_row_f32"][:], zr,
                       start=True, stop=True).inc(mm, 1)
            R.s.wait(mm)
            R.s.wait(evv)
            R.s.activation(zbc[:], p_zb[:], AF.Copy).inc(axs, 1)
            R.v.wait(axs)
            if ci >= 2 and h2 == 0:
                R.v.wait(mm, ci_mm[ci - 2])
            R.v.tensor_tensor(att[ci % 2][:, h2 * STRIPE:(h2 + 1) * STRIPE],
                              p_a[pa][:], zbc[:], ALU.mult).inc(evv, 1)
            att_done.append(R.val(evv))
        ci_mm[ci] = R.val(mm)
        for fsup in range(4):
            xp = ost["nsup"] % 2
            if ost["nsup"] >= 2:
                R.s.wait(evv, ost["sup_evac"][ost["nsup"] - 2])
            R.s.dma_start(
                out=xrb[:, xp * 4 * STRIPE:(xp + 1) * 4 * STRIPE]
                .rearrange("p (k t) -> p k t", k=4),
                in_=x3[:, fsup * 4:(fsup + 1) * 4,
                       ci * STRIPE:(ci + 1) * STRIPE]).inc(xld[xp], 16)
            wp = ost["wp"] % 2
            if ost["wp"] >= 2:
                R.g.wait(mm, ost["wo_mm"][ost["wp"] - 2])
            n0 = fsup * 512
            for kq in range(4):
                R.g.dma_start(
                    out=wo_blk[wp][:].rearrange("p (k n) -> p k n", k=NE)
                    [:, 4 * kq:4 * kq + 4, :],
                    in_=wo_src[:, 4 * kq:4 * kq + 4, n0:n0 + 512]
                ).inc(wo_ld[wp], 16)
            R.t.wait(wo_ld[wp])
            R.t.wait(evv)
            for ft4 in range(4):
                ft = fsup * 4 + ft4
                nft = ost["nft"]
                if nft >= 1:
                    R.t.wait(evv, ost["evac"][nft - 1])
                for k in range(NE):
                    R.t.matmul(p_op[0][:],
                               wo_blk[wp][:, k * 512 + ft4 * 128:
                                          k * 512 + (ft4 + 1) * 128],
                               att[ci % 2][:, k * STRIPE:(k + 1) * STRIPE],
                               start=(k == 0),
                               stop=(k == NE - 1)).inc(
                                   mm, 1 if k == NE - 1 else 0)
                R.v.wait(mm)
                R.v.wait(xld[xp])
                opair = ost["nft"] // 2
                op = opair % 2
                ot = o_sb[:, (op * 2 + ft4 % 2) * STRIPE:
                          (op * 2 + ft4 % 2 + 1) * STRIPE]
                if opair >= 2:
                    R.v.wait(sst[op])
                R.v.scalar_tensor_tensor(
                    ot, p_op[0][:], bo_sb[:, ft:ft + 1],
                    xrb[:, (xp * 4 + ft4) * STRIPE:(xp * 4 + ft4 + 1) * STRIPE],
                    ALU.add, ALU.add).inc(evv, 1)
                ost["evac"].append(R.val(evv))
                ost["nft"] += 1
                if ft4 % 2 == 1:
                    R.y.wait(evv)
                    R.y.dma_start(
                        out=xo3[:, fsup * 4 + ft4 - 1: fsup * 4 + ft4 + 1,
                                ci * STRIPE:(ci + 1) * STRIPE],
                        in_=o_sb[:, op * 2 * STRIPE:(op + 1) * 2 * STRIPE]
                        .rearrange("p (k t) -> p k t", k=2)).inc(sst[op], 16)
            ost["wo_mm"].append(R.val(mm))
            ost["wp"] += 1
            ost["sup_evac"].append(R.val(evv))
            ost["nsup"] += 1
    R.y.wait(sst[0])
    R.y.wait(sst[1])


def build_attn1a():
    nc = bass.Bass("TRN2", target_bir_lowering=False)
    with contextlib.ExitStack() as stack:
        R = Rec(nc, stack)
        W = {}
        x_in = nc.dram_tensor("x_in", [D, TPC], F32, kind="ExternalInput").ap()
        for nm, shp, dt in [("qw_t", [D, D], F32), ("kw_t", [D, D], F32),
                            ("vw_t", [D, D], F32), ("qb", [P, NE], F32),
                            ("kb", [1, D], F32), ("vb", [1, D], F32)]:
            W[nm] = nc.dram_tensor(nm, shp, dt, kind="ExternalInput").ap()
        W["qpT_out"] = nc.dram_tensor("qpT_out", [D, TPC], BF16, kind="ExternalOutput").ap()
        W["kv_out"] = nc.dram_tensor("kv_out", [2 * P, D], F32, kind="ExternalOutput").ap()
        W["ks_out"] = nc.dram_tensor("ks_out", [2, D], F32, kind="ExternalOutput").ap()
        cst = _mk_consts(R)
        emit_attn1a(R, x_in, W, cst, "a1")
        R.emit()
    return nc


def build_attn1b():
    nc = bass.Bass("TRN2", target_bir_lowering=False)
    with contextlib.ExitStack() as stack:
        R = Rec(nc, stack)
        W = {}
        W["x_in"] = nc.dram_tensor("x_in", [D, TPC], F32, kind="ExternalInput").ap()
        for nm, shp, dt in [("qpT", [D, TPC], BF16),
                            ("kv_all", [8 * 2 * P, D], F32),
                            ("ks_all", [8 * 2, D], F32),
                            ("ident", [P, P], BF16),
                            ("ow_t", [D, D], F32), ("ob", [P, NE], F32)]:
            W[nm] = nc.dram_tensor(nm, shp, dt, kind="ExternalInput").ap()
        W["x_out"] = nc.dram_tensor("x_out", [D, TPC], F32, kind="ExternalOutput").ap()
        cst = _mk_consts(R)
        emit_attn1b(R, W, cst, "b1")
        R.emit()
    return nc



def shard_x(x):
    out = []
    for r in range(NCORE):
        cols = [np.ascontiguousarray(x[b, s * STRIPE:(s + 1) * STRIPE, :].T)
                for (b, s) in chunks_of(r)]
        out.append(np.ascontiguousarray(np.concatenate(cols, axis=1)))
    return out


def unshard_x(slices):
    x = np.empty((B, S, D), np.float32)
    for r in range(NCORE):
        for i, (b, s) in enumerate(chunks_of(r)):
            x[b, s * STRIPE:(s + 1) * STRIPE, :] = \
                slices[r][:, i * STRIPE:(i + 1) * STRIPE].T
    return x


def scale_tile(s):
    return np.ascontiguousarray(np.asarray(s, np.float32).reshape(-1, P).T)


def build_fused():
    nc = bass.Bass("TRN2", target_bir_lowering=False)
    stack = contextlib.ExitStack()
    with stack:
        R = Rec(nc, stack)
        EI = lambda n, s, d: nc.dram_tensor(n, s, d, kind="ExternalInput").ap()
        IN = lambda n, s, d: nc.dram_tensor(n, s, d, kind="Internal").ap()

        x_in = EI("x_in", [D, TPC], F32)
        x_out = nc.dram_tensor("x_out", [D, TPC], F32, kind="ExternalOutput").ap()
        w_gate = EI("w_gate", [D, 8192], I8)
        w_up = EI("w_up", [D, 8192], I8)
        w_down = EI("w_down", [8192, D], I8)
        s_gate = EI("s_gate", [P, 64], F32)
        s_up = EI("s_up", [P, 64], F32)
        s_down = EI("s_down", [P, 16], F32)
        wq_t = EI("wq_t", [D, D], F32)
        wk_t = EI("wk_t", [D, D], F32)
        wv_t = EI("wv_t", [D, D], F32)
        bq = EI("bq", [P, NE], F32)
        bk = EI("bk", [P, NE], F32)
        bv = EI("bv", [1, D], F32)
        cs2 = EI("cs2", [P, 2 * TPC], F32)
        wo_t = EI("wo_t", [D, D], F32)
        bo = EI("bo", [P, NE], F32)
        masks = EI("masks", [P, 896], BF16)
        fut = EI("fut", [1, 128], F32)
        gWp = EI("gWp", [D, 1024], I32)
        gSc = EI("gSc", [D, 64], F32)
        uWp = EI("uWp", [D, 1024], I32)
        uSc = EI("uSc", [D, 64], F32)
        dWp = EI("dWp", [D, 1024], I32)
        dSc = EI("dSc", [D, 64], F32)
        ident = EI("ident", [P, P], BF16)
        qw_t = EI("qw_t", [D, D], F32)
        kw_t = EI("kw_t", [D, D], F32)
        vw_t = EI("vw_t", [D, D], F32)
        qb1 = EI("qb1", [P, NE], F32)
        kb1 = EI("kb1", [1, D], F32)
        vb1 = EI("vb1", [1, D], F32)
        ow_t = EI("ow_t", [D, D], F32)
        ob1 = EI("ob1", [P, NE], F32)

        x1 = IN("x1", [D, TPC], F32)
        x2 = IN("x2", [D, TPC], F32)
        x3 = IN("x3", [D, TPC], F32)
        x4 = IN("x4", [D, TPC], F32)
        x5 = IN("x5", [D, TPC], F32)
        qT = IN("qT", [D, TPC], BF16)
        kT = IN("kT", [D, TPC], BF16)
        v_hb = IN("v_hb", [NE * TPC, P], BF16)
        nrm = IN("nrm", [1, 64], F32)
        kn1 = IN("kn1", [1, 32], F32)
        kT_all = IN("kT_all", [8 * D, TPC], BF16)
        v_all = IN("v_all", [8 * NE * TPC, P], BF16)
        kn_all8 = IN("kn_all8", [8, 32], F32)
        dq_part = IN("dq_part", [3 * D, D], BF16)
        dq_all = IN("dq_all", [8 * 3 * D, D], BF16)
        qpT = IN("qpT", [D, TPC], BF16)
        kv_part = IN("kv_part", [2 * P, D], F32)
        ks_part = IN("ks_part", [2, D], F32)
        kv_sum = IN("kv_sum", [2 * P, D], F32)
        ks_sum = IN("ks_sum", [2, D], F32)

        ccs = [R.sem(f"cc{i}", barrier=False) for i in range(6)]
        RG = [list(range(NCORE))]
        cst = _mk_consts(R)

        sg_sb = R.sbuf("sg_sb", [P, 64], F32)
        su_sb = R.sbuf("su_sb", [P, 64], F32)
        sd_sb = R.sbuf("sd_sb", [P, 16], F32)
        sld = R.sem("sld")
        sini = R.sem("sini")
        R.g.dma_start(out=sg_sb[:], in_=s_gate).inc(sld, 16)
        R.g.dma_start(out=su_sb[:], in_=s_up).inc(sld, 16)
        R.g.dma_start(out=sd_sb[:], in_=s_down).inc(sld, 16)
        R.v.wait(sld)
        R.v.tensor_single_scalar(sd_sb[:], sd_sb[:], 0.5, ALU.mult).inc(sini, 1)
        for e in (R.t, R.s, R.y, R.g, R.v):
            e.wait(sld)
            e.wait(sini)

        W0 = dict(w_gate=w_gate, w_up=w_up, w_down=w_down,
                  sg_sb=sg_sb, su_sb=su_sb, sd_sb=sd_sb)

        def phase(fn, name="?", ccw=()):
            with contextlib.ExitStack() as ph:
                R.cur = ph
                R.phase = name
                R.phase_barrier()
                for eng, sem in ccw:
                    eng.wait(sem, 1)
                fn()
            R.cur = stack

        def f0a_body():
            run_jobs, njobs, finish = make_dequant_jobs(R, dict(
                gWp=gWp, gSc=gSc, uWp=uWp, uSc=uSc, dWp=dWp, dSc=dSc,
                ident=ident,
                wg_t_part=dq_part[0:D, :], wu_t_part=dq_part[D:2 * D, :],
                wd_t_part=dq_part[2 * D:3 * D, :]), cst, "dq")

            def mid():
                dq_joins = finish()
                for sem, valv in dq_joins:
                    R.g.wait(sem, valv)
                R.g.collective_compute("AllGather", ALU.bypass,
                                       replica_groups=RG,
                                       ins=[dq_part], outs=[dq_all]
                                       ).inc(ccs[0], 1)

            emit_ffn_pass(R, x_in, x1, W0, cst, "f0a", inter=8192, kind="l0",
                          co_job=lambda: run_jobs(6), mid_hook=mid)
        phase(f0a_body, "f0a")

        kcp = R.sem("kcp")

        def after_k(R2, joins):
            for sem, valv in joins:
                R2.y.wait(sem, valv)
            R2.y.dma_start(out=kn1, in_=R2.qk_nmax[0:1, 32:64]).inc(kcp, 16)
            for sem, valv in joins:
                R2.g.wait(sem, valv)
            R2.g.wait(kcp)
            R2.g.collective_compute("AllGather", ALU.bypass, replica_groups=RG,
                                    ins=[kT], outs=[kT_all]).inc(ccs[1], 1)
            R2.g.collective_compute("AllGather", ALU.bypass, replica_groups=RG,
                                    ins=[kn1], outs=[kn_all8]).inc(ccs[3], 1)

        def after_v(R2, joins):
            for sem, valv in joins:
                R2.g.wait(sem, valv)
            R2.g.collective_compute("AllGather", ALU.bypass, replica_groups=RG,
                                    ins=[v_hb], outs=[v_all]).inc(ccs[2], 1)

        phase(lambda: emit_qkv0(R, x1, dict(
            wq_t=wq_t, wk_t=wk_t, wv_t=wv_t, bq=bq, bk=bk, bv=bv, cs2=cs2,
            qT_out=qT, kT_out=kT, v_out=v_hb, nrm_out=nrm), cst, "qk",
            after_k=after_k, after_v=after_v), "qk")

        kn_all_view = bass.AP(kn_all8.tensor, 0, [[256, 1], [1, 256]])
        phase(lambda: emit_attn0(R, dict(
            x_in=x1, qT=qT, kT=kT, v_hb=v_hb, kT_all=kT_all, v_all=v_all,
            qn=nrm[0:1, 0:32], kn_all=kn_all_view, fut=fut, masks=masks,
            wo_t=wo_t, bo=bo, x_out=x2), cst, "a0"), "a0",
              ccw=((R.y, ccs[1]), (R.g, ccs[2]), (R.y, ccs[3])))

        phase(lambda: emit_ffn_pass(R, x2, x3, W0, cst, "f0b",
                                    inter=8192, kind="l0"), "f0b")

        def wg_slice(n0):
            rk = n0 // D
            nl = n0 % D
            return dq_all[rk * 3 * D: rk * 3 * D + D,
                          nl:nl + 512].rearrange("(k p) n -> p k n", p=P)

        def wu_slice(n0):
            rk = n0 // D
            nl = n0 % D
            return dq_all[rk * 3 * D + D: rk * 3 * D + 2 * D,
                          nl:nl + 512].rearrange("(k p) n -> p k n", p=P)

        def wd_slice(n0, h0, hl):
            rk = n0 // D
            nl = n0 % D
            base = rk * 3 * D + 2 * D
            return dq_all[base + nl: base + nl + P, h0:h0 + hl]

        W1 = dict(wg_slice=wg_slice, wu_slice=wu_slice, wd_slice=wd_slice)
        phase(lambda: emit_ffn_pass(R, x3, x4, W1, cst, "f1a",
                                    inter=16384, kind="l1"), "f1a",
              ccw=((R.g, ccs[0]),))

        phase(lambda: emit_attn1a(R, x4, dict(
            qw_t=qw_t, kw_t=kw_t, vw_t=vw_t, qb=qb1, kb=kb1, vb=vb1,
            qpT_out=qpT, kv_out=kv_part, ks_out=ks_part), cst, "a1"), "a1")
        for s in [s for s in R.semval if s not in getattr(R, "nobar", set())]:
            R.g.wait(s)
        R.g.collective_compute("AllReduce", ALU.add, replica_groups=RG,
                               ins=[kv_part], outs=[kv_sum]).inc(ccs[4], 1)
        R.g.collective_compute("AllReduce", ALU.add, replica_groups=RG,
                               ins=[ks_part], outs=[ks_sum]).inc(ccs[5], 1)

        phase(lambda: emit_attn1b(R, dict(
            x_in=x4, qpT=qpT, kv_sum=kv_sum, ks_sum=ks_sum, ident=ident,
            ow_t=ow_t, ob=ob1, x_out=x5), cst, "b1"), "b1",
              ccw=((R.g, ccs[4]), (R.g, ccs[5])))

        phase(lambda: emit_ffn_pass(R, x5, x_out, W1, cst, "f1b",
                                    inter=16384, kind="l1"), "f1b")
        R.phase_barrier()
        R.emit()
    return nc


def kernel_fused(**ins):
    cores = list(range(NCORE))
    PM = ROPE_PERM
    base = dict(
        w_gate=np.asarray(ins["L0_ffn_w_gate"]),
        w_up=np.asarray(ins["L0_ffn_w_up"]),
        w_down=np.asarray(ins["L0_ffn_w_down"]),
        s_gate=scale_tile(ins["L0_ffn_s_gate"]),
        s_up=scale_tile(ins["L0_ffn_s_up"]),
        s_down=scale_tile(ins["L0_ffn_s_down"]),
        wq_t=np.ascontiguousarray(np.asarray(ins["L0_attn_Wq"], np.float32)[PM].T),
        wk_t=np.ascontiguousarray(np.asarray(ins["L0_attn_Wk"], np.float32)[PM].T),
        wv_t=np.ascontiguousarray(np.asarray(ins["L0_attn_Wv"], np.float32).T),
        bq=bias_tile(ins["L0_attn_bq"], PM),
        bk=bias_tile(ins["L0_attn_bk"], PM),
        bv=np.asarray(ins["L0_attn_bv"], np.float32).reshape(1, D),
        wo_t=np.ascontiguousarray(np.asarray(ins["L0_attn_Wo"], np.float32).T),
        bo=bias_tile(ins["L0_attn_bo"]),
        masks=host_masks(),
        ident=np.eye(P).astype(BF),
        qw_t=np.ascontiguousarray(np.asarray(ins["L1_attn_q_w"], np.float32).T),
        kw_t=np.ascontiguousarray(np.asarray(ins["L1_attn_k_w"], np.float32).T),
        vw_t=np.ascontiguousarray(np.asarray(ins["L1_attn_v_w"], np.float32).T),
        qb1=bias_tile(ins["L1_attn_q_b"]),
        kb1=np.asarray(ins["L1_attn_k_b"], np.float32).reshape(1, D),
        vb1=np.asarray(ins["L1_attn_v_b"], np.float32).reshape(1, D),
        ow_t=np.ascontiguousarray(np.asarray(ins["L1_attn_o_w"], np.float32).T),
        ob1=bias_tile(ins["L1_attn_o_b"]),
    )
    x = np.asarray(ins["L0_attn_x"], np.float32)
    xsh = shard_x(x)
    im = []
    for r in cores:
        im.append(dict(
            x_in=xsh[r], cs2=host_rope_tables(r), fut=host_fut(r),
            gWp=np.asarray(ins["L1_ffn_gate_W_packed"])[r * D:(r + 1) * D],
            gSc=np.asarray(ins["L1_ffn_gate_scales"])[r * D:(r + 1) * D],
            uWp=np.asarray(ins["L1_ffn_up_W_packed"])[r * D:(r + 1) * D],
            uSc=np.asarray(ins["L1_ffn_up_scales"])[r * D:(r + 1) * D],
            dWp=np.ascontiguousarray(
                np.asarray(ins["L1_ffn_down_W_packed"])[:, r * 1024:(r + 1) * 1024]),
            dSc=np.ascontiguousarray(
                np.asarray(ins["L1_ffn_down_scales"])[:, r * 64:(r + 1) * 64]),
            **base))
    if "fused" not in _NC_CACHE:
        _NC_CACHE["fused"] = build_fused()
    res = run_bass_kernel_spmd(_NC_CACHE["fused"], im, core_ids=cores)
    return unshard_x([np.asarray(r["x_out"]) for r in res.results])


_NC_CACHE = {}


def _get_nc(name):
    if name not in _NC_CACHE:
        _NC_CACHE[name] = {"ffn0": build_ffn0, "qkv0": build_qkv0,
                           "attn0": build_attn0, "dequant": build_dequant,
                           "ffn1": build_ffn1, "attn1a": build_attn1a,
                           "attn1b": build_attn1b}[name]()
    return _NC_CACHE[name]


def _launch(name, in_maps, out_names):
    nc = _get_nc(name)
    res = run_bass_kernel_spmd(nc, in_maps, core_ids=list(range(NCORE)))
    return [{k: np.asarray(r[k]) for k in out_names} for r in res.results]


def kernel_unfused(**ins):
    cores = list(range(NCORE))
    ident = np.eye(P).astype(BF)
    masks = host_masks()

    x = np.asarray(ins["L0_attn_x"], np.float32)
    xsh = shard_x(x)
    f0_in = dict(w_gate=np.asarray(ins["L0_ffn_w_gate"]),
                 w_up=np.asarray(ins["L0_ffn_w_up"]),
                 w_down=np.asarray(ins["L0_ffn_w_down"]),
                 s_gate=scale_tile(ins["L0_ffn_s_gate"]),
                 s_up=scale_tile(ins["L0_ffn_s_up"]),
                 s_down=scale_tile(ins["L0_ffn_s_down"]))
    o = _launch("ffn0", [dict(x_in=xsh[r], **f0_in) for r in cores], ["x_out"])
    x1sh = [o[r]["x_out"] for r in cores]

    PM = ROPE_PERM
    qkv_in = dict(
        wq_t=np.ascontiguousarray(np.asarray(ins["L0_attn_Wq"], np.float32)[PM].T),
        wk_t=np.ascontiguousarray(np.asarray(ins["L0_attn_Wk"], np.float32)[PM].T),
        wv_t=np.ascontiguousarray(np.asarray(ins["L0_attn_Wv"], np.float32).T),
        bq=bias_tile(ins["L0_attn_bq"], PM),
        bk=bias_tile(ins["L0_attn_bk"], PM),
        bv=np.asarray(ins["L0_attn_bv"], np.float32).reshape(1, D))
    im = []
    for r in cores:
        im.append(dict(x_in=x1sh[r], cs2=host_rope_tables(r), **qkv_in))
    o = _launch("qkv0", im, ["qT_out", "kT_out", "v_out", "nrm_out"])
    kT_all = np.ascontiguousarray(np.concatenate([o[r]["kT_out"] for r in cores], axis=0))
    v_all = np.ascontiguousarray(np.concatenate([o[r]["v_out"] for r in cores], axis=0))
    kn_all = np.ascontiguousarray(np.concatenate([o[r]["nrm_out"][:, 32:64] for r in cores], axis=1))

    a0_in = dict(kT_all=kT_all, v_all=v_all, kn_all=kn_all, masks=masks,
                 wo_t=np.ascontiguousarray(np.asarray(ins["L0_attn_Wo"], np.float32).T),
                 bo=bias_tile(ins["L0_attn_bo"]))
    im = []
    for r in cores:
        im.append(dict(x_in=x1sh[r], qT=o[r]["qT_out"], kT=o[r]["kT_out"],
                       v_hb=o[r]["v_out"],
                       qn=np.ascontiguousarray(o[r]["nrm_out"][:, 0:32]),
                       fut=host_fut(r), **a0_in))
    o = _launch("attn0", im, ["x_out"])
    x2sh = [o[r]["x_out"] for r in cores]

    o = _launch("ffn0", [dict(x_in=x2sh[r], **f0_in) for r in cores], ["x_out"])
    x3sh = [o[r]["x_out"] for r in cores]

    im = []
    for r in cores:
        im.append(dict(
            gWp=np.asarray(ins["L1_ffn_gate_W_packed"])[r * D:(r + 1) * D],
            gSc=np.asarray(ins["L1_ffn_gate_scales"])[r * D:(r + 1) * D],
            uWp=np.asarray(ins["L1_ffn_up_W_packed"])[r * D:(r + 1) * D],
            uSc=np.asarray(ins["L1_ffn_up_scales"])[r * D:(r + 1) * D],
            dWp=np.ascontiguousarray(
                np.asarray(ins["L1_ffn_down_W_packed"])[:, r * 1024:(r + 1) * 1024]),
            dSc=np.ascontiguousarray(
                np.asarray(ins["L1_ffn_down_scales"])[:, r * 64:(r + 1) * 64]),
            ident=ident))
    o = _launch("dequant", im, ["wg_t_part", "wu_t_part", "wd_t_part"])
    wg_t = np.ascontiguousarray(np.concatenate([o[r]["wg_t_part"] for r in cores], axis=1))
    wu_t = np.ascontiguousarray(np.concatenate([o[r]["wu_t_part"] for r in cores], axis=1))
    wd_t = np.ascontiguousarray(np.concatenate([o[r]["wd_t_part"] for r in cores], axis=0))

    f1_in = dict(wg_t=wg_t, wu_t=wu_t, wd_t=wd_t)
    o = _launch("ffn1", [dict(x_in=x3sh[r], **f1_in) for r in cores], ["x_out"])
    x4sh = [o[r]["x_out"] for r in cores]

    a1_in = dict(
        qw_t=np.ascontiguousarray(np.asarray(ins["L1_attn_q_w"], np.float32).T),
        kw_t=np.ascontiguousarray(np.asarray(ins["L1_attn_k_w"], np.float32).T),
        vw_t=np.ascontiguousarray(np.asarray(ins["L1_attn_v_w"], np.float32).T),
        qb=bias_tile(ins["L1_attn_q_b"]),
        kb=np.asarray(ins["L1_attn_k_b"], np.float32).reshape(1, D),
        vb=np.asarray(ins["L1_attn_v_b"], np.float32).reshape(1, D))
    o = _launch("attn1a", [dict(x_in=x4sh[r], **a1_in) for r in cores],
                ["qpT_out", "kv_out", "ks_out"])
    kv_all = np.ascontiguousarray(np.concatenate([o[r]["kv_out"] for r in cores], axis=0))
    ks_all = np.ascontiguousarray(np.concatenate([o[r]["ks_out"] for r in cores], axis=0))

    b1_in = dict(kv_all=kv_all, ks_all=ks_all, ident=ident,
                 ow_t=np.ascontiguousarray(np.asarray(ins["L1_attn_o_w"], np.float32).T),
                 ob=bias_tile(ins["L1_attn_o_b"]))
    im = [dict(x_in=x4sh[r], qpT=o[r]["qpT_out"], **b1_in) for r in cores]
    o = _launch("attn1b", im, ["x_out"])
    x5sh = [o[r]["x_out"] for r in cores]

    o = _launch("ffn1", [dict(x_in=x5sh[r], **f1_in) for r in cores], ["x_out"])
    return unshard_x([o[r]["x_out"] for r in cores])


def kernel(**ins):
    return kernel_fused(**ins)


# version 26 (final)
# speedup vs baseline: 1.0806x; 1.0098x over previous
import contextlib
import numpy as np
import ml_dtypes
import concourse.bass as bass
import concourse.mybir as mybir
from concourse.bass_utils import run_bass_kernel_spmd

F32 = mybir.dt.float32
BF16 = mybir.dt.bfloat16
I32 = mybir.dt.int32
I8 = mybir.dt.int8
AF = mybir.ActivationFunctionType
ALU = mybir.AluOpType
AX = mybir.AxisListType

P = 128
D = 2048
NE = D // P
B = 2
S = 8192
NCORE = 8
STRIPE = 512
NSTRIPE = S // STRIPE
TPC = 2048
NTC = TPC // STRIPE
LN_EPS = 1e-5
BF = ml_dtypes.bfloat16


def stripes_of(r):
    return [r, NSTRIPE - 1 - r]


def chunks_of(r):
    return [(b, s) for b in range(B) for s in stripes_of(r)]



class _OpHandle:
    def __init__(self, rec, entry):
        self.rec = rec
        self.entry = entry

    def inc(self, sem, n=1):
        self.entry[3].append((sem, n))
        self.rec.semval[sem] = self.rec.semval.get(sem, 0) + n
        return self


class _EngProxy:
    def __init__(self, rec, eng):
        object.__setattr__(self, "_rec", rec)
        object.__setattr__(self, "_eng", eng)

    def wait(self, sem, val=None):
        rec = self._rec
        if val is None:
            val = rec.semval.get(sem, 0)
        if val <= 0:
            return
        if val <= rec.lastwait[self._eng].get(sem, -1):
            return
        rec.lastwait[self._eng][sem] = val
        rec.streams[self._eng].append(("wait", sem, val, []))

    def __getattr__(self, m):
        if m.startswith("_"):
            raise AttributeError(m)
        rec, eng = self._rec, self._eng

        def call(*a, **k):
            entry = ["op", m, (a, k), [], rec.phase]
            rec.streams[eng].append(entry)
            return _OpHandle(rec, entry)
        return call


class Rec:
    def __init__(self, nc, stack):
        self.nc = nc
        self.stack = stack
        self.cur = stack
        self.streams = {e: [] for e in ("tensor", "vector", "scalar", "gpsimd", "sync")}
        self.semval = {}
        self.lastwait = {e: {} for e in self.streams}
        self._semnames = 0
        self.phase = "init"
        self.t = _EngProxy(self, "tensor")
        self.v = _EngProxy(self, "vector")
        self.s = _EngProxy(self, "scalar")
        self.g = _EngProxy(self, "gpsimd")
        self.y = _EngProxy(self, "sync")

    def sem(self, name=None, pool=None, barrier=True):
        if not barrier:
            s = self.stack.enter_context(self.nc.semaphore(name or "nb"))
            self.semval[s] = 0
            if not hasattr(self, "nobar"):
                self.nobar = set()
            self.nobar.add(s)
            return s
        if pool is not None:
            if not hasattr(self, "pools"):
                self.pools = {}
            if pool not in self.pools:
                s = self.stack.enter_context(self.nc.semaphore(f"pool_{pool}"))
                self.semval[s] = 0
                self.pools[pool] = s
            return self.pools[pool]
        if name is None:
            name = f"sem{self._semnames}"
            self._semnames += 1
        s = self.stack.enter_context(self.nc.semaphore(name))
        self.semval[s] = 0
        return s

    def phase_barrier(self):
        nob = getattr(self, "nobar", set())
        sems = [s for s in self.semval if s not in nob]
        for e in (self.t, self.v, self.s, self.g, self.y):
            for s in sems:
                e.wait(s)

    def val(self, sem):
        return self.semval.get(sem, 0)

    def emit(self):
        with self.nc.Block() as block:
            for eng_name in ("tensor", "vector", "scalar", "gpsimd", "sync"):
                stream = self.streams[eng_name]
                if not stream:
                    continue

                def make(stream):
                    def body(eng):
                        for entry in stream:
                            if entry[0] == "wait":
                                _, sem, valv, _ = entry
                                eng.wait_ge(sem, valv)
                            else:
                                _, m, (a, k), incs, ph = entry
                                inst = getattr(eng, m)(*a, **k)
                                try:
                                    inst.annotate(ph)
                                except Exception:
                                    pass
                                for sem, n in incs:
                                    inst = inst.then_inc(sem, n)
                    return body
                getattr(block, eng_name)(make(stream))

    def sbuf(self, name, shape, dtype, stack=None):
        return (stack or self.cur).enter_context(
            self.nc.sbuf_tensor(name, shape, dtype))

    def psum(self, name, shape, dtype=F32, stack=None):
        return (stack or self.cur).enter_context(
            self.nc.psum_tensor(name, shape, dtype))



def emit_ln(R, x_dram, xn_sb, cst, tag, lstack):
    ld = [R.sem(pool=f"y{i}") for i in range(2)]
    mm = R.sem(pool="pe0")
    evs = R.sem(pool="act0")
    evv = R.sem(pool="dve0")
    x3 = x_dram.rearrange("(k p) t -> p k t", p=P)

    xbuf = [R.sbuf(f"{tag}_x{i}", [P, NE * STRIPE], F32, lstack) for i in range(2)]
    xbf = R.sbuf(f"{tag}_xbf", [P, NE * STRIPE], BF16, lstack)
    sqbuf = R.sbuf(f"{tag}_sq", [P, 2 * STRIPE], BF16, lstack)
    stat = R.sbuf(f"{tag}_stat", [1, 8 * STRIPE], F32, lstack)
    tmpbuf = R.sbuf(f"{tag}_tmp", [P, 2 * STRIPE], F32, lstack)
    bcast = R.sbuf(f"{tag}_bc", [P, 2 * STRIPE], F32, lstack)
    p_sum = R.psum(f"{tag}_ps", [1, STRIPE], stack=lstack)
    p_sq = R.psum(f"{tag}_pq", [1, STRIPE], stack=lstack)
    p_bc = R.psum(f"{tag}_pb", [P, 2 * STRIPE], stack=lstack)

    norm_done = {}
    sq_hist = []
    for tc in range(NTC):
        buf = xbuf[tc % 2]
        if tc >= 2:
            R.y.wait(evv, norm_done[tc - 2])
        for kq in range(4):
            R.y.dma_start(
                out=buf[:].rearrange("p (k t) -> p k t", k=NE)[:, 4 * kq:4 * kq + 4, :],
                in_=x3[:, 4 * kq:4 * kq + 4, tc * STRIPE:(tc + 1) * STRIPE],
            ).inc(ld[tc % 2], 16)
        R.t.wait(ld[tc % 2])
        R.s.wait(ld[tc % 2])
        R.v.wait(ld[tc % 2])
        if tc >= 2:
            R.v.wait(mm, sq_hist[-NE] if len(sq_hist) >= NE else 0)
        R.v.tensor_copy(xbf[:], buf[:]).inc(evv, 1)
        for k in range(NE):
            xt = xbf[:, k * STRIPE:(k + 1) * STRIPE]
            sq = sqbuf[:, (k % 2) * STRIPE:(k % 2 + 1) * STRIPE]
            slot = tc * NE + k
            if slot >= 2:
                R.s.wait(mm, sq_hist[slot - 2])
            R.s.wait(evv)
            R.s.activation(sq, xt, AF.Square).inc(evs, 1)
            R.t.wait(evs)
            R.t.matmul(p_sum[:], cst["ones_col_bf"][:], xt,
                       start=(k == 0), stop=(k == NE - 1))
            R.t.matmul(p_sq[:], cst["ones_col_bf"][:], sq,
                       start=(k == 0), stop=(k == NE - 1)).inc(mm, 1)
            sq_hist.append(R.val(mm))
        mean = stat[0:1, 0:STRIPE]
        m2 = stat[0:1, STRIPE:2 * STRIPE]
        var = stat[0:1, 2 * STRIPE:3 * STRIPE]
        sd = stat[0:1, 3 * STRIPE:4 * STRIPE]
        rstd = stat[0:1, 4 * STRIPE:5 * STRIPE]
        mb = stat[0:1, 5 * STRIPE:6 * STRIPE]
        R.v.wait(mm)
        R.v.tensor_single_scalar(mean, p_sum[:], 1.0 / D, ALU.mult).inc(evv, 1)
        R.s.wait(evv)
        R.s.activation(m2, mean, AF.Square).inc(evs, 1)
        R.v.wait(evs)
        R.v.scalar_tensor_tensor(var, p_sq[:], 1.0 / D, m2,
                                 ALU.mult, ALU.subtract).inc(evv, 1)
        R.v.wait(evv)
        R.v.tensor_single_scalar(var, var, LN_EPS, ALU.add).inc(evv, 1)
        R.s.wait(evv)
        R.s.activation(sd, var, AF.Sqrt).inc(evs, 1)
        R.v.wait(evs)
        R.v.reciprocal(rstd, sd).inc(evv, 1)
        R.v.wait(evv)
        R.v.scalar_tensor_tensor(mb, mean, -1.0, rstd, ALU.mult, ALU.mult).inc(evv, 1)
        R.t.wait(evv)
        R.t.matmul(p_bc[:, 0:STRIPE], cst["ones_row_f32"][:], rstd,
                   start=True, stop=True)
        R.t.matmul(p_bc[:, STRIPE:2 * STRIPE], cst["ones_row_f32"][:], mb,
                   start=True, stop=True).inc(mm, 1)
        R.s.wait(mm)
        R.s.activation(bcast[:], p_bc[:], AF.Copy).inc(evs, 1)
        R.v.wait(evs)
        for k in range(NE):
            xt = buf[:, k * STRIPE:(k + 1) * STRIPE]
            tmp = tmpbuf[:, (k % 2) * STRIPE:(k % 2 + 1) * STRIPE]
            R.v.wait(evv)
            R.v.tensor_tensor(tmp, xt, bcast[:, 0:STRIPE], ALU.mult).inc(evv, 1)
            R.v.wait(evv)
            R.v.tensor_tensor(
                xn_sb[:, k * TPC + tc * STRIPE: k * TPC + (tc + 1) * STRIPE],
                tmp, bcast[:, STRIPE:2 * STRIPE], ALU.add).inc(evv, 1)
        norm_done[tc] = R.val(evv)
    return evv, R.val(evv)


def emit_ffn_pass(R, x_dram, xout_dram, W, cst, tag, *, inter, kind, stop_after=None, max_chunks=8, co_job=None, mid_hook=None):
    nc = R.nc
    NSUP = inter // 512
    NSTEP = inter // 128
    DEPTH = 3
    xn = R.sbuf(f"{tag}_xn", [P, NE * TPC], BF16)

    with contextlib.ExitStack() as lnstack:
        lnev, ln_done = emit_ln(R, x_dram, xn, cst, f"{tag}_ln", lnstack)

    ld = [R.sem(pool=f"g{i}") for i in range(2)]
    fse = [R.sem(pool=f"y{[2, 3, 8][i]}") for i in range(DEPTH)]
    mm = R.sem(pool="pe0")
    evs = R.sem(pool="act0")
    evv = R.sem(pool="dve0")

    fuse_dram = nc.dram_tensor(f"{tag}_fuse", [inter, TPC], BF16, kind="Internal").ap()

    with contextlib.ExitStack() as p1stack:
        wg_buf = [R.sbuf(f"{tag}_wg{i}", [P, NE * 512], BF16, p1stack) for i in range(2)]
        wu_buf = [R.sbuf(f"{tag}_wu{i}", [P, NE * 512], BF16, p1stack) for i in range(2)]
        g_sb = R.sbuf(f"{tag}_g", [P, DEPTH * STRIPE], F32, p1stack)
        u_sb = R.sbuf(f"{tag}_u", [P, DEPTH * STRIPE], F32, p1stack)
        f_sb = R.sbuf(f"{tag}_f", [P, DEPTH * STRIPE], BF16, p1stack)
        gel = R.sbuf(f"{tag}_gel", [P, DEPTH * 2 * STRIPE], F32, p1stack) \
            if kind == "l1" else None
        p_g = [R.psum(f"{tag}_pg{i}", [P, STRIPE], stack=p1stack) for i in range(DEPTH)]
        p_u = [R.psum(f"{tag}_pu{i}", [P, STRIPE], stack=p1stack) for i in range(DEPTH)]

        if "wg_slice" in W:
            wg_slice = W["wg_slice"]
            wu_slice = W["wu_slice"]
        else:
            wg_full = (W["w_gate"] if kind == "l0" else W["wg_t"]).rearrange(
                "(k p) n -> p k n", p=P)
            wu_full = (W["w_up"] if kind == "l0" else W["wu_t"]).rearrange(
                "(k p) n -> p k n", p=P)
            wg_slice = lambda n0: wg_full[:, :, n0:n0 + 512]
            wu_slice = lambda n0: wu_full[:, :, n0:n0 + 512]

        R.g.wait(lnev, ln_done)
        sup_mm = []
        pipe = 0
        tile_evs = []
        tile_evv = []
        tile_fse = []
        for sup in range(NSUP):
            par = sup % 2
            if sup >= 2:
                R.g.wait(mm, sup_mm[sup - 2])
            n0 = sup * 512
            for kq in range(4):
                R.g.dma_start(
                    out=wg_buf[par][:].rearrange("p (k n) -> p k n", k=NE)
                    [:, 4 * kq:4 * kq + 4, :],
                    in_=wg_slice(n0)[:, 4 * kq:4 * kq + 4, :]).inc(ld[par], 16)
                R.g.dma_start(
                    out=wu_buf[par][:].rearrange("p (k n) -> p k n", k=NE)
                    [:, 4 * kq:4 * kq + 4, :],
                    in_=wu_slice(n0)[:, 4 * kq:4 * kq + 4, :]).inc(ld[par], 16)
            R.t.wait(ld[par])
            if sup == 0:
                R.t.wait(lnev, ln_done)
            if co_job is not None:
                co_job()
            for ns in range(4):
                for tc in range(NTC):
                    pp = pipe % DEPTH
                    if pipe >= DEPTH:
                        R.t.wait(evs, tile_evs[pipe - DEPTH])
                        if kind == "l1":
                            R.t.wait(evv, tile_evv[pipe - DEPTH])
                    for k in range(NE):
                        xnt = xn[:, k * TPC + tc * STRIPE: k * TPC + (tc + 1) * STRIPE]
                        R.t.matmul(p_g[pp][:],
                                   wg_buf[par][:, k * 512 + ns * 128:
                                               k * 512 + (ns + 1) * 128],
                                   xnt, start=(k == 0), stop=(k == NE - 1))
                    for k in range(NE):
                        xnt = xn[:, k * TPC + tc * STRIPE: k * TPC + (tc + 1) * STRIPE]
                        R.t.matmul(p_u[pp][:],
                                   wu_buf[par][:, k * 512 + ns * 128:
                                               k * 512 + (ns + 1) * 128],
                                   xnt, start=(k == 0),
                                   stop=(k == NE - 1)).inc(mm, 1 if k == NE - 1 else 0)
                    nt = sup * 4 + ns
                    gs = g_sb[:, pp * STRIPE:(pp + 1) * STRIPE]
                    us = u_sb[:, pp * STRIPE:(pp + 1) * STRIPE]
                    fs = f_sb[:, pp * STRIPE:(pp + 1) * STRIPE]
                    R.s.wait(mm)
                    if pipe >= DEPTH:
                        R.s.wait(evv, tile_evv[pipe - DEPTH])
                        R.s.wait(fse[pp], tile_fse[pipe - DEPTH])
                        R.v.wait(fse[pp], tile_fse[pipe - DEPTH])
                    if kind == "l0":
                        R.s.activation(gs, p_g[pp][:], AF.Silu,
                                       scale=W["sg_sb"][:, nt:nt + 1]).inc(evs, 1)
                        R.s.activation(us, p_u[pp][:], AF.Copy,
                                       scale=W["su_sb"][:, nt:nt + 1]).inc(evs, 1)
                        R.v.wait(evs)
                        R.v.tensor_tensor(fs, gs, us, ALU.mult).inc(evv, 1)
                    else:
                        emit_gelu_fuse(R, p_g[pp][:], p_u[pp][:], gs, us, fs,
                                       gel, pp, evs, evv,
                                       tile_evv[pipe - DEPTH] if pipe >= DEPTH else 0)
                    R.y.wait(evv)
                    R.y.dma_start(out=fuse_dram[nt * P:(nt + 1) * P,
                                                tc * STRIPE:(tc + 1) * STRIPE],
                                  in_=fs).inc(fse[pp], 16)
                    tile_evs.append(R.val(evs))
                    tile_evv.append(R.val(evv))
                    tile_fse.append(R.val(fse[pp]))
                    pipe += 1
            sup_mm.append(R.val(mm))
        phase1_evs = R.val(evs)
        phase1_evv = R.val(evv)
        phase1_fse = [R.val(s) for s in fse]

    if mid_hook is not None:
        mid_hook()

    HH = 1024
    with contextlib.ExitStack() as p2stack:
        wd_buf = [R.sbuf(f"{tag}_wd{i}", [P, 4 * HH], BF16, p2stack) for i in range(2)]
        fu_buf = [R.sbuf(f"{tag}_fu{i}", [P, 4 * STRIPE], BF16, p2stack) for i in range(2)]
        xr_buf = R.sbuf(f"{tag}_xr", [P, 8 * STRIPE], F32, p2stack)
        o_sb = R.sbuf(f"{tag}_o", [P, 8 * STRIPE], F32, p2stack)
        p_d = R.psum(f"{tag}_pd", [P, 8 * STRIPE], stack=p2stack)
        ld2g = [R.sem(pool=f"g{2 + i}") for i in range(2)]
        ld2y = [R.sem(pool=f"y{4 + i}") for i in range(2)]
        xld = R.sem(pool="s0")
        st = R.sem(pool="y6")
        x3 = x_dram.rearrange("(k p) t -> p k t", p=P)
        xo3 = xout_dram.rearrange("(k p) t -> p k t", p=P)
        if "wd_slice" in W:
            wd_slice = W["wd_slice"]
        else:
            wd_full = W["w_down"] if kind == "l0" else W["wd_t"]
            wd_slice = lambda n0, h0, hl: wd_full[n0:n0 + P, h0:h0 + hl]

        for e in (R.y, R.g, R.s, R.t):
            e.wait(evs, phase1_evs)
            e.wait(evv, phase1_evv)
            for i, s in enumerate(fse):
                e.wait(s, phase1_fse[i])

        step = 0
        step_mm = []
        chunk_evv = []
        chunk_st = []
        GRP = 4
        for hh in range(2):
            for tc in range(NTC):
                ci = hh * NTC + tc
                if ci >= 1:
                    R.s.wait(st, chunk_st[ci - 1])
                    R.s.wait(evv, chunk_evv[ci - 1])
                R.s.dma_start(
                    out=xr_buf[:].rearrange("p (k t) -> p k t", k=8),
                    in_=x3[:, hh * 8:hh * 8 + 8,
                           tc * STRIPE:(tc + 1) * STRIPE]).inc(xld, 16)
                for ng in range(NSTEP // GRP):
                    par = step % 2
                    if step >= 2:
                        R.g.wait(mm, step_mm[step - 2])
                        R.y.wait(mm, step_mm[step - 2])
                    for j in range(GRP):
                        R.g.dma_start(
                            out=wd_buf[par][:, j * HH:(j + 1) * HH],
                            in_=wd_slice((ng * GRP + j) * P, hh * HH, HH)
                            ).inc(ld2g[par], 16)
                    R.y.dma_start(
                        out=fu_buf[par][:].rearrange("p (j t) -> p j t", j=GRP),
                        in_=fuse_dram[ng * GRP * P:(ng + 1) * GRP * P,
                                      tc * STRIPE:(tc + 1) * STRIPE]
                        .rearrange("(j p) t -> p j t", p=P)).inc(ld2y[par], 16)
                    R.t.wait(ld2g[par])
                    R.t.wait(ld2y[par])
                    if ng == 0 and ci >= 1:
                        R.t.wait(evv, chunk_evv[ci - 1])
                    for j in range(GRP):
                        n = ng * GRP + j
                        for ht in range(8):
                            R.t.matmul(p_d[:, ht * STRIPE:(ht + 1) * STRIPE],
                                       wd_buf[par][:, j * HH + ht * P:
                                                    j * HH + (ht + 1) * P],
                                       fu_buf[par][:, j * STRIPE:(j + 1) * STRIPE],
                                       start=(n == 0),
                                       stop=(n == NSTEP - 1)).inc(
                                           mm, 1 if ht == 7 and j == GRP - 1 else 0)
                    step_mm.append(R.val(mm))
                    step += 1
                R.v.wait(mm)
                R.v.wait(xld)
                if ci >= 1:
                    R.v.wait(st, chunk_st[ci - 1])
                for ht in range(8):
                    k = hh * 8 + ht
                    ot = o_sb[:, ht * STRIPE:(ht + 1) * STRIPE]
                    xt = xr_buf[:, ht * STRIPE:(ht + 1) * STRIPE]
                    pd = p_d[:, ht * STRIPE:(ht + 1) * STRIPE]
                    if kind == "l0":
                        R.v.tensor_scalar(ot, pd, W["sd_sb"][:, k:k + 1], None,
                                          ALU.mult).inc(evv, 1)
                        R.v.wait(evv)
                        R.v.tensor_tensor(ot, ot, xt, ALU.add).inc(evv, 1)
                    else:
                        R.v.scalar_tensor_tensor(ot, pd, 0.5, xt,
                                                 ALU.mult, ALU.add).inc(evv, 1)
                chunk_evv.append(R.val(evv))
                R.y.wait(evv)
                R.y.dma_start(out=xo3[:, hh * 8:hh * 8 + 8,
                                      tc * STRIPE:(tc + 1) * STRIPE],
                              in_=o_sb[:].rearrange("p (k t) -> p k t", k=8)
                              ).inc(st, 16)
                chunk_st.append(R.val(st))
        R.y.wait(st)
    return st


def emit_gelu_fuse(R, p_g, p_u, gs, us, fs, gel, pp, evs, evv, prev_evv):
    s1 = gel[:, (2 * pp + 0) * STRIPE:(2 * pp + 1) * STRIPE]
    w = gel[:, (2 * pp + 1) * STRIPE:(2 * pp + 2) * STRIPE]
    CC = 0.7978845608028654
    R.s.activation(s1, p_g, AF.Square).inc(evs, 1)
    R.s.activation(us, p_u, AF.Copy).inc(evs, 1)
    R.v.wait(evs)
    R.v.wait(evv, prev_evv)
    R.v.tensor_scalar(s1, s1, CC * 0.044715, CC, ALU.mult, ALU.add).inc(evv, 1)
    R.v.wait(evv)
    R.v.tensor_tensor(w, s1, p_g, ALU.mult).inc(evv, 1)
    R.v.tensor_tensor(gs, us, p_g, ALU.mult).inc(evv, 1)
    R.s.wait(evv)
    R.s.activation(w, w, AF.Tanh).inc(evs, 1)
    R.v.wait(evs)
    R.v.tensor_scalar(w, w, 0.5, 0.5, ALU.mult, ALU.add).inc(evv, 1)
    R.v.wait(evv)
    R.v.tensor_tensor(fs, gs, w, ALU.mult).inc(evv, 1)


def _mk_consts(R):
    cst = {}
    cst["ones_col_f32"] = R.sbuf("c_ones_col", [P, 1], F32)
    cst["ones_row_f32"] = R.sbuf("c_ones_row", [1, P], F32)
    cst["ones_col_bf"] = R.sbuf("c_ones_col_bf", [P, 1], BF16)
    cst["ones_row_bf"] = R.sbuf("c_ones_row_bf", [1, P], BF16)
    csem = R.sem("c_init")
    R.v.memset(cst["ones_col_f32"][:], 1.0).inc(csem, 1)
    R.v.memset(cst["ones_row_f32"][:], 1.0).inc(csem, 1)
    R.v.memset(cst["ones_col_bf"][:], 1.0).inc(csem, 1)
    R.v.memset(cst["ones_row_bf"][:], 1.0).inc(csem, 1)
    for e in (R.t, R.s, R.y, R.g):
        e.wait(csem)
    return cst



def build_ffn0(stop_after=None, max_chunks=8):
    nc = bass.Bass("TRN2", target_bir_lowering=False)
    with contextlib.ExitStack() as stack:
        R = Rec(nc, stack)
        x_in = nc.dram_tensor("x_in", [D, TPC], F32, kind="ExternalInput").ap()
        w_gate = nc.dram_tensor("w_gate", [D, 8192], I8, kind="ExternalInput").ap()
        w_up = nc.dram_tensor("w_up", [D, 8192], I8, kind="ExternalInput").ap()
        w_down = nc.dram_tensor("w_down", [8192, D], I8, kind="ExternalInput").ap()
        s_gate = nc.dram_tensor("s_gate", [P, 64], F32, kind="ExternalInput").ap()
        s_up = nc.dram_tensor("s_up", [P, 64], F32, kind="ExternalInput").ap()
        s_down = nc.dram_tensor("s_down", [P, 16], F32, kind="ExternalInput").ap()
        x_out = nc.dram_tensor("x_out", [D, TPC], F32, kind="ExternalOutput").ap()

        cst = _mk_consts(R)
        sg_sb = R.sbuf("sg_sb", [P, 64], F32)
        su_sb = R.sbuf("su_sb", [P, 64], F32)
        sd_sb = R.sbuf("sd_sb", [P, 16], F32)
        sld = R.sem("sld")
        sini = R.sem("sini")
        R.g.dma_start(out=sg_sb[:], in_=s_gate).inc(sld, 16)
        R.g.dma_start(out=su_sb[:], in_=s_up).inc(sld, 16)
        R.g.dma_start(out=sd_sb[:], in_=s_down).inc(sld, 16)
        R.v.wait(sld)
        R.v.tensor_single_scalar(sd_sb[:], sd_sb[:], 0.5, ALU.mult).inc(sini, 1)
        for e in (R.t, R.s, R.y, R.g, R.v):
            e.wait(sld)
            e.wait(sini)

        W = dict(w_gate=w_gate, w_up=w_up, w_down=w_down,
                 sg_sb=sg_sb, su_sb=su_sb, sd_sb=sd_sb)
        emit_ffn_pass(R, x_in, x_out, W, cst, "f0", inter=8192, kind="l0",
                      stop_after=stop_after, max_chunks=max_chunks)
        R.emit()
    return nc


SC_ATTN = 0.08838834764831845


def emit_proj_ch(R, xn, w_dram, bias_sb, out_cb, cst, tag, joins):
    with contextlib.ExitStack() as pstack:
        ld = [R.sem(pool=f"g{i}") for i in range(2)]
        mm = R.sem(pool="pe0")
        evs = R.sem(pool="act0")
        wbuf = [R.sbuf(f"{tag}_w{i}", [P, NE * 512], BF16, pstack) for i in range(2)]
        e_sb = R.sbuf(f"{tag}_e", [P, 2 * STRIPE], F32, pstack)
        p_o = [R.psum(f"{tag}_po{i}", [P, STRIPE], stack=pstack) for i in range(2)]
        w_src = w_dram.rearrange("(k p) n -> p k n", p=P)

        for e in (R.g, R.t, R.s, R.v, R.y):
            for sem, valv in joins:
                e.wait(sem, valv)

        pipe = 0
        sup_mm = []
        tile_done = []
        tile_evs = []
        for sup in range(4):
            par = sup % 2
            if sup >= 2:
                R.g.wait(mm, sup_mm[sup - 2])
            n0 = sup * 512
            for kq in range(4):
                R.g.dma_start(
                    out=wbuf[par][:].rearrange("p (k n) -> p k n", k=NE)
                    [:, 4 * kq:4 * kq + 4, :],
                    in_=w_src[:, 4 * kq:4 * kq + 4, n0:n0 + 512]).inc(ld[par], 16)
            R.t.wait(ld[par])
            for ns in range(4):
                ft = sup * 4 + ns
                for tc in range(NTC):
                    pp = pipe % 2
                    if pipe >= 2:
                        for sem, valv in tile_done[pipe - 2]:
                            R.t.wait(sem, valv)
                        R.t.wait(evs, tile_evs[pipe - 2])
                    for k in range(NE):
                        xnt = xn[:, k * TPC + tc * STRIPE: k * TPC + (tc + 1) * STRIPE]
                        R.t.matmul(p_o[pp][:],
                                   wbuf[par][:, k * 512 + ns * 128:
                                             k * 512 + (ns + 1) * 128],
                                   xnt, start=(k == 0),
                                   stop=(k == NE - 1)).inc(
                                       mm, 1 if k == NE - 1 else 0)
                    R.s.wait(mm)
                    if pipe >= 2:
                        for sem, valv in tile_done[pipe - 2]:
                            R.s.wait(sem, valv)
                    es = e_sb[:, pp * STRIPE:(pp + 1) * STRIPE]
                    if bias_sb is not None:
                        R.s.activation(es, p_o[pp][:], AF.Identity,
                                       bias=bias_sb[:, ft:ft + 1]).inc(evs, 1)
                    else:
                        R.s.activation(es, p_o[pp][:], AF.Copy).inc(evs, 1)
                    tile_evs.append(R.val(evs))
                    tile_done.append(out_cb(ft, tc, pp, es, evs))
                    pipe += 1
            sup_mm.append(R.val(mm))
        out_joins = [(mm, R.val(mm)), (evs, R.val(evs))]
        for cond in tile_done[-2:]:
            out_joins += cond
    return out_joins


def emit_qkv0(R, x_dram, W, cst, tag, after_k=None, after_v=None):
    nc = R.nc
    xn = R.sbuf(f"{tag}_xn", [P, NE * TPC], BF16)
    nrm = R.sem(pool="pe1")
    nev_v = R.sem(pool="dve0")
    sq_sb = R.sbuf(f"{tag}_sqn", [P, 3 * STRIPE], F32)
    nmax = R.sbuf(f"{tag}_nmax", [1, 64], F32)
    R.qk_nmax = nmax
    p_n = R.psum(f"{tag}_pn", [1, TPC])
    out_sb = R.sbuf(f"{tag}_ob", [P, 3 * STRIPE], BF16)
    r_scr = R.sbuf(f"{tag}_rs", [64, 2 * STRIPE], F32)
    cs2 = R.sbuf(f"{tag}_cs2", [P, 2 * TPC], F32)
    bq_sb = R.sbuf(f"{tag}_bq", [P, NE], F32)
    bk_sb = R.sbuf(f"{tag}_bk", [P, NE], F32)
    bvrow = R.sbuf(f"{tag}_bv", [1, D], BF16)
    cl = R.sem(pool="y6")
    bl = R.sem(pool="g4")
    R.y.dma_start(out=cs2[:], in_=W["cs2"]).inc(cl, 16)
    R.y.dma_start(out=bq_sb[:], in_=W["bq"]).inc(cl, 16)
    R.y.dma_start(out=bk_sb[:], in_=W["bk"]).inc(cl, 16)
    R.g.dma_start(out=bvrow[0:1, :], in_=W["bv"]).inc(bl, 16)
    for e in (R.s, R.t, R.v):
        e.wait(cl)
        e.wait(bl)

    with contextlib.ExitStack() as lnstack:
        lnev, ln_done = emit_ln(R, x_dram, xn, cst, f"{tag}_ln", lnstack)
    R.g.wait(lnev, ln_done)
    QD = 3
    wbuf = [R.sbuf(f"{tag}_w{i}", [P, NE * 512], BF16) for i in range(2)]
    e_sb = R.sbuf(f"{tag}_e", [P, QD * STRIPE], F32)
    p_o = [R.psum(f"{tag}_po{i}", [P, STRIPE]) for i in range(QD)]

    ld = [R.sem(pool=f"g{i}") for i in range(2)]
    mm = R.sem(pool="pe0")
    evs = R.sem(pool="act0")
    sqs = R.sem(pool="act1")
    evv = R.sem(pool="dve0")
    sst = [R.sem(pool=f"y{[2, 3, 10][i]}") for i in range(QD)]

    qT3 = W["qT_out"].rearrange("(h p) t -> p h t", p=P)
    kT3 = W["kT_out"].rearrange("(h p) t -> p h t", p=P)
    vo = W["v_out"]
    srcs = dict(q=W["wq_t"].rearrange("(k p) n -> p k n", p=P),
                k=W["wk_t"].rearrange("(k p) n -> p k n", p=P),
                v=W["wv_t"].rearrange("(k p) n -> p k n", p=P))
    jobs = [("k", s) for s in range(4)] + [("v", s) for s in range(4)] + \
           [("q", s) for s in range(4)]

    pipe = 0
    job_mm = []
    tile_done = []
    tile_sst = []
    nrm_hist = []
    for ji, (proj, sup) in enumerate(jobs):
        par = ji % 2
        if ji >= 2:
            R.g.wait(mm, job_mm[ji - 2])
        n0 = sup * 512
        for kq in range(4):
            R.g.dma_start(
                out=wbuf[par][:].rearrange("p (k n) -> p k n", k=NE)
                [:, 4 * kq:4 * kq + 4, :],
                in_=srcs[proj][:, 4 * kq:4 * kq + 4, n0:n0 + 512]).inc(ld[par], 16)
        R.t.wait(ld[par])
        if ji == 0:
            R.t.wait(lnev, ln_done)
        for t16 in range(NE):
            pp = pipe % QD
            if pipe >= QD:
                for sem, valv in tile_done[pipe - QD]:
                    R.t.wait(sem, valv)
            if proj == "v":
                tt = t16
                for k in range(NE):
                    R.t.matmul(p_o[pp][:],
                               xn[:, k * TPC + tt * P: k * TPC + (tt + 1) * P],
                               wbuf[par][:, k * 512:(k + 1) * 512],
                               start=(k == 0), stop=False)
                R.t.matmul(p_o[pp][:], cst["ones_row_bf"][:],
                           bvrow[0:1, n0:n0 + 512],
                           start=False, stop=True).inc(mm, 1)
                R.s.wait(mm)
                if pipe >= QD:
                    for sem, valv in tile_done[pipe - QD]:
                        R.s.wait(sem, valv)
                    R.s.wait(sst[pp], tile_sst[pipe - QD])
                ob = out_sb[:, pp * STRIPE:(pp + 1) * STRIPE]
                R.s.activation(ob, p_o[pp][:], AF.Copy).inc(evs, 1)
                R.y.wait(evs)
                for hd in range(4):
                    h = sup * 4 + hd
                    R.y.dma_start(
                        out=vo[h * TPC + tt * P: h * TPC + (tt + 1) * P, :],
                        in_=ob[:, hd * P:(hd + 1) * P]).inc(sst[pp], 16)
                tile_done.append([(evs, R.val(evs))])
                tile_sst.append(R.val(sst[pp]))
            else:
                ns, tc = t16 // 4, t16 % 4
                ft = sup * 4 + ns
                for k in range(NE):
                    xnt = xn[:, k * TPC + tc * STRIPE: k * TPC + (tc + 1) * STRIPE]
                    R.t.matmul(p_o[pp][:],
                               wbuf[par][:, k * 512 + ns * 128:
                                         k * 512 + (ns + 1) * 128],
                               xnt, start=(k == 0),
                               stop=(k == NE - 1)).inc(mm, 1 if k == NE - 1 else 0)
                R.s.wait(mm)
                if pipe >= QD:
                    for sem, valv in tile_done[pipe - QD]:
                        R.s.wait(sem, valv)
                es = e_sb[:, pp * STRIPE:(pp + 1) * STRIPE]
                bias = bk_sb if proj == "k" else bq_sb
                R.s.activation(es, p_o[pp][:], AF.Identity,
                               bias=bias[:, ft:ft + 1]).inc(evs, 1)
                o3 = kT3 if proj == "k" else qT3
                qn_off = 32 if proj == "k" else 0
                qe = es[0:64, :]
                qo = es[64:128, :]
                ct0 = cs2[0:64, tc * STRIPE:(tc + 1) * STRIPE]
                sn64 = cs2[64:128, tc * STRIPE:(tc + 1) * STRIPE]
                sn0 = cs2[0:64, TPC + tc * STRIPE: TPC + (tc + 1) * STRIPE]
                ct64 = cs2[64:128, TPC + tc * STRIPE: TPC + (tc + 1) * STRIPE]
                r1 = r_scr[0:64, 0:STRIPE]
                r2 = r_scr[0:64, STRIPE:2 * STRIPE]
                ob = out_sb[:, pp * STRIPE:(pp + 1) * STRIPE]
                R.v.wait(evs)
                if pipe >= QD:
                    R.v.wait(sst[pp], tile_sst[pipe - QD])
                R.v.wait(evv)
                R.v.tensor_tensor(r1, qe, ct0, ALU.mult).inc(evv, 1)
                R.v.tensor_tensor(r2, qo, sn64, ALU.mult).inc(evv, 1)
                R.v.wait(evv)
                R.v.tensor_tensor(ob[0:64, :], r1, r2, ALU.subtract).inc(evv, 1)
                R.v.wait(evv)
                R.v.tensor_tensor(r1, qe, sn0, ALU.mult).inc(evv, 1)
                R.v.tensor_tensor(r2, qo, ct64, ALU.mult).inc(evv, 1)
                R.v.wait(evv)
                R.v.tensor_tensor(ob[64:128, :], r1, r2, ALU.add).inc(evv, 1)
                R.s.wait(evv)
                sqt = sq_sb[:, pp * STRIPE:(pp + 1) * STRIPE]
                if len(nrm_hist) >= 2:
                    R.s.wait(nrm, nrm_hist[-2])
                R.s.activation(sqt, ob, AF.Square).inc(sqs, 1)
                R.t.wait(sqs)
                if ft > 0 and tc == 0:
                    R.t.wait(nev_v)
                R.t.matmul(p_n[0:1, tc * STRIPE:(tc + 1) * STRIPE],
                           cst["ones_col_f32"][:], sqt,
                           start=True, stop=True).inc(nrm, 1)
                nrm_hist.append(R.val(nrm))
                R.y.wait(evv)
                R.y.dma_start(out=o3[:, ft, tc * STRIPE:(tc + 1) * STRIPE],
                              in_=ob).inc(sst[pp], 16)
                tile_sst.append(R.val(sst[pp]))
                if tc == NTC - 1:
                    R.v.wait(nrm)
                    for b in range(B):
                        R.v.tensor_reduce(
                            nmax[0:1, qn_off + b * 16 + ft:
                                 qn_off + b * 16 + ft + 1],
                            p_n[0:1, b * 1024:(b + 1) * 1024],
                            AX.X, ALU.max).inc(nev_v, 1)
                tile_done.append([(evv, R.val(evv)), (sqs, R.val(sqs))])
            pipe += 1
        job_mm.append(R.val(mm))
        if ji == 3 and after_k is not None:
            after_k(R, [(sst[0], R.val(sst[0])), (sst[1], R.val(sst[1])),
                        (nev_v, R.val(nev_v))])
        if ji == 7 and after_v is not None:
            after_v(R, [(sst[0], R.val(sst[0])), (sst[1], R.val(sst[1]))])

    fst = R.sem(pool="y7")
    R.y.wait(nev_v)
    R.y.dma_start(out=W["nrm_out"], in_=nmax[0:1, :]).inc(fst, 16)
    R.y.wait(fst)
    R.y.wait(sst[0])
    R.y.wait(sst[1])


def emit_v_tok(R, xn, W, cst, tag, bvrow, joins):
    with contextlib.ExitStack() as pstack:
        ld = [R.sem(pool=f"g{i}") for i in range(2)]
        mm = R.sem(pool="pe0")
        evs = R.sem(pool="act0")
        sst = [R.sem(pool=f"y{2 + i}") for i in range(2)]
        wbuf = [R.sbuf(f"{tag}_w{i}", [P, NE * 512], BF16, pstack) for i in range(2)]
        vb_sb = R.sbuf(f"{tag}_vb", [P, 2 * STRIPE], BF16, pstack)
        p_o = [R.psum(f"{tag}_po{i}", [P, STRIPE], stack=pstack) for i in range(2)]
        w_src = W["wv_t"].rearrange("(k p) n -> p k n", p=P)
        vo = W["v_out"]

        for e in (R.g, R.t, R.s, R.y):
            for sem, valv in joins:
                e.wait(sem, valv)

        pipe = 0
        sup_mm = []
        tile_done = []
        for sup in range(4):
            par = sup % 2
            if sup >= 2:
                R.g.wait(mm, sup_mm[sup - 2])
            n0 = sup * 512
            for kq in range(4):
                R.g.dma_start(
                    out=wbuf[par][:].rearrange("p (k n) -> p k n", k=NE)
                    [:, 4 * kq:4 * kq + 4, :],
                    in_=w_src[:, 4 * kq:4 * kq + 4, n0:n0 + 512]).inc(ld[par], 16)
            R.t.wait(ld[par])
            for tt in range(NE):
                pp = pipe % 2
                if pipe >= 2:
                    R.t.wait(evs, tile_done[pipe - 2])
                for k in range(NE):
                    R.t.matmul(p_o[pp][:],
                               xn[:, k * TPC + tt * P: k * TPC + (tt + 1) * P],
                               wbuf[par][:, k * 512:(k + 1) * 512],
                               start=(k == 0), stop=False)
                R.t.matmul(p_o[pp][:], cst["ones_row_bf"][:],
                           bvrow[0:1, n0:n0 + 512],
                           start=False, stop=True).inc(mm, 1)
                R.s.wait(mm)
                R.s.wait(sst[pp])
                vb = vb_sb[:, pp * STRIPE:(pp + 1) * STRIPE]
                R.s.activation(vb, p_o[pp][:], AF.Copy).inc(evs, 1)
                R.y.wait(evs)
                for hd in range(4):
                    h = sup * 4 + hd
                    R.y.dma_start(
                        out=vo[h * TPC + tt * P: h * TPC + (tt + 1) * P, :],
                        in_=vb[:, hd * P:(hd + 1) * P]).inc(sst[pp], 16)
                tile_done.append(R.val(evs))
                pipe += 1
            sup_mm.append(R.val(mm))
        out_joins = [(mm, R.val(mm)), (evs, R.val(evs)),
                     (sst[0], R.val(sst[0])), (sst[1], R.val(sst[1]))]
    return out_joins


def emit_attn0(R, W, cst, tag):
    nc = R.nc
    NPAIR = 32

    c0s = R.sem(pool="act0")
    c0v = R.sem(pool="dve0")
    c0m = R.sem(pool="pe0")
    cln = R.sem(pool="y6")
    qn_sb = R.sbuf(f"{tag}_qn", [1, 32], F32)
    kn_sb = R.sbuf(f"{tag}_kn", [1, 256], F32)
    c0_sb = R.sbuf(f"{tag}_c0", [1, 128], F32)
    fut_sb = R.sbuf(f"{tag}_fut", [1, 128], F32)
    mask_sb = R.sbuf(f"{tag}_mask", [P, 4 * 512], BF16)
    c0b = R.sbuf(f"{tag}_c0b", [P, 32], F32)
    futb = R.sbuf(f"{tag}_futb", [P, 128], F32)
    bias_sb = R.sbuf(f"{tag}_bias", [P, 2 * NPAIR], F32)
    c0stack = contextlib.ExitStack()
    p_c0 = R.psum(f"{tag}_pc0", [P, 160], stack=c0stack)
    R.y.dma_start(out=qn_sb[:], in_=W["qn"]).inc(cln, 16)
    R.y.dma_start(out=kn_sb[:], in_=W["kn_all"]).inc(cln, 16)
    R.y.dma_start(out=fut_sb[:], in_=W["fut"]).inc(cln, 16)
    R.y.dma_start(out=mask_sb[:], in_=W["masks"]).inc(cln, 16)
    R.v.wait(cln)
    km = c0_sb[0:1, 32:64]
    prod = c0_sb[0:1, 64:96]
    c0r = c0_sb[0:1, 0:32]
    for bh in range(32):
        R.v.tensor_reduce(km[0:1, bh:bh + 1],
                          bass.AP(kn_sb, bh, [[256, 1], [32, 8]]),
                          AX.X, ALU.max).inc(c0v, 1)
    R.v.wait(c0v)
    R.v.tensor_tensor(prod, qn_sb[0:1, :], km, ALU.mult).inc(c0v, 1)
    R.s.wait(c0v)
    R.s.activation(c0_sb[0:1, 96:128], prod, AF.Sqrt).inc(c0s, 1)
    R.v.wait(c0s)
    R.v.tensor_single_scalar(c0r, c0_sb[0:1, 96:128], -SC_ATTN,
                             ALU.mult).inc(c0v, 1)
    R.t.wait(c0v)
    R.t.matmul(p_c0[:, 0:32], cst["ones_row_f32"][:], c0r,
               start=True, stop=True)
    R.t.matmul(p_c0[:, 32:160], cst["ones_row_f32"][:], fut_sb[0:1, :],
               start=True, stop=True).inc(c0m, 1)
    R.s.wait(c0m)
    R.s.activation(c0b[:], p_c0[:, 0:32], AF.Copy).inc(c0s, 1)
    R.s.activation(futb[:], p_c0[:, 32:160], AF.Copy).inc(c0s, 1)
    c0_ready = R.val(c0s)
    c0stack.close()

    kbuf = [R.sbuf(f"{tag}_kb{i}", [P, 64 * P], BF16) for i in range(2)]
    vbuf = [R.sbuf(f"{tag}_vb{i}", [P, 64 * P], BF16) for i in range(2)]
    qbuf = [R.sbuf(f"{tag}_qb{i}", [P, NE * STRIPE], BF16) for i in range(2)]
    kdb = [R.sbuf(f"{tag}_kd{i}", [P, 4 * P], BF16) for i in range(2)]
    vdb = [R.sbuf(f"{tag}_vd{i}", [P, 4 * P], BF16) for i in range(2)]
    ptb = [R.sbuf(f"{tag}_pt{i}", [P, 1024], BF16) for i in range(2)]
    att = [R.sbuf(f"{tag}_at{i}", [P, NE * STRIPE], BF16) for i in range(2)]
    lsb = R.sbuf(f"{tag}_l", [1, 2 * STRIPE], F32)
    lbc = R.sbuf(f"{tag}_lbc", [P, STRIPE], F32)
    p_s = [R.psum(f"{tag}_ps{i}", [P, 1024]) for i in range(2)]
    p_o = [R.psum(f"{tag}_po{i}", [P, STRIPE]) for i in range(2)]
    p_l = R.psum(f"{tag}_pl", [1, STRIPE])
    p_b2 = R.psum(f"{tag}_pb2", [P, STRIPE])

    kld = [R.sem(pool=f"y{i}") for i in range(2)]
    vld = [R.sem(pool=f"g{i}") for i in range(2)]
    qld = [R.sem(pool=f"s{1 + i}") for i in range(2)]
    dld = [R.sem(pool=f"y{2 + i}") for i in range(2)]
    mm = R.sem(pool="pe0")
    exs = R.sem(pool="act0")
    evv = R.sem(pool="dve0")

    kT_all = W["kT_all"]
    v_all3 = W["v_all"].rearrange("(x p) d -> p x d", p=P)
    kT_own3 = W["kT"].rearrange("(h p) t -> p h t", p=P)
    qT3 = W["qT"].rearrange("(h p) t -> p h t", p=P)
    v_own3 = W["v_hb"].rearrange("(x p) d -> p x d", p=P)

    kvdone = []
    att_evv = []
    recip_done = []
    hist_mask = []
    diagdone = []
    expdone_slot = []
    hist_exp = []
    hist_ptb = []
    batch_mm = {}
    oproj_state = dict(wp=0, wo_mm=[], nft=0, evac=[], nsup=0, sup_evac=[])
    wo_ld = [R.sem(pool=f"g{2 + i}") for i in range(2)]
    xld = [R.sem(pool=["s0", "s3"][i]) for i in range(2)]
    sst = [R.sem(pool=f"y{4 + i}") for i in range(2)]
    wo_blk = [R.sbuf(f"{tag}_wo{i}", [P, NE * 512], BF16) for i in range(2)]
    xrb = R.sbuf(f"{tag}_xr", [P, 2 * 4 * STRIPE], F32)
    o_sb = R.sbuf(f"{tag}_osb", [P, 2 * 2 * STRIPE], F32)
    p_op = [p_b2]
    wo_src = W["wo_t"].rearrange("(k p) n -> p k n", p=P)
    bo_sb = R.sbuf(f"{tag}_bo", [P, NE], F32)
    bol = R.sem(pool="y7")
    R.y.dma_start(out=bo_sb[:], in_=W["bo"]).inc(bol, 16)
    R.v.wait(bol)

    x3 = W["x_in"].rearrange("(k p) t -> p k t", p=P)
    xo3 = W["x_out"].rearrange("(k p) t -> p k t", p=P)

    pipe_pair = 0
    hslot = 0
    for b in range(B):
        for h in range(NE):
            hp = hslot % 2
            if hslot >= 2:
                R.y.wait(mm, kvdone[hslot - 2])
                R.g.wait(mm, kvdone[hslot - 2])
            for rk in range(8):
                for j, s in enumerate([rk, 15 - rk]):
                    R.y.dma_start(
                        out=kbuf[hp][:, (4 * s) * P:(4 * s + 4) * P],
                        in_=kT_all[rk * D + h * P: rk * D + (h + 1) * P,
                                   (2 * b + j) * STRIPE:(2 * b + j + 1) * STRIPE]
                    ).inc(kld[hp], 16)
                    R.g.dma_start(
                        out=vbuf[hp][:].rearrange("p (kt d) -> p kt d", d=P)
                        [:, 4 * s:4 * s + 4, :],
                        in_=v_all3[:, ((rk * NE + h) * NE + (2 * b + j) * 4):
                                   ((rk * NE + h) * NE + (2 * b + j + 1) * 4), :]
                    ).inc(vld[hp], 16)
            R.t.wait(kld[hp])
            R.t.wait(vld[hp])
            for cil in range(2):
                ci = 2 * b + cil
                cp = ci % 2
                dslot = hslot * 2 + cil
                dp = dslot % 2
                bh = b * NE + h
                if h == 0:
                    if b > 0:
                        R.s.wait(mm, batch_mm[b - 1])
                    R.s.dma_start(
                        out=qbuf[cp][:].rearrange("p (k t) -> p k t", k=NE),
                        in_=qT3[:, :, ci * STRIPE:(ci + 1) * STRIPE]
                    ).inc(qld[cp], 16)
                if dslot >= 2:
                    R.y.wait(mm, diagdone[dslot - 2])
                R.y.dma_start(out=kdb[dp][:],
                              in_=kT_own3[:, h, ci * STRIPE:(ci + 1) * STRIPE]
                              ).inc(dld[dp], 16)
                R.y.dma_start(out=vdb[dp][:].rearrange("p (j d) -> p j d", d=P),
                              in_=v_own3[:, h * NE + ci * 4: h * NE + ci * 4 + 4, :]
                              ).inc(dld[dp], 16)
                R.v.wait(c0s, c0_ready)
                if dslot >= 2:
                    R.v.wait(exs, expdone_slot[dslot - 2])
                bt = bias_sb[:, dp * NPAIR:(dp + 1) * NPAIR]
                R.v.wait(evv)
                R.v.tensor_scalar(bt, futb[:, ci * NPAIR:(ci + 1) * NPAIR],
                                  -1e30, None, ALU.mult).inc(evv, 1)
                R.v.wait(evv)
                R.v.tensor_scalar(bt, bt, c0b[:, bh:bh + 1], None,
                                  ALU.add).inc(evv, 1)
                bias_ready = R.val(evv)
                R.t.wait(evv, bias_ready)
                R.t.wait(qld[cp])
                R.t.wait(dld[dp])
                qh = qbuf[cp][:, h * STRIPE:(h + 1) * STRIPE]
                npair_ci = 14 if cil == 0 else 30
                NPC = npair_ci + 2
                pob = (hslot * 2 + cil) % 2

                def pair_operands(p):
                    diag = p >= npair_ci
                    if diag:
                        j0 = (p - npair_ci) * 2
                        return (diag, j0,
                                kdb[dp][:, j0 * P:(j0 + 1) * P],
                                kdb[dp][:, (j0 + 1) * P:(j0 + 2) * P],
                                vdb[dp][:, j0 * P:(j0 + 1) * P],
                                vdb[dp][:, (j0 + 1) * P:(j0 + 2) * P])
                    return (diag, 0,
                            kbuf[hp][:, (2 * p) * P:(2 * p + 1) * P],
                            kbuf[hp][:, (2 * p + 1) * P:(2 * p + 2) * P],
                            vbuf[hp][:, (2 * p) * P:(2 * p + 1) * P],
                            vbuf[hp][:, (2 * p + 1) * P:(2 * p + 2) * P])

                for p in range(NPC + 1):
                    if p < NPC:
                        par = pipe_pair % 2
                        diag, j0, lk0, lk1, _, _ = pair_operands(p)
                        if pipe_pair >= 2:
                            R.t.wait(exs, hist_exp[pipe_pair - 2])
                        R.t.matmul(p_s[par][:, 0:512], lk0, qh,
                                   start=True, stop=True)
                        R.t.matmul(p_s[par][:, 512:1024], lk1, qh,
                                   start=True, stop=True).inc(mm, 1)
                        R.s.wait(mm)
                        if pipe_pair >= 2:
                            R.s.wait(mm, hist_ptb[pipe_pair - 2])
                        bias_ap = (c0b[:, bh:bh + 1] if diag
                                   else bt[:, p:p + 1])
                        R.s.activation(ptb[par][:], p_s[par][:], AF.Exp,
                                       bias=bias_ap, scale=SC_ATTN).inc(exs, 1)
                        hist_exp.append(R.val(exs))
                        if diag:
                            R.v.wait(exs)
                            R.v.tensor_tensor(
                                ptb[par][:], ptb[par][:],
                                mask_sb[:, j0 * 512:(j0 + 2) * 512],
                                ALU.mult).inc(evv, 1)
                        hist_mask.append(R.val(evv))
                        pipe_pair += 1
                    if p >= 1:
                        pv = p - 1
                        gp = pipe_pair - 2 if p < NPC else pipe_pair - 1
                        par2 = gp % 2
                        diag2, j02, _, _, lv0, lv1 = pair_operands(pv)
                        if diag2:
                            R.t.wait(evv, hist_mask[gp])
                        else:
                            R.t.wait(exs, hist_exp[gp])
                        first = (pv == 0)
                        last = (pv == NPC - 1)
                        if first:
                            ds = hslot * 2 + cil
                            if ds >= 2:
                                R.t.wait(evv, att_evv[ds - 2])
                            if ds >= 1:
                                R.t.wait(evv, recip_done[ds - 1])
                        R.t.matmul(p_o[pob][:], lv0, ptb[par2][:, 0:512],
                                   start=first, stop=False)
                        R.t.matmul(p_o[pob][:], lv1, ptb[par2][:, 512:1024],
                                   start=False, stop=last)
                        R.t.matmul(p_l[:], cst["ones_col_bf"][:],
                                   ptb[par2][:, 0:512],
                                   start=first, stop=False)
                        R.t.matmul(p_l[:], cst["ones_col_bf"][:],
                                   ptb[par2][:, 512:1024],
                                   start=False, stop=last).inc(mm, 1)
                        hist_ptb.append(R.val(mm))
                expdone_slot.append(R.val(exs))
                diagdone.append(R.val(mm))
                R.v.wait(mm)
                lrow = lsb[0:1, dp * STRIPE:(dp + 1) * STRIPE]
                R.v.reciprocal(lrow, p_l[:]).inc(evv, 1)
                recip_done.append(R.val(evv))
                R.t.wait(evv)
                R.t.wait(exs)
                R.t.matmul(p_b2[:], cst["ones_row_f32"][:], lrow,
                           start=True, stop=True).inc(mm, 1)
                R.s.wait(mm)
                R.s.wait(evv)
                R.s.activation(lbc[:], p_b2[:], AF.Copy).inc(exs, 1)
                R.v.wait(exs)
                if b > 0:
                    R.v.wait(mm, batch_mm[b - 1])
                R.v.tensor_tensor(att[cil][:, h * STRIPE:(h + 1) * STRIPE],
                                  p_o[pob][:], lbc[:], ALU.mult).inc(evv, 1)
                att_evv.append(R.val(evv))
            kvdone.append(R.val(mm))
            hslot += 1
        for cil in range(2):
            ci = 2 * b + cil
            for fsup in range(4):
                xp = oproj_state["nsup"] % 2
                if oproj_state["nsup"] >= 2:
                    R.s.wait(evv, oproj_state["sup_evac"][oproj_state["nsup"] - 2])
                R.s.dma_start(
                    out=xrb[:, xp * 4 * STRIPE:(xp + 1) * 4 * STRIPE]
                    .rearrange("p (k t) -> p k t", k=4),
                    in_=x3[:, fsup * 4:(fsup + 1) * 4,
                           ci * STRIPE:(ci + 1) * STRIPE]).inc(xld[xp], 16)
                wp = oproj_state["wp"] % 2
                if oproj_state["wp"] >= 2:
                    R.g.wait(mm, oproj_state["wo_mm"][oproj_state["wp"] - 2])
                n0 = fsup * 512
                for kq in range(4):
                    R.g.dma_start(
                        out=wo_blk[wp][:].rearrange("p (k n) -> p k n", k=NE)
                        [:, 4 * kq:4 * kq + 4, :],
                        in_=wo_src[:, 4 * kq:4 * kq + 4, n0:n0 + 512]
                    ).inc(wo_ld[wp], 16)
                R.t.wait(wo_ld[wp])
                R.t.wait(evv)
                for ft4 in range(4):
                    ft = fsup * 4 + ft4
                    nft = oproj_state["nft"]
                    pq = 0
                    if nft >= 1:
                        R.t.wait(evv, oproj_state["evac"][nft - 1])
                    for k in range(NE):
                        R.t.matmul(p_op[pq][:],
                                   wo_blk[wp][:, k * 512 + ft4 * 128:
                                              k * 512 + (ft4 + 1) * 128],
                                   att[cil][:, k * STRIPE:(k + 1) * STRIPE],
                                   start=(k == 0),
                                   stop=(k == NE - 1)).inc(
                                       mm, 1 if k == NE - 1 else 0)
                    R.v.wait(mm)
                    R.v.wait(xld[xp])
                    opair = oproj_state["nft"] // 2
                    op = opair % 2
                    ot = o_sb[:, (op * 2 + ft4 % 2) * STRIPE:
                              (op * 2 + ft4 % 2 + 1) * STRIPE]
                    if opair >= 2:
                        R.v.wait(sst[op])
                    R.v.scalar_tensor_tensor(
                        ot, p_op[pq][:], bo_sb[:, ft:ft + 1],
                        xrb[:, (xp * 4 + ft4) * STRIPE:(xp * 4 + ft4 + 1) * STRIPE],
                        ALU.add, ALU.add).inc(evv, 1)
                    oproj_state["evac"].append(R.val(evv))
                    oproj_state["nft"] += 1
                    if ft4 % 2 == 1:
                        R.y.wait(evv)
                        R.y.dma_start(
                            out=xo3[:, fsup * 4 + ft4 - 1: fsup * 4 + ft4 + 1,
                                    ci * STRIPE:(ci + 1) * STRIPE],
                            in_=o_sb[:, op * 2 * STRIPE:(op + 1) * 2 * STRIPE]
                            .rearrange("p (k t) -> p k t", k=2)).inc(sst[op], 16)
                oproj_state["wo_mm"].append(R.val(mm))
                oproj_state["wp"] += 1
                oproj_state["sup_evac"].append(R.val(evv))
                oproj_state["nsup"] += 1
        batch_mm[b] = R.val(mm)
    R.y.wait(sst[0])
    R.y.wait(sst[1])
    return


def build_qkv0():
    nc = bass.Bass("TRN2", target_bir_lowering=False)
    with contextlib.ExitStack() as stack:
        R = Rec(nc, stack)
        W = {}
        x_in = nc.dram_tensor("x_in", [D, TPC], F32, kind="ExternalInput").ap()
        for nm, shp, dt in [("wq_t", [D, D], F32), ("wk_t", [D, D], F32),
                            ("wv_t", [D, D], F32), ("bq", [P, NE], F32),
                            ("bk", [P, NE], F32), ("bv", [1, D], F32),
                            ("cs2", [P, 2 * TPC], F32)]:
            W[nm] = nc.dram_tensor(nm, shp, dt, kind="ExternalInput").ap()
        W["qT_out"] = nc.dram_tensor("qT_out", [D, TPC], BF16, kind="ExternalOutput").ap()
        W["kT_out"] = nc.dram_tensor("kT_out", [D, TPC], BF16, kind="ExternalOutput").ap()
        W["v_out"] = nc.dram_tensor("v_out", [NE * TPC, P], BF16, kind="ExternalOutput").ap()
        W["nrm_out"] = nc.dram_tensor("nrm_out", [1, 64], F32, kind="ExternalOutput").ap()
        cst = _mk_consts(R)
        emit_qkv0(R, x_in, W, cst, "qk")
        R.emit()
    return nc


def build_attn0():
    nc = bass.Bass("TRN2", target_bir_lowering=False)
    with contextlib.ExitStack() as stack:
        R = Rec(nc, stack)
        W = {}
        W["x_in"] = nc.dram_tensor("x_in", [D, TPC], F32, kind="ExternalInput").ap()
        for nm, shp, dt in [("qT", [D, TPC], BF16), ("kT", [D, TPC], BF16),
                            ("v_hb", [NE * TPC, P], BF16),
                            ("kT_all", [8 * D, TPC], BF16),
                            ("v_all", [8 * NE * TPC, P], BF16),
                            ("qn", [1, 32], F32), ("kn_all", [1, 256], F32),
                            ("fut", [1, 128], F32), ("masks", [P, 4 * 512], BF16),
                            ("wo_t", [D, D], F32), ("bo", [P, NE], F32)]:
            W[nm] = nc.dram_tensor(nm, shp, dt, kind="ExternalInput").ap()
        W["x_out"] = nc.dram_tensor("x_out", [D, TPC], F32, kind="ExternalOutput").ap()
        cst = _mk_consts(R)
        emit_attn0(R, W, cst, "a0")
        R.emit()
    return nc



ROPE_PERM = np.concatenate(
    [np.concatenate([h * 128 + 2 * np.arange(64),
                     h * 128 + 2 * np.arange(64) + 1]) for h in range(16)])


def host_rope_tables(r):
    inv = (10000.0 ** (-np.arange(0, 128, 2) / 128.0)).astype(np.float64)
    cols = []
    for (b, s) in chunks_of(r):
        pos = np.arange(s * STRIPE, (s + 1) * STRIPE, dtype=np.float64)
        cols.append(pos)
    pos = np.concatenate(cols)
    ang = pos[None, :] * inv[:, None]
    c = np.cos(ang).astype(np.float32)
    s_ = np.sin(ang).astype(np.float32)
    return np.ascontiguousarray(np.block([[c, s_], [s_, c]]))


def host_masks():
    k = np.arange(128)[:, None]
    q = np.arange(512)[None, :]
    tiles = [(q >= j * 128 + k).astype(np.float32) for j in range(4)]
    return np.concatenate(tiles, axis=1).astype(BF)


def host_fut(r):
    out = np.zeros((1, 128), np.float32)
    for ci, (b, s) in enumerate(chunks_of(r)):
        for pr in range(32):
            if 2 * pr >= 4 * s:
                out[0, ci * 32 + pr] = 1.0
    return out


def bias_tile(bvec, perm=None):
    v = np.asarray(bvec, np.float32)
    if perm is not None:
        v = v[perm]
    return np.ascontiguousarray(v.reshape(NE, P).T)


def emit_nvfp4_tile(R, sp, sc_view, out_bf, scr, evv):
    W2 = sp.shape[1] * 2
    c = scr["c"]
    sg = scr["sg"]
    m = scr["m"]
    g2 = scr["g2"]
    g1 = scr["g1"]
    a = scr["a"]
    def op(fn):
        R.v.wait(evv)
        fn().inc(evv, 1)
    op(lambda: R.v.tensor_single_scalar(c[:, 0:W2:2], sp, 15, ALU.bitwise_and))
    op(lambda: R.v.tensor_scalar(c[:, 1:W2:2], sp, 4, 15,
                                 ALU.logical_shift_right, ALU.bitwise_and))
    op(lambda: R.v.tensor_scalar(sg[:], c[:], 8, 28,
                                 ALU.bitwise_and, ALU.logical_shift_left))
    op(lambda: R.v.tensor_single_scalar(m[:], c[:], 7, ALU.bitwise_and))
    op(lambda: R.v.tensor_single_scalar(g2[:], m[:], 2, ALU.is_ge))
    op(lambda: R.v.tensor_single_scalar(g1[:], m[:], 1, ALU.is_ge))
    op(lambda: R.v.scalar_tensor_tensor(a[:], m[:], 251, g2[:], ALU.add, ALU.add))
    op(lambda: R.v.tensor_tensor(a[:], a[:], g1[:], ALU.mult))
    op(lambda: R.v.tensor_single_scalar(a[:], a[:], 22, ALU.logical_shift_left))
    op(lambda: R.v.tensor_tensor(a[:], a[:], sg[:], ALU.bitwise_or))
    op(lambda: R.v.tensor_tensor(
        out_bf.rearrange("p (b i) -> p b i", i=32),
        a[:].bitcast(F32).rearrange("p (b i) -> p b i", i=32),
        sc_view, ALU.mult))


def make_dequant_jobs(R, W, cst, tag, psum_stack=None):
    nc = R.nc
    ld = [R.sem(pool=f"g{4 + i}") for i in range(2)]
    mm = R.sem(pool="pe0")
    evs = R.sem(pool="act0")
    evv = R.sem(pool="dve0")
    sst = [R.sem(pool=["y7", "y9"][i]) for i in range(2)]

    sp_b = [R.sbuf(f"{tag}_sp{i}", [P, 512], I32) for i in range(2)]
    sc_b = [R.sbuf(f"{tag}_sc{i}", [P, 32], F32) for i in range(2)]
    scr = {k: R.sbuf(f"{tag}_{k}", [P, 1024], I32)
           for k in ("c", "sg", "m", "g2", "g1", "a")}
    vals = R.sbuf(f"{tag}_vals", [P, 1024], BF16)
    tv_b = [R.sbuf(f"{tag}_tv{i}", [P, 1024], BF16) for i in range(2)]
    ident = R.sbuf(f"{tag}_id", [P, P], BF16)
    p_t = [R.psum(f"{tag}_pt{i}", [P, 1024], BF16, stack=psum_stack)
           for i in range(2)]
    idl = R.sem(pool="y6")
    R.y.dma_start(out=ident[:], in_=W["ident"]).inc(idl, 16)
    R.t.wait(idl)

    jobs = []
    for mat, wp, sc, od in (("g", W["gWp"], W["gSc"], W["wg_t_part"]),
                            ("u", W["uWp"], W["uSc"], W["wu_t_part"]),
                            ("d", W["dWp"], W["dSc"], W["wd_t_part"])):
        for rt in range(16):
            for half in range(2):
                jobs.append((wp[rt * P:(rt + 1) * P, half * 512:(half + 1) * 512],
                             sc[rt * P:(rt + 1) * P, half * 32:(half + 1) * 32],
                             od, rt, half))

    state = dict(ji=0, tile_evv=[], tile_mm=[], tile_evs=[])

    def run_jobs(kk):
        for _ in range(kk):
            ji = state["ji"]
            if ji >= len(jobs):
                return
            src_, scsrc, outspec, rblk, half = jobs[ji]
            par = ji % 2
            if ji >= 2:
                R.g.wait(evv, state["tile_evv"][ji - 2])
            R.g.dma_start(out=sp_b[par][:], in_=src_).inc(ld[par], 16)
            R.g.dma_start(out=sc_b[par][:], in_=scsrc).inc(ld[par], 16)
            R.v.wait(ld[par])
            if ji >= 1:
                R.v.wait(mm, state["tile_mm"][ji - 1])
            sc_view = bass.AP(sc_b[par], 0, [[32, P], [1, 32], [0, 32]])
            emit_nvfp4_tile(R, sp_b[par][:], sc_view, vals[:], scr, evv)
            state["tile_evv"].append(R.val(evv))
            R.t.wait(evv)
            if ji >= 2:
                R.t.wait(evs, state["tile_evs"][ji - 2])
            for k in range(8):
                h = R.t.transpose(p_t[par][:, k * P:(k + 1) * P],
                                  vals[:, k * P:(k + 1) * P], ident[:])
            h.inc(mm, 1)
            state["tile_mm"].append(R.val(mm))
            R.s.wait(mm)
            if ji >= 2:
                R.s.wait(sst[par])
            R.s.activation(tv_b[par][:], p_t[par][:], AF.Copy).inc(evs, 1)
            state["tile_evs"].append(R.val(evs))
            R.y.wait(evs)
            o3 = outspec.rearrange("(k p) n -> p k n", p=P)
            for kq in range(2):
                R.y.dma_start(
                    out=o3[:, half * 8 + 4 * kq: half * 8 + 4 * kq + 4,
                           rblk * P:(rblk + 1) * P],
                    in_=tv_b[par][:].rearrange("p (k c) -> p k c", k=8)
                    [:, 4 * kq:4 * kq + 4, :]).inc(sst[par], 16)
            state["ji"] = ji + 1

    def finish():
        run_jobs(len(jobs) - state["ji"])
        return [(sst[0], R.val(sst[0])), (sst[1], R.val(sst[1])),
                (evs, R.val(evs))]

    return run_jobs, len(jobs), finish


def emit_dequant(R, W, cst, tag):
    run_jobs, n, finish = make_dequant_jobs(R, W, cst, tag)
    finish()


def build_dequant():
    nc = bass.Bass("TRN2", target_bir_lowering=False)
    with contextlib.ExitStack() as stack:
        R = Rec(nc, stack)
        W = {}
        W["gWp"] = nc.dram_tensor("gWp", [D, 1024], I32, kind="ExternalInput").ap()
        W["gSc"] = nc.dram_tensor("gSc", [D, 64], F32, kind="ExternalInput").ap()
        W["uWp"] = nc.dram_tensor("uWp", [D, 1024], I32, kind="ExternalInput").ap()
        W["uSc"] = nc.dram_tensor("uSc", [D, 64], F32, kind="ExternalInput").ap()
        W["dWp"] = nc.dram_tensor("dWp", [D, 1024], I32, kind="ExternalInput").ap()
        W["dSc"] = nc.dram_tensor("dSc", [D, 64], F32, kind="ExternalInput").ap()
        W["ident"] = nc.dram_tensor("ident", [P, P], BF16, kind="ExternalInput").ap()
        W["wg_t_part"] = nc.dram_tensor("wg_t_part", [D, D], BF16, kind="ExternalOutput").ap()
        W["wu_t_part"] = nc.dram_tensor("wu_t_part", [D, D], BF16, kind="ExternalOutput").ap()
        W["wd_t_part"] = nc.dram_tensor("wd_t_part", [D, D], BF16, kind="ExternalOutput").ap()
        cst = _mk_consts(R)
        emit_dequant(R, W, cst, "dq")
        R.emit()
    return nc


def build_ffn1():
    nc = bass.Bass("TRN2", target_bir_lowering=False)
    with contextlib.ExitStack() as stack:
        R = Rec(nc, stack)
        x_in = nc.dram_tensor("x_in", [D, TPC], F32, kind="ExternalInput").ap()
        wg_t = nc.dram_tensor("wg_t", [D, 16384], BF16, kind="ExternalInput").ap()
        wu_t = nc.dram_tensor("wu_t", [D, 16384], BF16, kind="ExternalInput").ap()
        wd_t = nc.dram_tensor("wd_t", [16384, D], BF16, kind="ExternalInput").ap()
        x_out = nc.dram_tensor("x_out", [D, TPC], F32, kind="ExternalOutput").ap()
        cst = _mk_consts(R)
        W = dict(wg_t=wg_t, wu_t=wu_t, wd_t=wd_t)
        emit_ffn_pass(R, x_in, x_out, W, cst, "f1", inter=16384, kind="l1")
        R.emit()
    return nc


def emit_attn1a(R, x_dram, W, cst, tag, after_kv=None):
    nc = R.nc
    xn = R.sbuf(f"{tag}_xn", [P, NE * TPC], BF16)
    bq_sb = R.sbuf(f"{tag}_bq", [P, NE], F32)
    bkrow = R.sbuf(f"{tag}_bk", [1, D], BF16)
    bvrow = R.sbuf(f"{tag}_bv", [1, D], BF16)
    elu_sb = R.sbuf(f"{tag}_elu", [P, 4 * STRIPE], F32)
    qp_sb = R.sbuf(f"{tag}_qp", [P, 2 * STRIPE], BF16)
    bl = R.sem(pool="y6")
    blg = R.sem(pool="g4")
    R.y.dma_start(out=bq_sb[:], in_=W["qb"]).inc(bl, 16)
    R.g.dma_start(out=bkrow[0:1, :], in_=W["kb"]).inc(blg, 16)
    R.g.dma_start(out=bvrow[0:1, :], in_=W["vb"]).inc(blg, 16)
    for e in (R.s, R.t, R.v):
        e.wait(bl)
        e.wait(blg)

    with contextlib.ExitStack() as lnstack:
        lnev, ln_done = emit_ln(R, x_dram, xn, cst, f"{tag}_ln", lnstack)
    joins = [(lnev, ln_done)]

    with contextlib.ExitStack() as ps:
        ld = [R.sem(pool=f"g{i}") for i in range(2)]
        mm = R.sem(pool="pe0")
        axs = R.sem(pool="act0")
        evv = R.sem(pool="dve0")
        kst = R.sem(pool="y4")
        wkb = [R.sbuf(f"{tag}_wk{i}", [P, NE * 512], BF16, ps) for i in range(2)]
        wvb = [R.sbuf(f"{tag}_wv{i}", [P, NE * 512], BF16, ps) for i in range(2)]
        kpt = [R.sbuf(f"{tag}_kp{i}", [P, 512], BF16, ps) for i in range(2)]
        vt = [R.sbuf(f"{tag}_vt{i}", [P, 512], BF16, ps) for i in range(2)]
        kvo = R.sbuf(f"{tag}_kvo", [P, 2 * 512], F32, ps)
        kso = R.sbuf(f"{tag}_kso", [1, 2 * 512], F32, ps)
        p_k = [R.psum(f"{tag}_pk{i}", [P, 512], stack=ps) for i in range(2)]
        p_v = [R.psum(f"{tag}_pv{i}", [P, 512], stack=ps) for i in range(2)]
        p_kv = [R.psum(f"{tag}_pkv{i}", [P, 512], stack=ps) for i in range(2)]
        p_ks = [R.psum(f"{tag}_pks{i}", [1, 512], stack=ps) for i in range(2)]
        wk_src = W["kw_t"].rearrange("(k p) n -> p k n", p=P)
        wv_src = W["vw_t"].rearrange("(k p) n -> p k n", p=P)

        for e in (R.g, R.t, R.s, R.v, R.y):
            for sem, valv in joins:
                e.wait(sem, valv)

        pipe = 0
        tdone = []
        fc_mm = []
        kvd = []
        for fc in range(4):
            par = fc % 2
            if fc >= 2:
                R.g.wait(mm, fc_mm[fc - 2])
            n0 = fc * 512
            for kq in range(4):
                R.g.dma_start(
                    out=wkb[par][:].rearrange("p (k n) -> p k n", k=NE)
                    [:, 4 * kq:4 * kq + 4, :],
                    in_=wk_src[:, 4 * kq:4 * kq + 4, n0:n0 + 512]).inc(ld[par], 16)
                R.g.dma_start(
                    out=wvb[par][:].rearrange("p (k n) -> p k n", k=NE)
                    [:, 4 * kq:4 * kq + 4, :],
                    in_=wv_src[:, 4 * kq:4 * kq + 4, n0:n0 + 512]).inc(ld[par], 16)
            R.t.wait(ld[par])
            for tt in range(NE):
                b = tt // 8
                pp = pipe % 2
                if tt == 0 and fc >= 1:
                    R.t.wait(evv, kvd[fc - 1])
                if pipe >= 2:
                    R.t.wait(evv, tdone[pipe - 2][0])
                    R.t.wait(axs, tdone[pipe - 2][1])
                for k in range(NE):
                    R.t.matmul(p_k[pp][:],
                               xn[:, k * TPC + tt * P: k * TPC + (tt + 1) * P],
                               wkb[par][:, k * 512:(k + 1) * 512],
                               start=(k == 0), stop=False)
                R.t.matmul(p_k[pp][:], cst["ones_row_bf"][:],
                           bkrow[0:1, n0:n0 + 512], start=False, stop=True)
                for k in range(NE):
                    R.t.matmul(p_v[pp][:],
                               xn[:, k * TPC + tt * P: k * TPC + (tt + 1) * P],
                               wvb[par][:, k * 512:(k + 1) * 512],
                               start=(k == 0), stop=False)
                R.t.matmul(p_v[pp][:], cst["ones_row_bf"][:],
                           bvrow[0:1, n0:n0 + 512], start=False,
                           stop=True).inc(mm, 1)
                R.s.wait(mm)
                m = elu_sb[:, (2 * pp) * STRIPE:(2 * pp + 1) * STRIPE]
                r_ = elu_sb[:, (2 * pp + 1) * STRIPE:(2 * pp + 2) * STRIPE]
                R.v.wait(mm)
                if pipe >= 2:
                    R.v.wait(mm, tdone[pipe - 2][2])
                    R.v.wait(evv, tdone[pipe - 2][0])
                R.v.tensor_scalar_min(m, p_k[pp][:], 0.0).inc(evv, 1)
                R.s.wait(evv)
                R.s.activation(m, m, AF.Exp).inc(axs, 1)
                R.v.tensor_scalar_max(r_, p_k[pp][:], 0.0).inc(evv, 1)
                R.v.wait(axs)
                R.v.wait(evv)
                R.v.tensor_tensor(kpt[pp][:], m, r_, ALU.add).inc(evv, 1)
                R.s.wait(evv)
                R.s.activation(vt[pp][:], p_v[pp][:], AF.Copy).inc(axs, 1)
                R.t.wait(evv)
                R.t.wait(axs)
                first = (tt % 8) == 0
                last = (tt % 8) == 7
                for hd in range(4):
                    R.t.matmul(p_kv[b][:, hd * P:(hd + 1) * P],
                               kpt[pp][:, hd * P:(hd + 1) * P],
                               vt[pp][:, hd * P:(hd + 1) * P],
                               start=(first and hd == 0),
                               stop=(last and hd == 3))
                R.t.matmul(p_ks[b][:], cst["ones_col_bf"][:], kpt[pp][:],
                           start=first, stop=last).inc(mm, 1)
                tdone.append((R.val(evv), R.val(axs), R.val(mm)))
                pipe += 1
            fc_mm.append(R.val(mm))
            R.v.wait(mm)
            if fc >= 1:
                R.v.wait(kst)
            for b in range(2):
                R.v.tensor_copy(kvo[:, b * 512:(b + 1) * 512],
                                p_kv[b][:]).inc(evv, 1)
                R.v.tensor_copy(kso[0:1, b * 512:(b + 1) * 512],
                                p_ks[b][:]).inc(evv, 1)
            kvd.append(R.val(evv))
            R.y.wait(evv)
            R.y.dma_start(out=W["kv_out"].rearrange("(b p) n -> p b n", p=P)
                          [:, :, n0:n0 + 512],
                          in_=kvo[:].rearrange("p (b n) -> p b n", b=2)
                          ).inc(kst, 16)
            for b in range(2):
                R.y.dma_start(out=W["ks_out"][b:b + 1, n0:n0 + 512],
                              in_=kso[0:1, b * 512:(b + 1) * 512]
                              ).inc(kst, 16)
        R.y.wait(kst)
        kv_joins = [(kst, R.val(kst)), (evv, R.val(evv))]

    if after_kv is not None:
        after_kv(R, kv_joins)

    evv_q = R.sem(pool="dve0")
    axs_q = R.sem(pool="act1")
    sst_q = [R.sem(pool=f"y{2 + i}") for i in range(2)]
    qpT3 = W["qpT_out"].rearrange("(h p) t -> p h t", p=P)

    def qp_cb(ft, tc, pp, es, evs):
        m = elu_sb[:, (2 * pp) * STRIPE:(2 * pp + 1) * STRIPE]
        r_ = elu_sb[:, (2 * pp + 1) * STRIPE:(2 * pp + 2) * STRIPE]
        ob = qp_sb[:, pp * STRIPE:(pp + 1) * STRIPE]
        R.v.wait(evs)
        R.v.wait(evv_q)
        R.v.tensor_scalar_min(m, es, 0.0).inc(evv_q, 1)
        R.s.wait(evv_q)
        R.s.activation(m, m, AF.Exp).inc(axs_q, 1)
        R.v.tensor_scalar_max(r_, es, 0.0).inc(evv_q, 1)
        R.v.wait(axs_q)
        R.v.wait(evv_q)
        R.v.wait(sst_q[pp])
        R.v.tensor_tensor(ob, m, r_, ALU.add).inc(evv_q, 1)
        R.y.wait(evv_q)
        R.y.dma_start(out=qpT3[:, ft, tc * STRIPE:(tc + 1) * STRIPE],
                      in_=ob).inc(sst_q[pp], 16)
        return [(evv_q, R.val(evv_q)), (axs_q, R.val(axs_q))]

    joins = emit_proj_ch(R, xn, W["qw_t"], bq_sb, qp_cb, cst, f"{tag}_q", joins)
    R.y.wait(sst_q[0])
    R.y.wait(sst_q[1])


def emit_attn1b(R, W, cst, tag):
    nc = R.nc
    EPS1 = 1e-6
    acc = R.sem(pool="g5")
    evv = R.sem(pool="dve0")
    axs = R.sem(pool="act0")
    mm = R.sem(pool="pe0")
    kv_f = R.sbuf(f"{tag}_kvf", [P, 2 * D], F32)
    kv_bf = R.sbuf(f"{tag}_kvb", [P, 2 * D], BF16)
    ks_f = R.sbuf(f"{tag}_ksf", [2, D], F32)
    ks_bf = R.sbuf(f"{tag}_ksb", [2, D], BF16)
    ksc = R.sbuf(f"{tag}_ksc", [P, 32], BF16)
    ident = R.sbuf(f"{tag}_id", [P, P], BF16)
    p_tr = R.psum(f"{tag}_ptr", [P, 32], BF16)
    idl = R.sem(pool="y6")
    R.y.dma_start(out=ident[:], in_=W["ident"]).inc(idl, 16)
    if "kv_sum" in W:
        R.g.dma_start(out=kv_f[:].rearrange("p (b n) -> p b n", b=2),
                      in_=W["kv_sum"].rearrange("(b p) n -> p b n", p=P)
                      ).inc(acc, 16)
        R.g.dma_start(out=ks_f[:], in_=W["ks_sum"]).inc(acc, 16)
    else:
        R.g.dma_start(out=kv_f[:].rearrange("p (b n) -> p b n", b=2),
                      in_=W["kv_all"].rearrange("(r b p) n -> p r b n", p=P, b=2)
                      [:, 0, :, :]).inc(acc, 16)
        for r in range(1, 8):
            R.g.wait(acc)
            R.g.dma_start(out=kv_f[:].rearrange("p (b n) -> p b n", b=2),
                          in_=W["kv_all"].rearrange("(r b p) n -> p r b n", p=P, b=2)
                          [:, r, :, :], accum_op=ALU.add).inc(acc, 16)
        R.g.wait(acc)
        R.g.dma_start(out=ks_f[:], in_=W["ks_all"][0:2, :]).inc(acc, 16)
        for r in range(1, 8):
            R.g.wait(acc)
            R.g.dma_start(out=ks_f[:], in_=W["ks_all"][2 * r:2 * r + 2, :],
                          accum_op=ALU.add).inc(acc, 16)
    R.v.wait(acc)
    R.v.tensor_copy(kv_bf[:], kv_f[:]).inc(evv, 1)
    R.v.tensor_copy(ks_bf[:], ks_f[:]).inc(evv, 1)
    R.t.wait(evv)
    R.t.wait(idl)
    h = None
    for k in range(NE):
        h = R.t.transpose(p_tr[:, 2 * k:2 * k + 2],
                          ks_bf[0:2, k * P:(k + 1) * P], ident[0:2, 0:2])
    h.inc(mm, 1)
    R.s.wait(mm)
    R.s.activation(ksc[:].rearrange("p (b k) -> p b k", b=2),
                   p_tr[:].rearrange("p (k b) -> p k b", b=2)
                   .transpose([0, 2, 1]), AF.Copy).inc(axs, 1)

    att = [R.sbuf(f"{tag}_at{i}", [P, NE * STRIPE], BF16) for i in range(2)]
    qpb = [R.sbuf(f"{tag}_qb{i}", [P, NE * STRIPE], BF16) for i in range(2)]
    zrow = R.sbuf(f"{tag}_z", [1, 2 * STRIPE], F32)
    zbc = R.sbuf(f"{tag}_zb", [P, STRIPE], F32)
    p_a = [R.psum(f"{tag}_pa{i}", [P, STRIPE]) for i in range(2)]
    p_z = R.psum(f"{tag}_pz", [1, STRIPE])
    p_zb = R.psum(f"{tag}_pzb", [P, STRIPE])
    qld = [R.sem(pool=f"s{1 + i}") for i in range(2)]
    qpT3 = W["qpT"].rearrange("(h p) t -> p h t", p=P)

    wo_ld = [R.sem(pool=f"g{2 + i}") for i in range(2)]
    xld = [R.sem(pool=["s0", "s3"][i]) for i in range(2)]
    sst = [R.sem(pool=f"y{4 + i}") for i in range(2)]
    wo_blk = [R.sbuf(f"{tag}_wo{i}", [P, NE * 512], BF16) for i in range(2)]
    xrb = R.sbuf(f"{tag}_xr", [P, 2 * 4 * STRIPE], F32)
    o_sb = R.sbuf(f"{tag}_osb", [P, 2 * 2 * STRIPE], F32)
    p_op = [R.psum(f"{tag}_pop0", [P, STRIPE])]
    wo_src = W["ow_t"].rearrange("(k p) n -> p k n", p=P)
    bo_sb = R.sbuf(f"{tag}_bo", [P, NE], F32)
    bol = R.sem(pool="y7")
    R.y.dma_start(out=bo_sb[:], in_=W["ob"]).inc(bol, 16)
    R.v.wait(bol)
    x3 = W["x_in"].rearrange("(k p) t -> p k t", p=P)
    xo3 = W["x_out"].rearrange("(k p) t -> p k t", p=P)
    ost = dict(wp=0, wo_mm=[], nft=0, evac=[], nsup=0, sup_evac=[])

    ci_mm = {}
    att_done = []
    for ci in range(NTC):
        b = ci // 2
        cp = ci % 2
        if ci >= 2:
            R.s.wait(mm, ci_mm[ci - 2])
        R.s.dma_start(out=qpb[cp][:].rearrange("p (k t) -> p k t", k=NE),
                      in_=qpT3[:, :, ci * STRIPE:(ci + 1) * STRIPE]
                      ).inc(qld[cp], 16)
        R.t.wait(qld[cp])
        R.t.wait(axs)
        for h2 in range(NE):
            bh = b * NE + h2
            pa = h2 % 2
            qh = qpb[cp][:, h2 * STRIPE:(h2 + 1) * STRIPE]
            R.t.wait(evv)
            if ci >= 2 and h2 == 0:
                R.t.wait(mm, ci_mm[ci - 2])
            R.t.matmul(p_a[pa][:],
                       kv_bf[:, b * D + h2 * P: b * D + (h2 + 1) * P],
                       qh, start=True, stop=True)
            R.t.matmul(p_z[:], ksc[:, bh:bh + 1], qh,
                       start=True, stop=True).inc(mm, 1)
            R.v.wait(mm)
            zr = zrow[0:1, (h2 % 2) * STRIPE:(h2 % 2 + 1) * STRIPE]
            R.v.tensor_single_scalar(zr, p_z[:], EPS1, ALU.add).inc(evv, 1)
            R.v.wait(evv)
            R.v.reciprocal(zr, zr).inc(evv, 1)
            R.t.wait(evv)
            R.t.matmul(p_zb[:], cst["ones_row_f32"][:], zr,
                       start=True, stop=True).inc(mm, 1)
            R.s.wait(mm)
            R.s.wait(evv)
            R.s.activation(zbc[:], p_zb[:], AF.Copy).inc(axs, 1)
            R.v.wait(axs)
            if ci >= 2 and h2 == 0:
                R.v.wait(mm, ci_mm[ci - 2])
            R.v.tensor_tensor(att[ci % 2][:, h2 * STRIPE:(h2 + 1) * STRIPE],
                              p_a[pa][:], zbc[:], ALU.mult).inc(evv, 1)
            att_done.append(R.val(evv))
        ci_mm[ci] = R.val(mm)
        for fsup in range(4):
            xp = ost["nsup"] % 2
            if ost["nsup"] >= 2:
                R.s.wait(evv, ost["sup_evac"][ost["nsup"] - 2])
            R.s.dma_start(
                out=xrb[:, xp * 4 * STRIPE:(xp + 1) * 4 * STRIPE]
                .rearrange("p (k t) -> p k t", k=4),
                in_=x3[:, fsup * 4:(fsup + 1) * 4,
                       ci * STRIPE:(ci + 1) * STRIPE]).inc(xld[xp], 16)
            wp = ost["wp"] % 2
            if ost["wp"] >= 2:
                R.g.wait(mm, ost["wo_mm"][ost["wp"] - 2])
            n0 = fsup * 512
            for kq in range(4):
                R.g.dma_start(
                    out=wo_blk[wp][:].rearrange("p (k n) -> p k n", k=NE)
                    [:, 4 * kq:4 * kq + 4, :],
                    in_=wo_src[:, 4 * kq:4 * kq + 4, n0:n0 + 512]
                ).inc(wo_ld[wp], 16)
            R.t.wait(wo_ld[wp])
            R.t.wait(evv)
            for ft4 in range(4):
                ft = fsup * 4 + ft4
                nft = ost["nft"]
                if nft >= 1:
                    R.t.wait(evv, ost["evac"][nft - 1])
                for k in range(NE):
                    R.t.matmul(p_op[0][:],
                               wo_blk[wp][:, k * 512 + ft4 * 128:
                                          k * 512 + (ft4 + 1) * 128],
                               att[ci % 2][:, k * STRIPE:(k + 1) * STRIPE],
                               start=(k == 0),
                               stop=(k == NE - 1)).inc(
                                   mm, 1 if k == NE - 1 else 0)
                R.v.wait(mm)
                R.v.wait(xld[xp])
                opair = ost["nft"] // 2
                op = opair % 2
                ot = o_sb[:, (op * 2 + ft4 % 2) * STRIPE:
                          (op * 2 + ft4 % 2 + 1) * STRIPE]
                if opair >= 2:
                    R.v.wait(sst[op])
                R.v.scalar_tensor_tensor(
                    ot, p_op[0][:], bo_sb[:, ft:ft + 1],
                    xrb[:, (xp * 4 + ft4) * STRIPE:(xp * 4 + ft4 + 1) * STRIPE],
                    ALU.add, ALU.add).inc(evv, 1)
                ost["evac"].append(R.val(evv))
                ost["nft"] += 1
                if ft4 % 2 == 1:
                    R.y.wait(evv)
                    R.y.dma_start(
                        out=xo3[:, fsup * 4 + ft4 - 1: fsup * 4 + ft4 + 1,
                                ci * STRIPE:(ci + 1) * STRIPE],
                        in_=o_sb[:, op * 2 * STRIPE:(op + 1) * 2 * STRIPE]
                        .rearrange("p (k t) -> p k t", k=2)).inc(sst[op], 16)
            ost["wo_mm"].append(R.val(mm))
            ost["wp"] += 1
            ost["sup_evac"].append(R.val(evv))
            ost["nsup"] += 1
    R.y.wait(sst[0])
    R.y.wait(sst[1])


def build_attn1a():
    nc = bass.Bass("TRN2", target_bir_lowering=False)
    with contextlib.ExitStack() as stack:
        R = Rec(nc, stack)
        W = {}
        x_in = nc.dram_tensor("x_in", [D, TPC], F32, kind="ExternalInput").ap()
        for nm, shp, dt in [("qw_t", [D, D], F32), ("kw_t", [D, D], F32),
                            ("vw_t", [D, D], F32), ("qb", [P, NE], F32),
                            ("kb", [1, D], F32), ("vb", [1, D], F32)]:
            W[nm] = nc.dram_tensor(nm, shp, dt, kind="ExternalInput").ap()
        W["qpT_out"] = nc.dram_tensor("qpT_out", [D, TPC], BF16, kind="ExternalOutput").ap()
        W["kv_out"] = nc.dram_tensor("kv_out", [2 * P, D], F32, kind="ExternalOutput").ap()
        W["ks_out"] = nc.dram_tensor("ks_out", [2, D], F32, kind="ExternalOutput").ap()
        cst = _mk_consts(R)
        emit_attn1a(R, x_in, W, cst, "a1")
        R.emit()
    return nc


def build_attn1b():
    nc = bass.Bass("TRN2", target_bir_lowering=False)
    with contextlib.ExitStack() as stack:
        R = Rec(nc, stack)
        W = {}
        W["x_in"] = nc.dram_tensor("x_in", [D, TPC], F32, kind="ExternalInput").ap()
        for nm, shp, dt in [("qpT", [D, TPC], BF16),
                            ("kv_all", [8 * 2 * P, D], F32),
                            ("ks_all", [8 * 2, D], F32),
                            ("ident", [P, P], BF16),
                            ("ow_t", [D, D], F32), ("ob", [P, NE], F32)]:
            W[nm] = nc.dram_tensor(nm, shp, dt, kind="ExternalInput").ap()
        W["x_out"] = nc.dram_tensor("x_out", [D, TPC], F32, kind="ExternalOutput").ap()
        cst = _mk_consts(R)
        emit_attn1b(R, W, cst, "b1")
        R.emit()
    return nc



def shard_x(x):
    out = []
    for r in range(NCORE):
        cols = [np.ascontiguousarray(x[b, s * STRIPE:(s + 1) * STRIPE, :].T)
                for (b, s) in chunks_of(r)]
        out.append(np.ascontiguousarray(np.concatenate(cols, axis=1)))
    return out


def unshard_x(slices):
    x = np.empty((B, S, D), np.float32)
    for r in range(NCORE):
        for i, (b, s) in enumerate(chunks_of(r)):
            x[b, s * STRIPE:(s + 1) * STRIPE, :] = \
                slices[r][:, i * STRIPE:(i + 1) * STRIPE].T
    return x


def scale_tile(s):
    return np.ascontiguousarray(np.asarray(s, np.float32).reshape(-1, P).T)


def build_fused():
    nc = bass.Bass("TRN2", target_bir_lowering=False)
    stack = contextlib.ExitStack()
    with stack:
        R = Rec(nc, stack)
        EI = lambda n, s, d: nc.dram_tensor(n, s, d, kind="ExternalInput").ap()
        IN = lambda n, s, d: nc.dram_tensor(n, s, d, kind="Internal").ap()

        x_in = EI("x_in", [D, TPC], F32)
        x_out = nc.dram_tensor("x_out", [D, TPC], F32, kind="ExternalOutput").ap()
        w_gate = EI("w_gate", [D, 8192], I8)
        w_up = EI("w_up", [D, 8192], I8)
        w_down = EI("w_down", [8192, D], I8)
        s_gate = EI("s_gate", [P, 64], F32)
        s_up = EI("s_up", [P, 64], F32)
        s_down = EI("s_down", [P, 16], F32)
        wq_t = EI("wq_t", [D, D], F32)
        wk_t = EI("wk_t", [D, D], F32)
        wv_t = EI("wv_t", [D, D], F32)
        bq = EI("bq", [P, NE], F32)
        bk = EI("bk", [P, NE], F32)
        bv = EI("bv", [1, D], F32)
        cs2 = EI("cs2", [P, 2 * TPC], F32)
        wo_t = EI("wo_t", [D, D], F32)
        bo = EI("bo", [P, NE], F32)
        masks = EI("masks", [P, 4 * 512], BF16)
        fut = EI("fut", [1, 128], F32)
        gWp = EI("gWp", [D, 1024], I32)
        gSc = EI("gSc", [D, 64], F32)
        uWp = EI("uWp", [D, 1024], I32)
        uSc = EI("uSc", [D, 64], F32)
        dWp = EI("dWp", [D, 1024], I32)
        dSc = EI("dSc", [D, 64], F32)
        ident = EI("ident", [P, P], BF16)
        qw_t = EI("qw_t", [D, D], F32)
        kw_t = EI("kw_t", [D, D], F32)
        vw_t = EI("vw_t", [D, D], F32)
        qb1 = EI("qb1", [P, NE], F32)
        kb1 = EI("kb1", [1, D], F32)
        vb1 = EI("vb1", [1, D], F32)
        ow_t = EI("ow_t", [D, D], F32)
        ob1 = EI("ob1", [P, NE], F32)

        x1 = IN("x1", [D, TPC], F32)
        x2 = IN("x2", [D, TPC], F32)
        x3 = IN("x3", [D, TPC], F32)
        x4 = IN("x4", [D, TPC], F32)
        x5 = IN("x5", [D, TPC], F32)
        qT = IN("qT", [D, TPC], BF16)
        kT = IN("kT", [D, TPC], BF16)
        v_hb = IN("v_hb", [NE * TPC, P], BF16)
        nrm = IN("nrm", [1, 64], F32)
        kn1 = IN("kn1", [1, 32], F32)
        kT_all = IN("kT_all", [8 * D, TPC], BF16)
        v_all = IN("v_all", [8 * NE * TPC, P], BF16)
        kn_all8 = IN("kn_all8", [8, 32], F32)
        dq_part = IN("dq_part", [3 * D, D], BF16)
        dq_all = IN("dq_all", [8 * 3 * D, D], BF16)
        qpT = IN("qpT", [D, TPC], BF16)
        kv_part = IN("kv_part", [2 * P, D], F32)
        ks_part = IN("ks_part", [2, D], F32)
        kv_sum = IN("kv_sum", [2 * P, D], F32)
        ks_sum = IN("ks_sum", [2, D], F32)

        ccs = [R.sem(f"cc{i}", barrier=False) for i in range(6)]
        RG = [list(range(NCORE))]
        cst = _mk_consts(R)

        sg_sb = R.sbuf("sg_sb", [P, 64], F32)
        su_sb = R.sbuf("su_sb", [P, 64], F32)
        sd_sb = R.sbuf("sd_sb", [P, 16], F32)
        sld = R.sem("sld")
        sini = R.sem("sini")
        R.g.dma_start(out=sg_sb[:], in_=s_gate).inc(sld, 16)
        R.g.dma_start(out=su_sb[:], in_=s_up).inc(sld, 16)
        R.g.dma_start(out=sd_sb[:], in_=s_down).inc(sld, 16)
        R.v.wait(sld)
        R.v.tensor_single_scalar(sd_sb[:], sd_sb[:], 0.5, ALU.mult).inc(sini, 1)
        for e in (R.t, R.s, R.y, R.g, R.v):
            e.wait(sld)
            e.wait(sini)

        W0 = dict(w_gate=w_gate, w_up=w_up, w_down=w_down,
                  sg_sb=sg_sb, su_sb=su_sb, sd_sb=sd_sb)

        def phase(fn, name="?", ccw=()):
            with contextlib.ExitStack() as ph:
                R.cur = ph
                R.phase = name
                R.phase_barrier()
                for eng, sem in ccw:
                    eng.wait(sem, 1)
                fn()
            R.cur = stack

        def f0a_body():
            dqpsum = contextlib.ExitStack()
            run_jobs, njobs, finish = make_dequant_jobs(R, dict(
                gWp=gWp, gSc=gSc, uWp=uWp, uSc=uSc, dWp=dWp, dSc=dSc,
                ident=ident,
                wg_t_part=dq_part[0:D, :], wu_t_part=dq_part[D:2 * D, :],
                wd_t_part=dq_part[2 * D:3 * D, :]), cst, "dq",
                psum_stack=dqpsum)

            def mid():
                dq_joins = finish()
                for sem, valv in dq_joins:
                    R.g.wait(sem, valv)
                R.g.collective_compute("AllGather", ALU.bypass,
                                       replica_groups=RG,
                                       ins=[dq_part], outs=[dq_all]
                                       ).inc(ccs[0], 1)
                dqpsum.close()

            emit_ffn_pass(R, x_in, x1, W0, cst, "f0a", inter=8192, kind="l0",
                          co_job=lambda: run_jobs(6), mid_hook=mid)
        phase(f0a_body, "f0a")

        kcp = R.sem("kcp")

        def after_k(R2, joins):
            for sem, valv in joins:
                R2.y.wait(sem, valv)
            R2.y.dma_start(out=kn1, in_=R2.qk_nmax[0:1, 32:64]).inc(kcp, 16)
            for sem, valv in joins:
                R2.g.wait(sem, valv)
            R2.g.wait(kcp)
            R2.g.collective_compute("AllGather", ALU.bypass, replica_groups=RG,
                                    ins=[kT], outs=[kT_all]).inc(ccs[1], 1)
            R2.g.collective_compute("AllGather", ALU.bypass, replica_groups=RG,
                                    ins=[kn1], outs=[kn_all8]).inc(ccs[3], 1)

        def after_v(R2, joins):
            for sem, valv in joins:
                R2.g.wait(sem, valv)
            R2.g.collective_compute("AllGather", ALU.bypass, replica_groups=RG,
                                    ins=[v_hb], outs=[v_all]).inc(ccs[2], 1)

        phase(lambda: emit_qkv0(R, x1, dict(
            wq_t=wq_t, wk_t=wk_t, wv_t=wv_t, bq=bq, bk=bk, bv=bv, cs2=cs2,
            qT_out=qT, kT_out=kT, v_out=v_hb, nrm_out=nrm), cst, "qk",
            after_k=after_k, after_v=after_v), "qk")

        kn_all_view = bass.AP(kn_all8.tensor, 0, [[256, 1], [1, 256]])
        phase(lambda: emit_attn0(R, dict(
            x_in=x1, qT=qT, kT=kT, v_hb=v_hb, kT_all=kT_all, v_all=v_all,
            qn=nrm[0:1, 0:32], kn_all=kn_all_view, fut=fut, masks=masks,
            wo_t=wo_t, bo=bo, x_out=x2), cst, "a0"), "a0",
              ccw=((R.y, ccs[1]), (R.g, ccs[2]), (R.y, ccs[3])))

        phase(lambda: emit_ffn_pass(R, x2, x3, W0, cst, "f0b",
                                    inter=8192, kind="l0"), "f0b")

        def wg_slice(n0):
            rk = n0 // D
            nl = n0 % D
            return dq_all[rk * 3 * D: rk * 3 * D + D,
                          nl:nl + 512].rearrange("(k p) n -> p k n", p=P)

        def wu_slice(n0):
            rk = n0 // D
            nl = n0 % D
            return dq_all[rk * 3 * D + D: rk * 3 * D + 2 * D,
                          nl:nl + 512].rearrange("(k p) n -> p k n", p=P)

        def wd_slice(n0, h0, hl):
            rk = n0 // D
            nl = n0 % D
            base = rk * 3 * D + 2 * D
            return dq_all[base + nl: base + nl + P, h0:h0 + hl]

        W1 = dict(wg_slice=wg_slice, wu_slice=wu_slice, wd_slice=wd_slice)
        phase(lambda: emit_ffn_pass(R, x3, x4, W1, cst, "f1a",
                                    inter=16384, kind="l1"), "f1a",
              ccw=((R.g, ccs[0]),))

        def after_kv(R2, kv_joins):
            for sem, valv in kv_joins:
                R2.g.wait(sem, valv)
            R2.g.collective_compute("AllReduce", ALU.add, replica_groups=RG,
                                    ins=[kv_part], outs=[kv_sum]).inc(ccs[4], 1)
            R2.g.collective_compute("AllReduce", ALU.add, replica_groups=RG,
                                    ins=[ks_part], outs=[ks_sum]).inc(ccs[5], 1)

        phase(lambda: emit_attn1a(R, x4, dict(
            qw_t=qw_t, kw_t=kw_t, vw_t=vw_t, qb=qb1, kb=kb1, vb=vb1,
            qpT_out=qpT, kv_out=kv_part, ks_out=ks_part), cst, "a1",
            after_kv=after_kv), "a1")

        phase(lambda: emit_attn1b(R, dict(
            x_in=x4, qpT=qpT, kv_sum=kv_sum, ks_sum=ks_sum, ident=ident,
            ow_t=ow_t, ob=ob1, x_out=x5), cst, "b1"), "b1",
              ccw=((R.g, ccs[4]), (R.g, ccs[5])))

        phase(lambda: emit_ffn_pass(R, x5, x_out, W1, cst, "f1b",
                                    inter=16384, kind="l1"), "f1b")
        R.phase_barrier()
        R.emit()
    return nc


def kernel_fused(**ins):
    cores = list(range(NCORE))
    PM = ROPE_PERM
    base = dict(
        w_gate=np.asarray(ins["L0_ffn_w_gate"]),
        w_up=np.asarray(ins["L0_ffn_w_up"]),
        w_down=np.asarray(ins["L0_ffn_w_down"]),
        s_gate=scale_tile(ins["L0_ffn_s_gate"]),
        s_up=scale_tile(ins["L0_ffn_s_up"]),
        s_down=scale_tile(ins["L0_ffn_s_down"]),
        wq_t=np.ascontiguousarray(np.asarray(ins["L0_attn_Wq"], np.float32)[PM].T),
        wk_t=np.ascontiguousarray(np.asarray(ins["L0_attn_Wk"], np.float32)[PM].T),
        wv_t=np.ascontiguousarray(np.asarray(ins["L0_attn_Wv"], np.float32).T),
        bq=bias_tile(ins["L0_attn_bq"], PM),
        bk=bias_tile(ins["L0_attn_bk"], PM),
        bv=np.asarray(ins["L0_attn_bv"], np.float32).reshape(1, D),
        wo_t=np.ascontiguousarray(np.asarray(ins["L0_attn_Wo"], np.float32).T),
        bo=bias_tile(ins["L0_attn_bo"]),
        masks=host_masks(),
        ident=np.eye(P).astype(BF),
        qw_t=np.ascontiguousarray(np.asarray(ins["L1_attn_q_w"], np.float32).T),
        kw_t=np.ascontiguousarray(np.asarray(ins["L1_attn_k_w"], np.float32).T),
        vw_t=np.ascontiguousarray(np.asarray(ins["L1_attn_v_w"], np.float32).T),
        qb1=bias_tile(ins["L1_attn_q_b"]),
        kb1=np.asarray(ins["L1_attn_k_b"], np.float32).reshape(1, D),
        vb1=np.asarray(ins["L1_attn_v_b"], np.float32).reshape(1, D),
        ow_t=np.ascontiguousarray(np.asarray(ins["L1_attn_o_w"], np.float32).T),
        ob1=bias_tile(ins["L1_attn_o_b"]),
    )
    x = np.asarray(ins["L0_attn_x"], np.float32)
    xsh = shard_x(x)
    im = []
    for r in cores:
        im.append(dict(
            x_in=xsh[r], cs2=host_rope_tables(r), fut=host_fut(r),
            gWp=np.asarray(ins["L1_ffn_gate_W_packed"])[r * D:(r + 1) * D],
            gSc=np.asarray(ins["L1_ffn_gate_scales"])[r * D:(r + 1) * D],
            uWp=np.asarray(ins["L1_ffn_up_W_packed"])[r * D:(r + 1) * D],
            uSc=np.asarray(ins["L1_ffn_up_scales"])[r * D:(r + 1) * D],
            dWp=np.ascontiguousarray(
                np.asarray(ins["L1_ffn_down_W_packed"])[:, r * 1024:(r + 1) * 1024]),
            dSc=np.ascontiguousarray(
                np.asarray(ins["L1_ffn_down_scales"])[:, r * 64:(r + 1) * 64]),
            **base))
    if "fused" not in _NC_CACHE:
        _NC_CACHE["fused"] = build_fused()
    res = run_bass_kernel_spmd(_NC_CACHE["fused"], im, core_ids=cores)
    return unshard_x([np.asarray(r["x_out"]) for r in res.results])


_NC_CACHE = {}


def _get_nc(name):
    if name not in _NC_CACHE:
        _NC_CACHE[name] = {"ffn0": build_ffn0, "qkv0": build_qkv0,
                           "attn0": build_attn0, "dequant": build_dequant,
                           "ffn1": build_ffn1, "attn1a": build_attn1a,
                           "attn1b": build_attn1b}[name]()
    return _NC_CACHE[name]


def _launch(name, in_maps, out_names):
    nc = _get_nc(name)
    res = run_bass_kernel_spmd(nc, in_maps, core_ids=list(range(NCORE)))
    return [{k: np.asarray(r[k]) for k in out_names} for r in res.results]


def kernel_unfused(**ins):
    cores = list(range(NCORE))
    ident = np.eye(P).astype(BF)
    masks = host_masks()

    x = np.asarray(ins["L0_attn_x"], np.float32)
    xsh = shard_x(x)
    f0_in = dict(w_gate=np.asarray(ins["L0_ffn_w_gate"]),
                 w_up=np.asarray(ins["L0_ffn_w_up"]),
                 w_down=np.asarray(ins["L0_ffn_w_down"]),
                 s_gate=scale_tile(ins["L0_ffn_s_gate"]),
                 s_up=scale_tile(ins["L0_ffn_s_up"]),
                 s_down=scale_tile(ins["L0_ffn_s_down"]))
    o = _launch("ffn0", [dict(x_in=xsh[r], **f0_in) for r in cores], ["x_out"])
    x1sh = [o[r]["x_out"] for r in cores]

    PM = ROPE_PERM
    qkv_in = dict(
        wq_t=np.ascontiguousarray(np.asarray(ins["L0_attn_Wq"], np.float32)[PM].T),
        wk_t=np.ascontiguousarray(np.asarray(ins["L0_attn_Wk"], np.float32)[PM].T),
        wv_t=np.ascontiguousarray(np.asarray(ins["L0_attn_Wv"], np.float32).T),
        bq=bias_tile(ins["L0_attn_bq"], PM),
        bk=bias_tile(ins["L0_attn_bk"], PM),
        bv=np.asarray(ins["L0_attn_bv"], np.float32).reshape(1, D))
    im = []
    for r in cores:
        im.append(dict(x_in=x1sh[r], cs2=host_rope_tables(r), **qkv_in))
    o = _launch("qkv0", im, ["qT_out", "kT_out", "v_out", "nrm_out"])
    kT_all = np.ascontiguousarray(np.concatenate([o[r]["kT_out"] for r in cores], axis=0))
    v_all = np.ascontiguousarray(np.concatenate([o[r]["v_out"] for r in cores], axis=0))
    kn_all = np.ascontiguousarray(np.concatenate([o[r]["nrm_out"][:, 32:64] for r in cores], axis=1))

    a0_in = dict(kT_all=kT_all, v_all=v_all, kn_all=kn_all, masks=masks,
                 wo_t=np.ascontiguousarray(np.asarray(ins["L0_attn_Wo"], np.float32).T),
                 bo=bias_tile(ins["L0_attn_bo"]))
    im = []
    for r in cores:
        im.append(dict(x_in=x1sh[r], qT=o[r]["qT_out"], kT=o[r]["kT_out"],
                       v_hb=o[r]["v_out"],
                       qn=np.ascontiguousarray(o[r]["nrm_out"][:, 0:32]),
                       fut=host_fut(r), **a0_in))
    o = _launch("attn0", im, ["x_out"])
    x2sh = [o[r]["x_out"] for r in cores]

    o = _launch("ffn0", [dict(x_in=x2sh[r], **f0_in) for r in cores], ["x_out"])
    x3sh = [o[r]["x_out"] for r in cores]

    im = []
    for r in cores:
        im.append(dict(
            gWp=np.asarray(ins["L1_ffn_gate_W_packed"])[r * D:(r + 1) * D],
            gSc=np.asarray(ins["L1_ffn_gate_scales"])[r * D:(r + 1) * D],
            uWp=np.asarray(ins["L1_ffn_up_W_packed"])[r * D:(r + 1) * D],
            uSc=np.asarray(ins["L1_ffn_up_scales"])[r * D:(r + 1) * D],
            dWp=np.ascontiguousarray(
                np.asarray(ins["L1_ffn_down_W_packed"])[:, r * 1024:(r + 1) * 1024]),
            dSc=np.ascontiguousarray(
                np.asarray(ins["L1_ffn_down_scales"])[:, r * 64:(r + 1) * 64]),
            ident=ident))
    o = _launch("dequant", im, ["wg_t_part", "wu_t_part", "wd_t_part"])
    wg_t = np.ascontiguousarray(np.concatenate([o[r]["wg_t_part"] for r in cores], axis=1))
    wu_t = np.ascontiguousarray(np.concatenate([o[r]["wu_t_part"] for r in cores], axis=1))
    wd_t = np.ascontiguousarray(np.concatenate([o[r]["wd_t_part"] for r in cores], axis=0))

    f1_in = dict(wg_t=wg_t, wu_t=wu_t, wd_t=wd_t)
    o = _launch("ffn1", [dict(x_in=x3sh[r], **f1_in) for r in cores], ["x_out"])
    x4sh = [o[r]["x_out"] for r in cores]

    a1_in = dict(
        qw_t=np.ascontiguousarray(np.asarray(ins["L1_attn_q_w"], np.float32).T),
        kw_t=np.ascontiguousarray(np.asarray(ins["L1_attn_k_w"], np.float32).T),
        vw_t=np.ascontiguousarray(np.asarray(ins["L1_attn_v_w"], np.float32).T),
        qb=bias_tile(ins["L1_attn_q_b"]),
        kb=np.asarray(ins["L1_attn_k_b"], np.float32).reshape(1, D),
        vb=np.asarray(ins["L1_attn_v_b"], np.float32).reshape(1, D))
    o = _launch("attn1a", [dict(x_in=x4sh[r], **a1_in) for r in cores],
                ["qpT_out", "kv_out", "ks_out"])
    kv_all = np.ascontiguousarray(np.concatenate([o[r]["kv_out"] for r in cores], axis=0))
    ks_all = np.ascontiguousarray(np.concatenate([o[r]["ks_out"] for r in cores], axis=0))

    b1_in = dict(kv_all=kv_all, ks_all=ks_all, ident=ident,
                 ow_t=np.ascontiguousarray(np.asarray(ins["L1_attn_o_w"], np.float32).T),
                 ob=bias_tile(ins["L1_attn_o_b"]))
    im = [dict(x_in=x4sh[r], qpT=o[r]["qpT_out"], **b1_in) for r in cores]
    o = _launch("attn1b", im, ["x_out"])
    x5sh = [o[r]["x_out"] for r in cores]

    o = _launch("ffn1", [dict(x_in=x5sh[r], **f1_in) for r in cores], ["x_out"])
    return unshard_x([o[r]["x_out"] for r in cores])


def kernel(**ins):
    return kernel_fused(**ins)
